# version 12
# baseline (speedup 1.0000x reference)
import math
from contextlib import ExitStack

import numpy as np

import concourse.bass as bass
import concourse.mybir as mybir
from concourse.bass_utils import run_bass_kernel_spmd

F32 = mybir.dt.float32
BF16 = mybir.dt.bfloat16
I32 = mybir.dt.int32
AF = mybir.ActivationFunctionType
ALU = mybir.AluOpType
X = mybir.AxisListType.X

NS_A, NS_B, NS_C = 3.4445, -4.775, 2.0315
BASE_LR_INV = float(np.log(np.expm1(0.01)))
MAGIC = 0x5F3759DF
TINY = 1e-37

CH = 512
HD = 128
P = 128


class Buf:
    __slots__ = ("writers", "readers", "name")

    def __init__(self, name=""):
        self.writers = []
        self.readers = []
        self.name = name


class Emit:

    N_QS = 16
    N_QP = 8

    def __init__(self, nc, stack, max_ops=None):
        self.max_ops = max_ops
        self.ncalls = 0
        self.nc = nc
        self.sems = {}
        self.cnt = {}
        for name in ("pe", "act", "dve", "pool"):
            self.sems[name] = stack.enter_context(nc.semaphore("sem_" + name))
            self.cnt[name] = 0
        for i in range(self.N_QS):
            self.sems[f"qs{i}"] = stack.enter_context(
                nc.semaphore(f"sem_qs{i}"))
            self.cnt[f"qs{i}"] = 0
        for i in range(self.N_QP):
            self.sems[f"qp{i}"] = stack.enter_context(
                nc.semaphore(f"sem_qp{i}"))
            self.cnt[f"qp{i}"] = 0
        self.rr = {"qs": 0, "qp": 0}
        self.eng = {"pe": nc.tensor, "act": nc.scalar, "dve": nc.vector,
                    "pool": nc.gpsimd, "sp": nc.sync}
        self.own = {"pe": "pe", "act": "act", "dve": "dve", "pool": "pool"}
        self.floors = {e: {} for e in self.eng}
        self.nwaits = 0

    def _wait(self, ename, sem_name, val):
        if val <= 0:
            return
        fl = self.floors[ename]
        if fl.get(sem_name, 0) >= val:
            return
        self.eng[ename].wait_ge(self.sems[sem_name], val)
        fl[sem_name] = val
        self.nwaits += 1

    def _need(self, ename, toks):
        best = {}
        for (s, v) in toks:
            if ename == "pe" and s == "pe":
                continue
            if best.get(s, 0) < v:
                best[s] = v
        for s in sorted(best):
            self._wait(ename, s, best[s])

    def _collect(self, reads, writes, wpart):
        toks = []
        for b in reads:
            toks += b.writers
        for b in writes:
            toks += b.readers + b.writers
        for b in wpart:
            toks += b.readers
        return toks

    def _apply(self, tok, reads, writes, wpart):
        for b in reads:
            b.readers.append(tok)
        for b in writes:
            b.writers = [tok]
            b.readers = []
        for b in wpart:
            b.writers.append(tok)

    def op(self, ename, fn, reads=(), writes=(), wpart=()):
        self.ncalls += 1
        if self.max_ops is not None and self.ncalls > self.max_ops:
            return ("pe", 0)
        self._need(ename, self._collect(reads, writes, wpart))
        instr = fn(self.eng[ename])
        sem = self.own[ename]
        self.cnt[sem] += 1
        instr.then_inc(self.sems[sem], 1)
        tok = (sem, self.cnt[sem])
        self._apply(tok, reads, writes, wpart)
        return tok

    def mm(self, fns, reads=(), writes=(), wpart=()):
        self.ncalls += 1
        if self.max_ops is not None and self.ncalls > self.max_ops:
            return ("pe", 0)
        self._need("pe", self._collect(reads, writes, wpart))
        instrs = [fn(self.nc.tensor) for fn in fns]
        self.cnt["pe"] += 1
        instrs[-1].then_inc(self.sems["pe"], 1)
        tok = ("pe", self.cnt["pe"])
        self._apply(tok, reads, writes, wpart)
        return tok

    def dma(self, ring, fn, reads=(), writes=(), wpart=()):
        self.ncalls += 1
        if self.max_ops is not None and self.ncalls > self.max_ops:
            return ("pe", 0)
        ename = "sp" if ring == "qs" else "pool"
        n = self.N_QS if ring == "qs" else self.N_QP
        sem = f"{ring}{self.rr[ring] % n}"
        self.rr[ring] += 1
        self._wait(ename, sem, self.cnt[sem])
        self._need(ename, self._collect(reads, writes, wpart))
        instr = fn(self.eng[ename])
        self.cnt[sem] += 16
        instr.then_inc(self.sems[sem], 16)
        tok = (sem, self.cnt[sem])
        self._apply(tok, reads, writes, wpart)
        return tok

    def fence(self):
        vals = dict(self.cnt)
        for e in self.eng:
            for s, v in vals.items():
                if self.own.get(e) == s:
                    continue
                self._wait(e, s, v)

    def finish(self):
        for s, v in self.cnt.items():
            self._wait("sp", s, v)


def _rsqrt(E, ename, out, out_b, s_ap, s_b, t1, t1_b, magic_ap, magic_b,
           iters=2):
    ov = out.bitcast(I32)
    sv = s_ap.bitcast(I32)
    E.op(ename, lambda e: e.tensor_single_scalar(ov, sv, 1,
                                                 ALU.arith_shift_right),
         reads=[s_b], writes=[out_b])
    E.op(ename, lambda e: e.tensor_tensor(ov, magic_ap, ov, ALU.subtract),
         reads=[magic_b, out_b], writes=[out_b])
    for _ in range(iters):
        E.op(ename, lambda e: e.tensor_tensor(t1, out, out, ALU.mult),
             reads=[out_b], writes=[t1_b])
        E.op(ename, lambda e: e.scalar_tensor_tensor(
            t1, t1, -0.5, s_ap, ALU.mult, ALU.mult),
            reads=[t1_b, s_b], writes=[t1_b])
        E.op(ename, lambda e: e.scalar_tensor_tensor(
            out, t1, 1.5, out, ALU.add, ALU.mult),
            reads=[out_b, t1_b], writes=[out_b])



def _inv_sqrt_eps(E, ename, out, out_b, s_ap, s_b, t1, t1_b, magic_ap,
                  magic_b, eps):
    _rsqrt(E, ename, out, out_b, s_ap, s_b, t1, t1_b, magic_ap, magic_b)
    E.op(ename, lambda e: e.tensor_scalar(t1, out, -eps, 1.0, ALU.mult,
                                          ALU.add),
         reads=[out_b], writes=[t1_b])
    E.op(ename, lambda e: e.tensor_tensor(out, out, t1, ALU.mult),
         reads=[out_b, t1_b], writes=[out_b])


def build_nc(n_dc=16, n_chunks=8, n_hc=64, ts_hop=4096,
             max_ops=None):
    D = n_dc * P
    HID = n_hc * P
    L = n_chunks * CH
    NH = n_dc
    D3 = 3 * D
    JW = 256
    n_jg = D3 // JW
    invD = 1.0 / D

    nc = bass.Bass("TRN2", target_bir_lowering=False)
    g = ExitStack()

    x_d = nc.dram_tensor("x", [L, D], F32, kind="ExternalInput").ap()
    w0_d = nc.dram_tensor("w0", [NH, HD, HD], F32, kind="ExternalInput").ap()
    w1_d = nc.dram_tensor("w1", [NH, HD, HD], F32, kind="ExternalInput").ap()
    w2_d = nc.dram_tensor("w2", [NH, HD, HD], F32, kind="ExternalInput").ap()
    wqkv_d = nc.dram_tensor("wqkv", [D, D3], F32, kind="ExternalInput").ap()
    wlr_d = nc.dram_tensor("wlr", [D, 3 * NH], F32, kind="ExternalInput").ap()
    blr_d = nc.dram_tensor("blr", [3 * NH], F32, kind="ExternalInput").ap()
    wproj_d = nc.dram_tensor("wproj", [D, D], F32, kind="ExternalInput").ap()
    onw_d = nc.dram_tensor("onw", [HD], F32, kind="ExternalInput").ap()
    wg_d = nc.dram_tensor("wg", [HID, D], F32, kind="ExternalInput").ap()
    wu_d = nc.dram_tensor("wu", [HID, D], F32, kind="ExternalInput").ap()
    wd_d = nc.dram_tensor("wd", [D, HID], F32, kind="ExternalInput").ap()
    y_d = nc.dram_tensor("y", [L, D], F32, kind="ExternalOutput").ap()

    wqkv_bf = nc.dram_tensor("wqkv_bf", [D, D3], BF16, kind="Internal").ap()
    wproj_bf = nc.dram_tensor("wproj_bf", [D, D], BF16, kind="Internal").ap()
    wg_blk = nc.dram_tensor("wg_blk", [n_dc, HID, P], BF16,
                            kind="Internal").ap()
    wu_blk = nc.dram_tensor("wu_blk", [n_dc, HID, P], BF16,
                            kind="Internal").ap()
    wd_blk = nc.dram_tensor("wd_blk", [n_hc, D, P], BF16,
                            kind="Internal").ap()
    wgT = nc.dram_tensor("wgT", [D, HID], BF16, kind="Internal").ap()
    wuT = nc.dram_tensor("wuT", [D, HID], BF16, kind="Internal").ap()
    wdT = nc.dram_tensor("wdT", [HID, D], BF16, kind="Internal").ap()
    att_scr = nc.dram_tensor("att_scr", [D, CH], BF16, kind="Internal").ap()

    import ml_dtypes
    ident_f_d = nc.inline_tensor(np.eye(P, dtype=np.float32), "c_ident_f").ap()
    ident_b_d = nc.inline_tensor(np.eye(P, dtype=ml_dtypes.bfloat16),
                                 "c_ident_b").ap()
    ones_f_d = nc.inline_tensor(np.ones((P, P), np.float32), "c_ones_f").ap()
    ones_b_d = nc.inline_tensor(np.ones((P, P), ml_dtypes.bfloat16),
                                "c_ones_b").ap()
    aI_d = nc.inline_tensor((NS_A * np.eye(P)).astype(ml_dtypes.bfloat16),
                            "c_aI_b").ap()
    magic_d = nc.inline_tensor(np.full((P, 16), MAGIC, np.int32),
                               "c_magic").ap()
    magic64_d = nc.inline_tensor(np.full((P, 4 * NH), MAGIC, np.int32),
                                 "c_magic64").ap()

    def sb(name, shape, dtype, st=g):
        t = st.enter_context(nc.sbuf_tensor(name, shape, dtype))
        return t, Buf(name)

    ident_f, ident_f_b = sb("ident_f", [P, P], F32)
    ident_b, ident_b_b = sb("ident_b", [P, P], BF16)
    ones_f, ones_f_b = sb("ones_f", [P, P], F32)
    ones_b, ones_b_b = sb("ones_b", [P, P], BF16)
    aI, aI_b = sb("aI", [P, P], BF16)
    magic, magic_b = sb("magic", [P, 16], I32)
    magic64, magic64_b = sb("magic64", [P, 4 * NH], I32)
    blr_row, blr_row_b = sb("blr_row", [1, 3 * NH], F32)
    gam, gam_b = sb("gam", [P, 1], F32)
    wlr_bf, wlr_bf_b = sb("wlr_bf", [P, n_dc * 3 * NH], BF16)

    m0, m0_b = sb("m0", [P, NH * HD], F32)
    m1, m1_b = sb("m1", [P, NH * HD], F32)
    m2, m2_b = sb("m2", [P, NH * HD], F32)
    wn, wn_b = sb("wn", [P, 3 * NH], F32)
    w0n, w0n_b = sb("w0n", [P, NH * HD], BF16)
    w2n, w2n_b = sb("w2n", [P, NH * HD], BF16)
    w1t, w1t_b = sb("w1t", [P, NH * HD], BF16)
    w1n2, w1n2_b = sb("w1n2", [P, NH * HD], BF16)

    xnfm, xnfm_b = sb("xnfm", [P, n_dc * CH], BF16)
    lrf, lrf_b = sb("lrf", [P, 4 * 3 * NH], F32)
    rmsS, rmsS_b = sb("rmsS", [P, 4 * NH], F32)
    colsR, colsR_b = sb("colsR", [P, 4 * NH], F32)
    rmsT, rmsT_b = sb("rmsT", [P, 4 * NH], F32)

    xf = [sb(f"xf{i}", [P, D], F32) for i in range(2)]
    stg, stg_b = sb("stg", [P, max(D, 2 * CH)], BF16)
    colA, colA_b = sb("colA", [P, 16], F32)
    colB, colB_b = sb("colB", [P, 16], F32)
    colC, colC_b = sb("colC", [P, 16], F32)
    colD, colD_b = sb("colD", [P, 16], F32)
    rowA, rowA_b = sb("rowA", [1, CH], F32)
    TS_HOP = min(ts_hop, HID, D)
    tT, tT_b = sb("hop2stg", [P, TS_HOP], BF16)

    pb = []
    for i in range(8):
        t = g.enter_context(nc.psum_tensor(f"pb{i}", [P, 512], F32))
        pb.append((t, Buf(f"pb{i}")))

    def pbf(i):
        return pb[i][0].ap().bitcast(BF16)

    E = Emit(nc, g, max_ops=max_ops)

    with nc.Block():
        for (dst, dst_b, src) in ((ident_f, ident_f_b, ident_f_d),
                                  (ident_b, ident_b_b, ident_b_d),
                                  (ones_f, ones_f_b, ones_f_d),
                                  (ones_b, ones_b_b, ones_b_d),
                                  (aI, aI_b, aI_d),
                                  (magic, magic_b, magic_d),
                                  (magic64, magic64_b, magic64_d)):
            E.dma("qs", lambda e, d=dst, s=src: e.dma_start(out=d[:], in_=s),
                  writes=[dst_b])
        E.dma("qs", lambda e: e.dma_start(out=blr_row[:], in_=blr_d[None, :]),
              writes=[blr_row_b])
        E.dma("qs", lambda e: e.dma_start(out=gam[:], in_=onw_d[:, None]),
              writes=[gam_b])
        E.op("dve", lambda e: e.tensor_scalar(
            blr_row[:], blr_row[:], BASE_LR_INV, None, ALU.add),
            reads=[blr_row_b], writes=[blr_row_b])
        for dk in range(n_dc):
            E.dma("qp", lambda e, dk=dk: e.dma_start(
                out=wlr_bf[:, dk * 3 * NH:(dk + 1) * 3 * NH],
                in_=wlr_d[dk * P:(dk + 1) * P, :]),
                wpart=[wlr_bf_b])

        wqkv_bf_B = Buf("wqkv_bf")
        wproj_bf_B = Buf("wproj_bf")
        E.dma("qp", lambda e: e.dma_start(out=wqkv_bf[:], in_=wqkv_d[:]),
              writes=[wqkv_bf_B])
        wg_blk_B = [Buf(f"wg_blk{i}") for i in range(n_dc)]
        wu_blk_B = [Buf(f"wu_blk{i}") for i in range(n_dc)]
        wd_blk_B = [Buf(f"wd_blk{i}") for i in range(n_hc)]
        for dk in range(n_dc):
            E.dma("qp", lambda e, dk=dk: e.dma_start(
                out=wg_blk[dk], in_=wg_d[:, dk * P:(dk + 1) * P]),
                writes=[wg_blk_B[dk]])
            E.dma("qp", lambda e, dk=dk: e.dma_start(
                out=wu_blk[dk], in_=wu_d[:, dk * P:(dk + 1) * P]),
                writes=[wu_blk_B[dk]])
        E.dma("qp", lambda e: e.dma_start(out=wproj_bf[:], in_=wproj_d[:]),
              writes=[wproj_bf_B])
        for hc in range(n_hc):
            E.dma("qp", lambda e, hc=hc: e.dma_start(
                out=wd_blk[hc], in_=wd_d[:, hc * P:(hc + 1) * P]),
                writes=[wd_blk_B[hc]])

        def emit_P1_chain(cc, tc):
            if True:
                xt, xt_b = xf[tc % 2]
                r0 = cc * CH + tc * P
                E.dma("qs", lambda e, xt=xt, r0=r0: e.dma_start(
                    out=xt[:], in_=x_d[r0:r0 + P, :]), writes=[xt_b])
                E.op("dve", lambda e, xt=xt, tc=tc: e.tensor_reduce(
                    colA[:, tc:tc + 1], xt[:], X, ALU.add),
                    reads=[xt_b], wpart=[colA_b])
                E.op("act", lambda e, xt=xt, tc=tc: e.activation(
                    stg[:, 0:D], xt[:], AF.Square,
                    accum_out=colB[:, tc:tc + 1]),
                    reads=[xt_b], writes=[stg_b], wpart=[colB_b])
                E.op("dve", lambda e, tc=tc: e.tensor_scalar(
                    colC[:, 0:1], colA[:, tc:tc + 1], invD, None, ALU.mult),
                    reads=[colA_b], writes=[colC_b])
                E.op("dve", lambda e: e.tensor_tensor(
                    colD[:, 0:1], colC[:, 0:1], colC[:, 0:1], ALU.mult),
                    reads=[colC_b], writes=[colD_b])
                E.op("dve", lambda e, tc=tc: e.scalar_tensor_tensor(
                    colD[:, 1:2], colB[:, tc:tc + 1], invD, colD[:, 0:1],
                    ALU.mult, ALU.subtract),
                    reads=[colB_b, colD_b], wpart=[colD_b])
                E.op("dve", lambda e: e.tensor_scalar(
                    colD[:, 1:2], colD[:, 1:2], 1e-5, None, ALU.add),
                    reads=[colD_b], wpart=[colD_b])
                _rsqrt(E, "dve", colC[:, 1:2], colC_b, colD[:, 1:2], colD_b,
                       colD[:, 2:3], colD_b, magic[:, 0:1], magic_b)
                E.op("dve", lambda e: e.scalar_tensor_tensor(
                    colD[:, 3:4], colC[:, 0:1], -1.0, colC[:, 1:2],
                    ALU.mult, ALU.mult),
                    reads=[colC_b], wpart=[colD_b])
                E.op("act", lambda e, xt=xt: e.activation(
                    stg[:, 0:D], xt[:], AF.Identity,
                    bias=colD[:, 3:4], scale=colC[:, 1:2]),
                    reads=[xt_b, colC_b, colD_b], writes=[stg_b])

        def emit_P1_tr(cc, tc):
            if True:
                for dk0 in range(0, n_dc, 4):
                    ng = min(4, n_dc - dk0)
                    fns = []
                    for q in range(ng):
                        dk = dk0 + q
                        fns.append(lambda t, dk=dk, q=q: t.transpose(
                            pbf(7)[:, q * P:(q + 1) * P],
                            stg[:, dk * P:(dk + 1) * P], ident_b[:]))
                    E.mm(fns, reads=[stg_b, ident_b_b], writes=[pb[7][1]])
                    dst = xnfm[:].rearrange("p (dk t) -> p dk t", t=CH)[
                        :, dk0:dk0 + ng, tc * P:(tc + 1) * P]
                    E.op("dve", lambda e, dst=dst, ng=ng: e.tensor_copy(
                        dst, pbf(7)[:, 0:ng * P].rearrange(
                            "p (q t) -> p q t", t=P)),
                        reads=[pb[7][1]], wpart=[xnfm_b])



        def emit_P1_tc(cc, tc):
            emit_P1_chain(cc, tc)
            emit_P1_tr(cc, tc)

        def emit_P1(cc):
            for tc in range(4):
                emit_P1_tc(cc, tc)

        emit_P1(0)

        wtmp = [sb(f"wtmp{i}", [P, HD], F32) for i in range(2)]
        wns, wns_b = sb("wns", [P, 3 * NH], F32)
        for h in range(NH):
            for wi, (wdin, mast, mast_b, nat, nat_b) in enumerate(
                    ((w0_d, m0, m0_b, w0n, w0n_b),
                     (w1_d, m1, m1_b, w1n2, w1n2_b),
                     (w2_d, m2, m2_b, w2n, w2n_b))):
                wt, wt_b = wtmp[(h * 3 + wi) % 2]
                sl = slice(h * HD, (h + 1) * HD)
                E.dma("qs", lambda e, s=wdin, w=wt, h=h: e.dma_start(
                    out=w[:], in_=s[h]), writes=[wt_b])
                if nat is not None:
                    E.op("dve", lambda e, w=wt, n=nat, sl=sl:
                         e.tensor_copy(n[:, sl], w[:]),
                         reads=[wt_b], wpart=[nat_b])
                E.mm([lambda t, w=wt: t.transpose(pb[7][0][:, 0:HD], w[:],
                                                  ident_f[:])],
                     reads=[wt_b, ident_f_b], writes=[pb[7][1]])
                E.op("dve", lambda e, m=mast, sl=sl:
                     e.tensor_copy(m[:, sl], pb[7][0][:, 0:HD]),
                     reads=[pb[7][1]], wpart=[mast_b])
                if wi == 1:
                    E.op("dve", lambda e, sl=sl:
                         e.tensor_copy(w1t[:, sl], m1[:, sl]),
                         reads=[m1_b], wpart=[w1t_b])
                E.op("act", lambda e, m=mast, sl=sl, wi=wi:
                     e.activation(pb[7][0][:, HD:2 * HD], m[:, sl], AF.Square,
                                  accum_out=colA[:, wi:wi + 1]),
                     reads=[mast_b], writes=[pb[7][1], colA_b])
                E.op("dve", lambda e, wi=wi, h=h: e.tensor_scalar(
                    wn[:, wi * NH + h:wi * NH + h + 1], colA[:, wi:wi + 1],
                    TINY, None, ALU.add),
                    reads=[colA_b], wpart=[wn_b])
        ncol = 3 * NH
        for c0 in range(0, ncol, 16):
            c1 = min(c0 + 16, ncol)
            _rsqrt(E, "dve", wns[:, c0:c1], wns_b, wn[:, c0:c1], wn_b,
                   colB[:, 0:c1 - c0], colB_b, magic[:, 0:c1 - c0], magic_b)
        E.op("dve", lambda e: e.tensor_tensor(wn[:, 0:ncol], wn[:, 0:ncol],
                                              wns[:, 0:ncol], ALU.mult),
             reads=[wn_b, wns_b], writes=[wn_b])

        wgT_B, wuT_B, wdT_B = Buf("wgT"), Buf("wuT"), Buf("wdT")
        _hop2 = {"g": (wg_blk, wg_blk_B, wgT, wgT_B, n_dc, HID),
                 "u": (wu_blk, wu_blk_B, wuT, wuT_B, n_dc, HID),
                 "d": (wd_blk, wd_blk_B, wdT, wdT_B, n_hc, D)}

        def emit_hop2(which):
            blk, blkB, dstT, dstB, nblk, rows = _hop2[which]
            for k in range(nblk):
                for r0 in range(0, rows, TS_HOP):
                    r1 = min(r0 + TS_HOP, rows)
                    E.dma("qs", lambda e, k=k, blk=blk, r0=r0, r1=r1:
                          e.dma_start_transpose(tT[:, 0:r1 - r0],
                                                blk[k][r0:r1, :]),
                          reads=[blkB[k]], writes=[tT_b])
                    E.dma("qp", lambda e, k=k, d=dstT, r0=r0, r1=r1:
                          e.dma_start(out=d[k * P:(k + 1) * P, r0:r1],
                                      in_=tT[:, 0:r1 - r0]),
                          reads=[tT_b], wpart=[dstB])


        for c in range(n_chunks):
            cs = ExitStack()

            def csb(name, shape, dtype):
                return sb(f"{name}_c{c}", shape, dtype, cs)

            qkv, qkv_b = csb("qkv", [P, 4 * D3], BF16)
            wst = [csb(f"wst{i}", [P, n_dc * JW], BF16) for i in range(2)]
            onc, onc_b = csb("onc", [P, NH * CH], BF16)
            gaten, gaten_b = csb("gaten", [P, CH], F32)
            hbn, hbn_b = csb("hbn", [P, CH], F32)
            sig, sig_b = csb("sig", [P, CH], F32)
            f1s, f1s_b = csb("f1s", [P, CH], F32)
            sg, sg_b = csb("sg", [P, CH], BF16)
            hl1, hl1_b = csb("hl1", [P, CH], BF16)
            dh2, dh2_b = csb("dh2", [P, CH], BF16)
            dg0, dg0_b = csb("dg0", [P, CH], BF16)
            zss, zss_b = csb("zss", [P, CH], BF16)
            zfm, zfm_b = csb("zfm", [P, CH], BF16)
            sqs, sqs_b = csb("sqs", [P, CH], BF16)
            bcs, bcs_b = csb("bcs", [P, CH], F32)
            sqb, sqb_b = csb("sqb", [P, D], BF16)
            ew2, ew2_b = csb("ew2", [P, CH], F32)
            nsx = []
            for i in range(3):
                nsx.append({k: csb(f"ns_{k}{i}", [P, HD],
                                   BF16 if k in ("x", "xt", "as", "bm")
                                   else F32)
                            for k in ("x", "xt", "as", "bas", "bm")})
            kfm = [csb(f"kfm{i}", [P, CH], BF16) for i in range(2)]
            vfm = [csb(f"vfm{i}", [P, CH], BF16) for i in range(2)]
            qfm = [csb(f"qfm{i}", [P, CH], BF16) for i in range(2)]

            za = 3 * NH
            for tc in range(4):
                fns = []
                for dk in range(n_dc):
                    fns.append(lambda t, dk=dk, tc=tc: t.matmul(
                        pb[6][0][:, 0:za],
                        xnfm[:, dk * CH + tc * P:dk * CH + (tc + 1) * P],
                        wlr_bf[:, dk * za:(dk + 1) * za],
                        start=(dk == 0), stop=False))
                fns.append(lambda t: t.matmul(
                    pb[6][0][:, 0:za], ones_f[0:1, :],
                    blr_row[:], start=False, stop=True))
                E.mm(fns, reads=[xnfm_b, wlr_bf_b, ones_f_b, blr_row_b],
                     writes=[pb[6][1]])
                E.op("act", lambda e: e.activation(
                    gaten[:, 0:za], pb[6][0][:, 0:za], AF.Abs),
                    reads=[pb[6][1]], writes=[gaten_b])
                E.op("act", lambda e: e.activation(
                    gaten[:, 0:za], gaten[:, 0:za], AF.Exp, scale=-1.0),
                    reads=[gaten_b], writes=[gaten_b])
                E.op("act", lambda e: e.activation(
                    gaten[:, 0:za], gaten[:, 0:za], AF.Ln, bias=1.0),
                    reads=[gaten_b], writes=[gaten_b])
                E.op("act", lambda e: e.activation(
                    hbn[:, 0:za], pb[6][0][:, 0:za], AF.Relu),
                    reads=[pb[6][1]], writes=[hbn_b])
                E.op("dve", lambda e, tc=tc: e.tensor_tensor(
                    lrf[:, tc * za:(tc + 1) * za], hbn[:, 0:za],
                    gaten[:, 0:za], ALU.add),
                    reads=[hbn_b, gaten_b], wpart=[lrf_b])

            for jg in range(n_jg):
                wq, wq_b = wst[jg % 2]
                src = wqkv_bf.rearrange("(dk p) j -> p dk j", p=P)[
                    :, :, jg * JW:(jg + 1) * JW]
                E.dma("qs", lambda e, wq=wq, src=src: e.dma_start(
                    out=wq[:].rearrange("p (dk j) -> p dk j", j=JW), in_=src),
                    reads=[wqkv_bf_B], writes=[wq_b])
                for tc in range(4):
                    bank = pb[(jg * 4 + tc) % 2]
                    psl = bank[0][:, 0:JW]
                    fns = []
                    for dk in range(n_dc):
                        fns.append(lambda t, dk=dk, tc=tc, psl=psl, wq=wq:
                                   t.matmul(
                            psl, xnfm[:, dk * CH + tc * P:
                                      dk * CH + (tc + 1) * P],
                            wq[:, dk * JW:(dk + 1) * JW],
                            start=(dk == 0), stop=(dk == n_dc - 1)))
                    E.mm(fns, reads=[xnfm_b, wq_b], writes=[bank[1]])
                    E.op("act", lambda e, psl=psl, tc=tc, jg=jg: e.activation(
                        qkv[:, tc * D3 + jg * JW:tc * D3 + (jg + 1) * JW],
                        psl, AF.Silu),
                        reads=[bank[1]], wpart=[qkv_b])

            if c == 0:
                emit_hop2("g")
                emit_hop2("u")

            for tc in range(4):
                for si, (dst, dst_b) in enumerate(((colA, colA_b),
                                                   (colB, colB_b))):
                    qs0 = tc * D3 + si * D
                    E.op("dve", lambda e, qs0=qs0: e.tensor_tensor(
                        sqb[:], qkv[:, qs0:qs0 + D], qkv[:, qs0:qs0 + D],
                        ALU.mult), reads=[qkv_b], writes=[sqb_b])
                    E.op("dve", lambda e, dst=dst: e.tensor_reduce(
                        dst[:, 0:NH], sqb[:].rearrange(
                            "p (h e) -> p h e", e=HD), X, ALU.add),
                        reads=[sqb_b], writes=[dst_b])
                    E.op("dve", lambda e, dst=dst: e.tensor_scalar(
                        dst[:, 0:NH], dst[:, 0:NH], TINY, None, ALU.add),
                        reads=[dst_b], wpart=[dst_b])
                for si, (src, src_b) in enumerate(((colA, colA_b),
                                                   (colB, colB_b))):
                    _inv_sqrt_eps(E, "dve", colD[:, 0:NH], colD_b,
                                  src[:, 0:NH], src_b, colC[:, 0:NH], colC_b,
                                  magic[:, 0:NH], magic_b, 1e-5)
                    for h in range(NH):
                        qs0 = tc * D3 + si * D + h * HD
                        E.op("dve", lambda e, qs0=qs0, h=h:
                             e.tensor_scalar(
                                 qkv[:, qs0:qs0 + HD],
                                 qkv[:, qs0:qs0 + HD],
                                 colD[:, h:h + 1], None, ALU.mult),
                             reads=[colD_b, qkv_b], wpart=[qkv_b])

            def head_TF(h):
                par = h % 2
                kf, kf_b = kfm[par]
                vf, vf_b = vfm[par]
                qf, qf_b = qfm[par]
                hsl = slice(h * HD, (h + 1) * HD)
                for (si, dst, dst_b) in ((1, kf, kf_b), (2, vf, vf_b),
                                         (0, qf, qf_b)):
                    fns = []
                    for tc in range(4):
                        src = qkv[:, tc * D3 + si * D + h * HD:
                                  tc * D3 + si * D + (h + 1) * HD]
                        fns.append(lambda t, src=src, tc=tc: t.transpose(
                            pbf(7)[:, tc * P:(tc + 1) * P], src, ident_b[:]))
                    E.mm(fns, reads=[qkv_b, ident_b_b], writes=[pb[7][1]])
                    E.op("dve", lambda e, dst=dst: e.tensor_copy(
                        dst[:], pbf(7)[:, 0:512]),
                        reads=[pb[7][1]], writes=[dst_b])
                for (bank, wmat, wmat_b, opfm, opfm_b) in (
                        (pb[0], w0n, w0n_b, kf, kf_b),
                        (pb[1], w2n, w2n_b, kf, kf_b),
                        (pb[2], w1t, w1t_b, vf, vf_b)):
                    fns = []
                    for tc in range(4):
                        fns.append(lambda t, tc=tc, bank=bank, opfm=opfm,
                                   wmat=wmat, hsl=hsl: t.matmul(
                            bank[0][:, tc * P:(tc + 1) * P],
                            opfm[:, tc * P:(tc + 1) * P], wmat[:, hsl],
                            start=True, stop=True))
                    E.mm(fns, reads=[opfm_b, wmat_b], writes=[bank[1]])

            def head_E_early(h):
                E.op("act", lambda e: e.activation(sg[:], pb[0][0][:],
                                                   AF.Silu),
                     reads=[pb[0][1]], writes=[sg_b])
                E.op("act", lambda e: e.activation(sig[:], pb[0][0][:],
                                                   AF.Tanh, scale=0.5),
                     reads=[pb[0][1]], writes=[sig_b])
                E.op("dve", lambda e: e.tensor_scalar(
                    ew2[:], sg[:], -1.0, 1.0, ALU.mult, ALU.add),
                    reads=[sg_b], writes=[ew2_b])
                E.op("dve", lambda e: e.tensor_tensor(
                    ew2[:], pb[0][0][:], ew2[:], ALU.add),
                    reads=[pb[0][1], ew2_b], writes=[ew2_b])
                E.op("dve", lambda e: e.tensor_copy(hbn[:], pb[1][0][:]),
                     reads=[pb[1][1]], writes=[hbn_b])
                E.op("dve", lambda e: e.tensor_tensor(
                    f1s[:], pb[2][0][:], hbn[:], ALU.mult),
                    reads=[pb[2][1], hbn_b], writes=[f1s_b])
                for tc in range(4):
                    tsl = slice(tc * P, (tc + 1) * P)
                    E.op("dve", lambda e, tsl=tsl, tc=tc: e.tensor_scalar(
                        bcs[:, tsl], pb[2][0][:, tsl],
                        lrf[:, tc * za + 2 * NH + h:
                            tc * za + 2 * NH + h + 1], None, ALU.mult),
                        reads=[pb[2][1], lrf_b], wpart=[bcs_b])

            def head_O(h):
                par = h % 2
                qf, qf_b = qfm[par]
                hsl = slice(h * HD, (h + 1) * HD)
                E.mm([lambda t, hsl=hsl: t.matmul(
                    pb[0][0][:], w0n[:, hsl], qf[:], start=True, stop=True)],
                    reads=[w0n_b, qf_b], writes=[pb[0][1]])
                E.mm([lambda t, hsl=hsl: t.matmul(
                    pb[1][0][:], w2n[:, hsl], qf[:], start=True, stop=True)],
                    reads=[w2n_b, qf_b], writes=[pb[1][1]])
                E.op("act", lambda e: e.activation(zss[:], pb[0][0][:],
                                                   AF.Silu),
                     reads=[pb[0][1]], writes=[zss_b])
                E.op("dve", lambda e: e.tensor_tensor(
                    zfm[:], zss[:], pb[1][0][:], ALU.mult),
                    reads=[zss_b, pb[1][1]], writes=[zfm_b])
                E.mm([lambda t: t.matmul(pb[2][0][:], w1n2[:, hsl], zfm[:],
                                         start=True, stop=True)],
                     reads=[w1n2_b, zfm_b], writes=[pb[2][1]])
                E.op("act", lambda e: e.activation(sqs[:], pb[2][0][:],
                                                   AF.Square),
                     reads=[pb[2][1]], writes=[sqs_b])
                fns = [lambda t, tc=tc: t.matmul(
                    pb[6][0][:, 4 + tc:5 + tc],
                    sqs[:, tc * P:(tc + 1) * P], ones_b[:, 0:1],
                    start=True, stop=True) for tc in range(4)]
                E.mm(fns, reads=[sqs_b, ones_b_b], wpart=[pb[6][1]])
                E.op("dve", lambda e, h=h: e.tensor_scalar(
                    rmsS[:, h * 4:(h + 1) * 4], pb[6][0][:, 4:8], 1.0 / HD,
                    1e-5, ALU.mult, ALU.add),
                    reads=[pb[6][1]], wpart=[rmsS_b])
                E.op("act", lambda e, h=h: e.activation(
                    onc[:, h * CH:(h + 1) * CH], pb[2][0][:], AF.Copy,
                    scale=gam[:, 0:1]),
                    reads=[pb[2][1], gam_b], wpart=[onc_b])

            def head_E_late(h):
                E.op("dve", lambda e: e.tensor_scalar(
                    sig[:], sig[:], 0.5, 0.5, ALU.mult, ALU.add),
                    reads=[sig_b], writes=[sig_b])
                E.op("dve", lambda e: e.tensor_tensor(
                    sig[:], sig[:], ew2[:], ALU.mult),
                    reads=[sig_b, ew2_b], writes=[sig_b])
                E.op("dve", lambda e: e.tensor_tensor(
                    dh2[:], bcs[:], sg[:], ALU.mult),
                    reads=[bcs_b, sg_b], writes=[dh2_b])
                for tc in range(4):
                    tsl = slice(tc * P, (tc + 1) * P)
                    E.op("dve", lambda e, tsl=tsl, tc=tc:
                         e.scalar_tensor_tensor(
                             hl1[:, tsl], hbn[:, tsl],
                             lrf[:, tc * za + NH + h:tc * za + NH + h + 1],
                             sg[:, tsl], ALU.mult, ALU.mult),
                         reads=[hbn_b, lrf_b, sg_b], wpart=[hl1_b])
                    E.op("dve", lambda e, tsl=tsl, tc=tc:
                         e.scalar_tensor_tensor(
                             dg0[:, tsl], f1s[:, tsl],
                             lrf[:, tc * za + h:tc * za + h + 1],
                             sig[:, tsl], ALU.mult, ALU.mult),
                         reads=[f1s_b, lrf_b, sig_b], wpart=[dg0_b])

            def head_G(h):
                hsl = slice(h * HD, (h + 1) * HD)
                kts = [qkv[:, tc * D3 + D + hsl.start:
                           tc * D3 + D + hsl.stop] for tc in range(4)]
                vts = [qkv[:, tc * D3 + 2 * D + hsl.start:
                           tc * D3 + 2 * D + hsl.stop] for tc in range(4)]
                for gi, (bank, lhss, lhs_b, rhss, rhs_b) in enumerate((
                        (pb[3], [dg0[:, tc * P:(tc + 1) * P]
                                 for tc in range(4)], dg0_b, kts, qkv_b),
                        (pb[4], vts, qkv_b,
                         [hl1[:, tc * P:(tc + 1) * P]
                          for tc in range(4)], hl1_b),
                        (pb[5], [dh2[:, tc * P:(tc + 1) * P]
                                 for tc in range(4)], dh2_b, kts, qkv_b))):
                    fns = []
                    for tc in range(4):
                        fns.append(lambda t, lh=lhss[tc], rh=rhss[tc],
                                   bank=bank, tc=tc:
                                   t.matmul(bank[0][:, 0:HD], lh, rh,
                                            start=(tc == 0),
                                            stop=(tc == 3)))
                    E.mm(fns, reads=[lhs_b, rhs_b], writes=[bank[1]])

            def head_NS(h):
                hsl = slice(h * HD, (h + 1) * HD)
                for i in range(3):
                    xg, xg_b = nsx[i]["x"]
                    E.op("act", lambda e, i=i, xg=xg: e.activation(
                        xg[:], pb[3 + i][0][:, 0:HD], AF.Copy),
                        reads=[pb[3 + i][1]], writes=[xg_b])
                    E.op("act", lambda e, i=i, xg=xg: e.activation(
                        sqs[:, 0:HD], xg[:], AF.Square,
                        accum_out=colA[:, i:i + 1]),
                        reads=[xg_b], writes=[sqs_b], wpart=[colA_b])
                fns = [lambda t, i=i: t.matmul(
                    pb[6][0][:, i:i + 1], ones_f[:], colA[:, i:i + 1],
                    start=True, stop=True) for i in range(3)]
                E.mm(fns, reads=[ones_f_b, colA_b], writes=[pb[6][1]])
                E.op("dve", lambda e: e.tensor_scalar(
                    colB[:, 0:3], pb[6][0][:, 0:3], TINY, None, ALU.add),
                    reads=[pb[6][1]], writes=[colB_b])
                _rsqrt(E, "dve", colC[:, 0:3], colC_b, colB[:, 0:3], colB_b,
                       colD[:, 0:3], colD_b, magic[:, 0:3], magic_b, iters=1)
                E.op("dve", lambda e: e.tensor_scalar(
                    colD[:, 0:3], colC[:, 0:3], -1e-7, 1.0, ALU.mult,
                    ALU.add),
                    reads=[colC_b], writes=[colD_b])
                E.op("dve", lambda e: e.tensor_tensor(
                    colC[:, 0:3], colC[:, 0:3], colD[:, 0:3], ALU.mult),
                    reads=[colC_b, colD_b], writes=[colC_b])
                E.op("dve", lambda e: e.tensor_tensor(
                    colB[:, 0:3], colC[:, 0:3], colC[:, 0:3], ALU.mult),
                    reads=[colC_b], writes=[colB_b])
                E.op("dve", lambda e: e.tensor_scalar(
                    colD[:, 0:3], colB[:, 0:3], NS_B, None, ALU.mult),
                    reads=[colB_b], writes=[colD_b])
                for it in range(2):
                    for i in range(3):
                        xg, xg_b = nsx[i]["x"]
                        xt, xt_b = nsx[i]["xt"]
                        E.mm([lambda t, xg=xg: t.transpose(
                            pbf(7)[:, 0:HD], xg[:], ident_b[:])],
                            reads=[xg_b, ident_b_b], writes=[pb[7][1]])
                        E.op("dve", lambda e, xt=xt: e.tensor_copy(
                            xt[:], pbf(7)[:, 0:HD]),
                            reads=[pb[7][1]], writes=[xt_b])
                    for i in range(3):
                        xt, xt_b = nsx[i]["xt"]
                        E.mm([lambda t, i=i, xt=xt: t.matmul(
                            pb[3 + i][0][:, 0:HD], xt[:], xt[:],
                            start=True, stop=True)],
                            reads=[xt_b], writes=[pb[3 + i][1]])
                    for i in range(3):
                        a_s, a_s_b = nsx[i]["as"]
                        bas, bas_b = nsx[i]["bas"]
                        E.op("act", lambda e, i=i, a_s=a_s: e.activation(
                            a_s[:], pb[3 + i][0][:, 0:HD], AF.Copy,
                            scale=colB[:, i:i + 1]),
                            reads=[pb[3 + i][1], colB_b], writes=[a_s_b])
                        E.op("act", lambda e, i=i, bas=bas: e.activation(
                            bas[:], pb[3 + i][0][:, 0:HD], AF.Copy,
                            scale=colD[:, i:i + 1]),
                            reads=[pb[3 + i][1], colD_b], writes=[bas_b])
                    for i in range(3):
                        a_s, a_s_b = nsx[i]["as"]
                        E.mm([lambda t, i=i, a_s=a_s: t.matmul(
                            pb[3 + i][0][:, HD:2 * HD], a_s[:], a_s[:],
                            start=True, stop=True)],
                            reads=[a_s_b], wpart=[pb[3 + i][1]])
                    for i in range(3):
                        bm, bm_b = nsx[i]["bm"]
                        bas, bas_b = nsx[i]["bas"]
                        E.op("dve", lambda e, i=i, bm=bm, bas=bas:
                             e.scalar_tensor_tensor(
                                 bm[:], pb[3 + i][0][:, HD:2 * HD], NS_C,
                                 bas[:], ALU.mult, ALU.add),
                             reads=[pb[3 + i][1], bas_b], writes=[bm_b])
                    for i in range(3):
                        bm, bm_b = nsx[i]["bm"]
                        xg, xg_b = nsx[i]["x"]
                        E.mm([lambda t, i=i, bm=bm, xg=xg: t.matmul(
                            pb[3 + i][0][:, 2 * HD:3 * HD], bm[:], xg[:],
                            start=True, stop=False),
                            lambda t, i=i, xg=xg: t.matmul(
                            pb[3 + i][0][:, 2 * HD:3 * HD], aI[:], xg[:],
                            start=False, stop=True)],
                            reads=[bm_b, xg_b, aI_b],
                            wpart=[pb[3 + i][1]])
                    if it == 0:
                        for i in range(3):
                            xg, xg_b = nsx[i]["x"]
                            E.op("act", lambda e, i=i, xg=xg: e.activation(
                                xg[:], pb[3 + i][0][:, 2 * HD:3 * HD],
                                AF.Copy),
                                reads=[pb[3 + i][1]], writes=[xg_b])

            def head_UPD(h):
                hsl = slice(h * HD, (h + 1) * HD)
                for i, (mast, mast_b) in enumerate(((m0, m0_b), (m1, m1_b),
                                                    (m2, m2_b))):
                    E.op("dve", lambda e, i=i, mast=mast, hsl=hsl:
                         e.scalar_tensor_tensor(
                             mast[:, hsl], pb[3 + i][0][:, 2 * HD:3 * HD],
                             colC[:, i:i + 1], mast[:, hsl],
                             ALU.mult, ALU.add),
                         reads=[pb[3 + i][1], mast_b, colC_b],
                         wpart=[mast_b])
                    E.op("act", lambda e, i=i, mast=mast, hsl=hsl:
                         e.activation(
                             sqs[:, 0:HD], mast[:, hsl], AF.Square,
                             accum_out=colA[:, i:i + 1]),
                         reads=[mast_b], writes=[sqs_b], wpart=[colA_b])
                E.op("dve", lambda e: e.tensor_scalar(
                    colB[:, 0:3], colA[:, 0:3], TINY, None, ALU.add),
                    reads=[colA_b], writes=[colB_b])
                _inv_sqrt_eps(E, "dve", colC[:, 0:3], colC_b, colB[:, 0:3],
                              colB_b, colD[:, 0:3], colD_b, magic[:, 0:3],
                              magic_b, 1e-5)
                wncols = wn[:].rearrange("p (w h) -> p w h", h=NH)[:, :, h]
                E.op("dve", lambda e, wncols=wncols: e.tensor_tensor(
                    colD[:, 0:3], wncols, colC[:, 0:3], ALU.mult),
                    reads=[wn_b, colC_b], writes=[colD_b])
                for i, (mast, mast_b) in enumerate(((m0, m0_b), (m1, m1_b),
                                                    (m2, m2_b))):
                    E.op("dve", lambda e, i=i, mast=mast, hsl=hsl:
                         e.tensor_scalar(
                             mast[:, hsl], mast[:, hsl], colD[:, i:i + 1],
                             None, ALU.mult),
                         reads=[colD_b, mast_b], wpart=[mast_b])
                for (mast, mast_b, dst, dst_b, dsl) in (
                        (m0, m0_b, w0n, w0n_b, hsl),
                        (m2, m2_b, w2n, w2n_b, hsl),
                        (m1, m1_b, w1n2, w1n2_b, hsl)):
                    E.mm([lambda t, mast=mast, hsl=hsl: t.transpose(
                        pb[7][0][:, 0:HD], mast[:, hsl], ident_f[:])],
                        reads=[mast_b, ident_f_b], writes=[pb[7][1]])
                    E.op("act", lambda e, dst=dst, dsl=dsl: e.activation(
                        dst[:, dsl], pb[7][0][:, 0:HD], AF.Copy),
                        reads=[pb[7][1]], wpart=[dst_b])
                E.op("dve", lambda e, hsl=hsl: e.tensor_copy(
                    w1t[:, hsl], m1[:, hsl]),
                    reads=[m1_b], wpart=[w1t_b])

            head_TF(0)
            for h in range(NH):
                head_E_early(h)
                if h > 0:
                    head_O(h - 1)
                if h + 1 < NH:
                    head_TF(h + 1)
                head_E_late(h)
                head_G(h)
                head_NS(h)
                head_UPD(h)
            head_O(NH - 1)

            for h0 in range(0, 4 * NH, 64):
                h1 = min(h0 + 64, 4 * NH)
                _rsqrt(E, "dve", colsR[:, h0:h1], colsR_b, rmsS[:, h0:h1],
                       rmsS_b, rmsT[:, h0:h1], rmsT_b, magic64[:, 0:h1 - h0],
                       magic64_b)
            for h in range(NH):
                fns = [lambda t, tc=tc, h=h: t.transpose(
                    pb[6][0][0:1, tc * P:(tc + 1) * P],
                    colsR[:, h * 4 + tc:h * 4 + tc + 1], ident_f[:])
                    for tc in range(4)]
                E.mm(fns, reads=[colsR_b, ident_f_b], writes=[pb[6][1]])
                E.op("dve", lambda e: e.tensor_copy(
                    rowA[:], pb[6][0][0:1, 0:CH]),
                    reads=[pb[6][1]], writes=[rowA_b])
                E.mm([lambda t: t.matmul(pb[6][0][:], ones_f[0:1, :],
                                         rowA[:], start=True, stop=True)],
                     reads=[ones_f_b, rowA_b], writes=[pb[6][1]])
                E.op("dve", lambda e: e.tensor_copy(bcs[:], pb[6][0][:]),
                     reads=[pb[6][1]], writes=[bcs_b])
                E.op("dve", lambda e, h=h: e.tensor_tensor(
                    onc[:, h * CH:(h + 1) * CH], onc[:, h * CH:(h + 1) * CH],
                    bcs[:], ALU.mult),
                    reads=[bcs_b, onc_b], wpart=[onc_b])

            att_B = Buf(f"att_c{c}")
            for dc in range(n_dc):
                wp, wp_b = wst[dc % 2]
                wpv = wp[:, 0:n_dc * P]
                src = wproj_bf.rearrange("(hk p) j -> p hk j", p=P)[
                    :, :, dc * P:(dc + 1) * P]
                E.dma("qs", lambda e, wpv=wpv, src=src: e.dma_start(
                    out=wpv.rearrange("p (hk j) -> p hk j", j=P), in_=src),
                    reads=[wproj_bf_B], writes=[wp_b])
                bank = pb[dc % 2]
                fns = []
                for hk in range(NH):
                    fns.append(lambda t, hk=hk, wpv=wpv, bank=bank: t.matmul(
                        bank[0][:], wpv[:, hk * P:(hk + 1) * P],
                        onc[:, hk * CH:(hk + 1) * CH],
                        start=(hk == 0), stop=(hk == NH - 1)))
                E.mm(fns, reads=[wp_b, onc_b], writes=[bank[1]])
                ssl = slice((dc % 2) * CH, (dc % 2) * CH + CH)
                E.op("act", lambda e, bank=bank, ssl=ssl: e.activation(
                    stg[:, ssl], bank[0][:], AF.Copy),
                    reads=[bank[1]], writes=[stg_b])
                E.dma("qp", lambda e, dc=dc, ssl=ssl: e.dma_start(
                    out=att_scr[dc * P:(dc + 1) * P, :], in_=stg[:, ssl]),
                    reads=[stg_b], wpart=[att_B])

            if c == 0:
                emit_hop2("d")

            cs.close()
            E.fence()

            fs = ExitStack()

            def fsb(name, shape, dtype):
                return sb(f"{name}_c{c}", shape, dtype, fs)

            gated, gated_b = fsb("gated", [P, n_hc * CH], BF16)
            wgp = [fsb(f"wgp{i}", [P, n_dc * P], BF16) for i in range(2)]
            wup = [fsb(f"wup{i}", [P, n_dc * P], BF16) for i in range(2)]
            nhh = n_hc // 2
            wdp = [fsb(f"wdp{i}", [P, nhh * P], BF16) for i in range(2)]
            zz, zz_b = fsb("zz", [P, CH], BF16)
            att2 = [fsb(f"att2{i}", [P, CH], BF16) for i in range(2)]
            xr2 = [fsb(f"xr2{i}", [P, CH], F32) for i in range(2)]
            ost, ost_b = fsb("ost", [P, CH], F32)
            sfm, sfm_b = fsb("sfm", [P, CH], F32)

            for half in range(2):
                wdx, wd_b2 = wdp[half]
                h0 = half * nhh
                src2 = wdT.rearrange("(hc p) j -> p hc j", p=P)[
                    :, h0:h0 + nhh, 0:P]
                E.dma("qs", lambda e, wdx=wdx, src2=src2: e.dma_start(
                    out=wdx[:].rearrange("p (hc j) -> p hc j", j=P),
                    in_=src2),
                    reads=[wdT_B], writes=[wd_b2])
            for hc in range(n_hc):
                wgx, wg_b2 = wgp[hc % 2]
                wux, wu_b2 = wup[hc % 2]
                for (wx, wxb, srcT, srcB) in ((wgx, wg_b2, wgT, wgT_B),
                                              (wux, wu_b2, wuT, wuT_B)):
                    src = srcT.rearrange("(dk p) j -> p dk j", p=P)[
                        :, :, hc * P:(hc + 1) * P]
                    E.dma("qs", lambda e, wx=wx, src=src: e.dma_start(
                        out=wx[:].rearrange("p (dk j) -> p dk j", j=P),
                        in_=src),
                        reads=[srcB], writes=[wxb])
                bg = pb[(hc % 2) * 2]
                bu = pb[(hc % 2) * 2 + 1]
                for (bank, wx, wxb) in ((bg, wgx, wg_b2), (bu, wux, wu_b2)):
                    fns = []
                    for dk in range(n_dc):
                        fns.append(lambda t, dk=dk, wx=wx, bank=bank:
                                   t.matmul(
                            bank[0][:], wx[:, dk * P:(dk + 1) * P],
                            xnfm[:, dk * CH:(dk + 1) * CH],
                            start=(dk == 0), stop=(dk == n_dc - 1)))
                    E.mm(fns, reads=[wxb, xnfm_b], writes=[bank[1]])
                E.op("act", lambda e, bg=bg: e.activation(
                    zz[:], bg[0][:], AF.Silu),
                    reads=[bg[1]], writes=[zz_b])
                E.op("dve", lambda e, hc=hc, bu=bu: e.tensor_tensor(
                    gated[:, hc * CH:(hc + 1) * CH], zz[:], bu[0][:],
                    ALU.mult),
                    reads=[zz_b, bu[1]], wpart=[gated_b])

            for dc in range(n_dc):
                bank = pb[4 + dc % 2]
                for half in range(2):
                    wdx, wd_b2 = wdp[half]
                    h0 = half * nhh
                    if dc > 0:
                        src2 = wdT.rearrange("(hc p) j -> p hc j", p=P)[
                            :, h0:h0 + nhh, dc * P:(dc + 1) * P]
                        E.dma("qs", lambda e, wdx=wdx, src2=src2:
                              e.dma_start(
                                  out=wdx[:].rearrange(
                                      "p (hc j) -> p hc j", j=P),
                                  in_=src2),
                              reads=[wdT_B], writes=[wd_b2])
                    fns = []
                    for hh in range(nhh):
                        hc = h0 + hh
                        fns.append(lambda t, hh=hh, hc=hc, wdx=wdx,
                                   bank=bank: t.matmul(
                            bank[0][:], wdx[:, hh * P:(hh + 1) * P],
                            gated[:, hc * CH:(hc + 1) * CH],
                            start=(hc == 0), stop=(hc == n_hc - 1)))
                    if half == 0:
                        E.mm(fns, reads=[wd_b2, gated_b], writes=[bank[1]])
                    else:
                        E.mm(fns, reads=[wd_b2, gated_b], wpart=[bank[1]])
                at, at_b = att2[dc % 2]
                E.dma("qs", lambda e, at=at, dc=dc: e.dma_start(
                    out=at[:], in_=att_scr[dc * P:(dc + 1) * P, :]),
                    reads=[att_B], writes=[at_b])
                xr, xr_b = xr2[dc % 2]
                xsrc = x_d[c * CH:(c + 1) * CH,
                           dc * P:(dc + 1) * P].rearrange(
                    "(tc p) d -> p tc d", p=P)
                E.dma("qs", lambda e, xr=xr, xsrc=xsrc: e.dma_start(
                    out=xr[:].rearrange("p (tc d) -> p tc d", d=P),
                    in_=xsrc), writes=[xr_b])
                E.op("dve", lambda e, bank=bank, at=at: e.tensor_tensor(
                    sfm[:], bank[0][:], at[:], ALU.add),
                    reads=[bank[1], at_b], writes=[sfm_b])
                tb = pb[6]
                fns = [lambda t, tc=tc, tb=tb: t.transpose(
                    tb[0][:, tc * P:(tc + 1) * P],
                    sfm[:, tc * P:(tc + 1) * P], ident_f[:])
                    for tc in range(4)]
                E.mm(fns, reads=[sfm_b, ident_f_b], writes=[tb[1]])
                E.op("dve", lambda e, tb=tb, xr=xr: e.tensor_tensor(
                    ost[:].rearrange("p (tc d) -> p tc d", d=P),
                    tb[0][:, 0:CH].rearrange("p (tc d) -> p tc d", d=P),
                    xr[:].rearrange("p (tc d) -> p tc d", d=P), ALU.add),
                    reads=[tb[1], xr_b], writes=[ost_b])
                ydst = y_d[c * CH:(c + 1) * CH, dc * P:(dc + 1) * P]
                E.dma("qp", lambda e, ydst=ydst: e.dma_start(
                    out=ydst.rearrange("(tc p) d -> p tc d", p=P),
                    in_=ost[:].rearrange("p (tc d) -> p tc d", d=P)),
                    reads=[ost_b])
                if c + 1 < n_chunks:
                    step = max(1, n_dc // 4)
                    if n_dc >= 8:
                        for tc_ in range(4):
                            if dc == tc_ * step:
                                emit_P1_chain(c + 1, tc_)
                            if dc == tc_ * step + step // 2:
                                emit_P1_tr(c + 1, tc_)
                    else:
                        for tc_ in range(4):
                            if dc == min(tc_ * step, n_dc - 1):
                                emit_P1_tc(c + 1, tc_)
            fs.close()
            E.fence()

        E.finish()

    return nc


_CACHE = {}


def _get_nc(key):
    if key not in _CACHE:
        _CACHE[key] = build_nc(*key)
    return _CACHE[key]


def kernel(L0_attn_x, L0_attn_w0, L0_attn_w1, L0_attn_w2, L0_attn_W_qkv,
           L0_attn_W_lr, L0_attn_b_lr, L0_attn_W_proj, L0_attn_onorm_w,
           L0_ffn_w_gate, L0_ffn_w_up, L0_ffn_w_down):
    x = np.ascontiguousarray(np.asarray(L0_attn_x, dtype=np.float32))
    B = x.shape[0]
    nc = _get_nc((16, 8, 64))
    common = dict(
        w0=np.ascontiguousarray(L0_attn_w0, np.float32),
        w1=np.ascontiguousarray(L0_attn_w1, np.float32),
        w2=np.ascontiguousarray(L0_attn_w2, np.float32),
        wqkv=np.ascontiguousarray(L0_attn_W_qkv, np.float32),
        wlr=np.ascontiguousarray(L0_attn_W_lr, np.float32),
        blr=np.ascontiguousarray(L0_attn_b_lr, np.float32),
        wproj=np.ascontiguousarray(L0_attn_W_proj, np.float32),
        onw=np.ascontiguousarray(L0_attn_onorm_w, np.float32),
        wg=np.ascontiguousarray(L0_ffn_w_gate, np.float32),
        wu=np.ascontiguousarray(L0_ffn_w_up, np.float32),
        wd=np.ascontiguousarray(L0_ffn_w_down, np.float32),
    )
    in_maps = [dict(common, x=np.ascontiguousarray(x[i])) for i in range(B)]
    res = run_bass_kernel_spmd(nc, in_maps, core_ids=list(range(B)))
    return np.stack([np.asarray(r["y"], dtype=np.float32)
                     for r in res.results], axis=0)


# version 13
# speedup vs baseline: 1.1924x; 1.1924x over previous
import math
from contextlib import ExitStack

import numpy as np

import concourse.bass as bass
import concourse.mybir as mybir
from concourse.bass_utils import run_bass_kernel_spmd

F32 = mybir.dt.float32
BF16 = mybir.dt.bfloat16
I32 = mybir.dt.int32
AF = mybir.ActivationFunctionType
ALU = mybir.AluOpType
X = mybir.AxisListType.X

NS_A, NS_B, NS_C = 3.4445, -4.775, 2.0315
BASE_LR_INV = float(np.log(np.expm1(0.01)))
MAGIC = 0x5F3759DF
TINY = 1e-37

CH = 512
HD = 128
P = 128


class Buf:
    __slots__ = ("writers", "readers", "name")

    def __init__(self, name=""):
        self.writers = []
        self.readers = []
        self.name = name


class Emit:

    N_QS = 16
    N_QP = 8

    def __init__(self, nc, stack, max_ops=None):
        self.max_ops = max_ops
        self.ncalls = 0
        self.nc = nc
        self.sems = {}
        self.cnt = {}
        for name in ("pe", "act", "dve", "pool"):
            self.sems[name] = stack.enter_context(nc.semaphore("sem_" + name))
            self.cnt[name] = 0
        for i in range(self.N_QS):
            self.sems[f"qs{i}"] = stack.enter_context(
                nc.semaphore(f"sem_qs{i}"))
            self.cnt[f"qs{i}"] = 0
        for i in range(self.N_QP):
            self.sems[f"qp{i}"] = stack.enter_context(
                nc.semaphore(f"sem_qp{i}"))
            self.cnt[f"qp{i}"] = 0
        self.rr = {"qs": 0, "qp": 0}
        self.eng = {"pe": nc.tensor, "act": nc.scalar, "dve": nc.vector,
                    "pool": nc.gpsimd, "sp": nc.sync}
        self.own = {"pe": "pe", "act": "act", "dve": "dve", "pool": "pool"}
        self.floors = {e: {} for e in self.eng}
        self.nwaits = 0

    def _wait(self, ename, sem_name, val):
        if val <= 0:
            return
        fl = self.floors[ename]
        if fl.get(sem_name, 0) >= val:
            return
        self.eng[ename].wait_ge(self.sems[sem_name], val)
        fl[sem_name] = val
        self.nwaits += 1

    def _need(self, ename, toks):
        best = {}
        for (s, v) in toks:
            if ename == "pe" and s == "pe":
                continue
            if best.get(s, 0) < v:
                best[s] = v
        for s in sorted(best):
            self._wait(ename, s, best[s])

    def _collect(self, reads, writes, wpart):
        toks = []
        for b in reads:
            toks += b.writers
        for b in writes:
            toks += b.readers + b.writers
        for b in wpart:
            toks += b.readers
        return toks

    def _apply(self, tok, reads, writes, wpart):
        for b in reads:
            b.readers.append(tok)
        for b in writes:
            b.writers = [tok]
            b.readers = []
        for b in wpart:
            b.writers.append(tok)

    def op(self, ename, fn, reads=(), writes=(), wpart=()):
        self.ncalls += 1
        if self.max_ops is not None and self.ncalls > self.max_ops:
            return ("pe", 0)
        self._need(ename, self._collect(reads, writes, wpart))
        instr = fn(self.eng[ename])
        sem = self.own[ename]
        self.cnt[sem] += 1
        instr.then_inc(self.sems[sem], 1)
        tok = (sem, self.cnt[sem])
        self._apply(tok, reads, writes, wpart)
        return tok

    def mm(self, fns, reads=(), writes=(), wpart=()):
        self.ncalls += 1
        if self.max_ops is not None and self.ncalls > self.max_ops:
            return ("pe", 0)
        self._need("pe", self._collect(reads, writes, wpart))
        instrs = [fn(self.nc.tensor) for fn in fns]
        self.cnt["pe"] += 1
        instrs[-1].then_inc(self.sems["pe"], 1)
        tok = ("pe", self.cnt["pe"])
        self._apply(tok, reads, writes, wpart)
        return tok

    def dma(self, ring, fn, reads=(), writes=(), wpart=()):
        self.ncalls += 1
        if self.max_ops is not None and self.ncalls > self.max_ops:
            return ("pe", 0)
        ename = "sp" if ring == "qs" else "pool"
        n = self.N_QS if ring == "qs" else self.N_QP
        sem = f"{ring}{self.rr[ring] % n}"
        self.rr[ring] += 1
        self._wait(ename, sem, self.cnt[sem])
        self._need(ename, self._collect(reads, writes, wpart))
        instr = fn(self.eng[ename])
        self.cnt[sem] += 16
        instr.then_inc(self.sems[sem], 16)
        tok = (sem, self.cnt[sem])
        self._apply(tok, reads, writes, wpart)
        return tok

    def fence(self):
        vals = dict(self.cnt)
        for e in self.eng:
            for s, v in vals.items():
                if self.own.get(e) == s:
                    continue
                self._wait(e, s, v)

    def finish(self):
        for s, v in self.cnt.items():
            self._wait("sp", s, v)


def _rsqrt(E, ename, out, out_b, s_ap, s_b, t1, t1_b, magic_ap, magic_b,
           iters=2):
    ov = out.bitcast(I32)
    sv = s_ap.bitcast(I32)
    E.op(ename, lambda e: e.tensor_single_scalar(ov, sv, 1,
                                                 ALU.arith_shift_right),
         reads=[s_b], writes=[out_b])
    E.op(ename, lambda e: e.tensor_tensor(ov, magic_ap, ov, ALU.subtract),
         reads=[magic_b, out_b], writes=[out_b])
    for _ in range(iters):
        E.op(ename, lambda e: e.tensor_tensor(t1, out, out, ALU.mult),
             reads=[out_b], writes=[t1_b])
        E.op(ename, lambda e: e.scalar_tensor_tensor(
            t1, t1, -0.5, s_ap, ALU.mult, ALU.mult),
            reads=[t1_b, s_b], writes=[t1_b])
        E.op(ename, lambda e: e.scalar_tensor_tensor(
            out, t1, 1.5, out, ALU.add, ALU.mult),
            reads=[out_b, t1_b], writes=[out_b])



def _inv_sqrt_eps(E, ename, out, out_b, s_ap, s_b, t1, t1_b, magic_ap,
                  magic_b, eps):
    _rsqrt(E, ename, out, out_b, s_ap, s_b, t1, t1_b, magic_ap, magic_b)
    E.op(ename, lambda e: e.tensor_scalar(t1, out, -eps, 1.0, ALU.mult,
                                          ALU.add),
         reads=[out_b], writes=[t1_b])
    E.op(ename, lambda e: e.tensor_tensor(out, out, t1, ALU.mult),
         reads=[out_b, t1_b], writes=[out_b])


def build_nc(n_dc=16, n_chunks=8, n_hc=64, ts_hop=4096,
             max_ops=None):
    D = n_dc * P
    HID = n_hc * P
    L = n_chunks * CH
    NH = n_dc
    D3 = 3 * D
    JW = 256
    n_jg = D3 // JW
    invD = 1.0 / D

    nc = bass.Bass("TRN2", target_bir_lowering=False)
    g = ExitStack()

    x_d = nc.dram_tensor("x", [L, D], F32, kind="ExternalInput").ap()
    w0_d = nc.dram_tensor("w0", [NH, HD, HD], F32, kind="ExternalInput").ap()
    w1_d = nc.dram_tensor("w1", [NH, HD, HD], F32, kind="ExternalInput").ap()
    w2_d = nc.dram_tensor("w2", [NH, HD, HD], F32, kind="ExternalInput").ap()
    wqkv_d = nc.dram_tensor("wqkv", [D, D3], F32, kind="ExternalInput").ap()
    wlr_d = nc.dram_tensor("wlr", [D, 3 * NH], F32, kind="ExternalInput").ap()
    blr_d = nc.dram_tensor("blr", [3 * NH], F32, kind="ExternalInput").ap()
    wproj_d = nc.dram_tensor("wproj", [D, D], F32, kind="ExternalInput").ap()
    onw_d = nc.dram_tensor("onw", [HD], F32, kind="ExternalInput").ap()
    wg_d = nc.dram_tensor("wg", [HID, D], F32, kind="ExternalInput").ap()
    wu_d = nc.dram_tensor("wu", [HID, D], F32, kind="ExternalInput").ap()
    wd_d = nc.dram_tensor("wd", [D, HID], F32, kind="ExternalInput").ap()
    y_d = nc.dram_tensor("y", [L, D], F32, kind="ExternalOutput").ap()

    wqkv_bf = nc.dram_tensor("wqkv_bf", [D, D3], BF16, kind="Internal").ap()
    wproj_bf = nc.dram_tensor("wproj_bf", [D, D], BF16, kind="Internal").ap()
    wg_blk = nc.dram_tensor("wg_blk", [n_dc, HID, P], BF16,
                            kind="Internal").ap()
    wu_blk = nc.dram_tensor("wu_blk", [n_dc, HID, P], BF16,
                            kind="Internal").ap()
    wd_blk = nc.dram_tensor("wd_blk", [n_hc, D, P], BF16,
                            kind="Internal").ap()
    wgT = nc.dram_tensor("wgT", [D, HID], BF16, kind="Internal").ap()
    wuT = nc.dram_tensor("wuT", [D, HID], BF16, kind="Internal").ap()
    wdT = nc.dram_tensor("wdT", [HID, D], BF16, kind="Internal").ap()
    att_scr = nc.dram_tensor("att_scr", [D, CH], BF16, kind="Internal").ap()

    import ml_dtypes
    ident_f_d = nc.inline_tensor(np.eye(P, dtype=np.float32), "c_ident_f").ap()
    ident_b_d = nc.inline_tensor(np.eye(P, dtype=ml_dtypes.bfloat16),
                                 "c_ident_b").ap()
    ones_f_d = nc.inline_tensor(np.ones((P, P), np.float32), "c_ones_f").ap()
    ones_b_d = nc.inline_tensor(np.ones((P, P), ml_dtypes.bfloat16),
                                "c_ones_b").ap()
    aI_d = nc.inline_tensor((NS_A * np.eye(P)).astype(ml_dtypes.bfloat16),
                            "c_aI_b").ap()
    magic_d = nc.inline_tensor(np.full((P, 16), MAGIC, np.int32),
                               "c_magic").ap()
    magic64_d = nc.inline_tensor(np.full((P, 4 * NH), MAGIC, np.int32),
                                 "c_magic64").ap()

    def sb(name, shape, dtype, st=g):
        t = st.enter_context(nc.sbuf_tensor(name, shape, dtype))
        return t, Buf(name)

    ident_f, ident_f_b = sb("ident_f", [P, P], F32)
    ident_b, ident_b_b = sb("ident_b", [P, P], BF16)
    ones_f, ones_f_b = sb("ones_f", [P, P], F32)
    ones_b, ones_b_b = sb("ones_b", [P, P], BF16)
    aI, aI_b = sb("aI", [P, P], BF16)
    magic, magic_b = sb("magic", [P, 16], I32)
    magic64, magic64_b = sb("magic64", [P, 4 * NH], I32)
    blr_row, blr_row_b = sb("blr_row", [1, 3 * NH], F32)
    gam, gam_b = sb("gam", [P, 1], F32)
    wlr_bf, wlr_bf_b = sb("wlr_bf", [P, n_dc * 3 * NH], BF16)

    m0, m0_b = sb("m0", [P, NH * HD], F32)
    m1, m1_b = sb("m1", [P, NH * HD], F32)
    m2, m2_b = sb("m2", [P, NH * HD], F32)
    wn, wn_b = sb("wn", [P, 3 * NH], F32)
    w0n, w0n_b = sb("w0n", [P, NH * HD], BF16)
    w2n, w2n_b = sb("w2n", [P, NH * HD], BF16)
    w1t, w1t_b = sb("w1t", [P, NH * HD], BF16)
    w1n2, w1n2_b = sb("w1n2", [P, NH * HD], BF16)

    xnfm, xnfm_b = sb("xnfm", [P, n_dc * CH], BF16)
    lrf, lrf_b = sb("lrf", [P, 4 * 3 * NH], F32)
    rmsS, rmsS_b = sb("rmsS", [P, 4 * NH], F32)
    colsR, colsR_b = sb("colsR", [P, 4 * NH], F32)
    rmsT, rmsT_b = sb("rmsT", [P, 4 * NH], F32)

    xf = [sb(f"xf{i}", [P, D], F32) for i in range(2)]
    stg, stg_b = sb("stg", [P, max(D, 2 * CH)], BF16)
    colA, colA_b = sb("colA", [P, 16], F32)
    colB, colB_b = sb("colB", [P, 16], F32)
    colC, colC_b = sb("colC", [P, 16], F32)
    colD, colD_b = sb("colD", [P, 16], F32)
    rowA, rowA_b = sb("rowA", [1, CH], F32)
    TS_HOP = min(ts_hop, HID, D)
    tT, tT_b = sb("hop2stg", [P, TS_HOP], BF16)

    pb = []
    for i in range(8):
        t = g.enter_context(nc.psum_tensor(f"pb{i}", [P, 512], F32))
        pb.append((t, Buf(f"pb{i}")))

    def pbf(i):
        return pb[i][0].ap().bitcast(BF16)

    E = Emit(nc, g, max_ops=max_ops)

    with nc.Block():
        for (dst, dst_b, src) in ((ident_f, ident_f_b, ident_f_d),
                                  (ident_b, ident_b_b, ident_b_d),
                                  (ones_f, ones_f_b, ones_f_d),
                                  (ones_b, ones_b_b, ones_b_d),
                                  (aI, aI_b, aI_d),
                                  (magic, magic_b, magic_d),
                                  (magic64, magic64_b, magic64_d)):
            E.dma("qs", lambda e, d=dst, s=src: e.dma_start(out=d[:], in_=s),
                  writes=[dst_b])
        E.dma("qs", lambda e: e.dma_start(out=blr_row[:], in_=blr_d[None, :]),
              writes=[blr_row_b])
        E.dma("qs", lambda e: e.dma_start(out=gam[:], in_=onw_d[:, None]),
              writes=[gam_b])
        E.op("dve", lambda e: e.tensor_scalar(
            blr_row[:], blr_row[:], BASE_LR_INV, None, ALU.add),
            reads=[blr_row_b], writes=[blr_row_b])
        for dk in range(n_dc):
            E.dma("qp", lambda e, dk=dk: e.dma_start(
                out=wlr_bf[:, dk * 3 * NH:(dk + 1) * 3 * NH],
                in_=wlr_d[dk * P:(dk + 1) * P, :]),
                wpart=[wlr_bf_b])

        wqkv_bf_B = Buf("wqkv_bf")
        wproj_bf_B = Buf("wproj_bf")
        E.dma("qp", lambda e: e.dma_start(out=wqkv_bf[:], in_=wqkv_d[:]),
              writes=[wqkv_bf_B])
        wg_blk_B = [Buf(f"wg_blk{i}") for i in range(n_dc)]
        wu_blk_B = [Buf(f"wu_blk{i}") for i in range(n_dc)]
        wd_blk_B = [Buf(f"wd_blk{i}") for i in range(n_hc)]
        for dk in range(n_dc):
            E.dma("qp", lambda e, dk=dk: e.dma_start(
                out=wg_blk[dk], in_=wg_d[:, dk * P:(dk + 1) * P]),
                writes=[wg_blk_B[dk]])
            E.dma("qp", lambda e, dk=dk: e.dma_start(
                out=wu_blk[dk], in_=wu_d[:, dk * P:(dk + 1) * P]),
                writes=[wu_blk_B[dk]])
        E.dma("qp", lambda e: e.dma_start(out=wproj_bf[:], in_=wproj_d[:]),
              writes=[wproj_bf_B])
        for hc in range(n_hc):
            E.dma("qp", lambda e, hc=hc: e.dma_start(
                out=wd_blk[hc], in_=wd_d[:, hc * P:(hc + 1) * P]),
                writes=[wd_blk_B[hc]])

        def emit_P1_chain(cc, tc):
            if True:
                xt, xt_b = xf[tc % 2]
                r0 = cc * CH + tc * P
                E.dma("qs", lambda e, xt=xt, r0=r0: e.dma_start(
                    out=xt[:], in_=x_d[r0:r0 + P, :]), writes=[xt_b])
                E.op("dve", lambda e, xt=xt, tc=tc: e.tensor_reduce(
                    colA[:, tc:tc + 1], xt[:], X, ALU.add),
                    reads=[xt_b], wpart=[colA_b])
                E.op("act", lambda e, xt=xt, tc=tc: e.activation(
                    stg[:, 0:D], xt[:], AF.Square,
                    accum_out=colB[:, tc:tc + 1]),
                    reads=[xt_b], writes=[stg_b], wpart=[colB_b])
                E.op("dve", lambda e, tc=tc: e.tensor_scalar(
                    colC[:, 0:1], colA[:, tc:tc + 1], invD, None, ALU.mult),
                    reads=[colA_b], writes=[colC_b])
                E.op("dve", lambda e: e.tensor_tensor(
                    colD[:, 0:1], colC[:, 0:1], colC[:, 0:1], ALU.mult),
                    reads=[colC_b], writes=[colD_b])
                E.op("dve", lambda e, tc=tc: e.scalar_tensor_tensor(
                    colD[:, 1:2], colB[:, tc:tc + 1], invD, colD[:, 0:1],
                    ALU.mult, ALU.subtract),
                    reads=[colB_b, colD_b], wpart=[colD_b])
                E.op("dve", lambda e: e.tensor_scalar(
                    colD[:, 1:2], colD[:, 1:2], 1e-5, None, ALU.add),
                    reads=[colD_b], wpart=[colD_b])
                _rsqrt(E, "dve", colC[:, 1:2], colC_b, colD[:, 1:2], colD_b,
                       colD[:, 2:3], colD_b, magic[:, 0:1], magic_b)
                E.op("dve", lambda e: e.scalar_tensor_tensor(
                    colD[:, 3:4], colC[:, 0:1], -1.0, colC[:, 1:2],
                    ALU.mult, ALU.mult),
                    reads=[colC_b], wpart=[colD_b])
                E.op("act", lambda e, xt=xt: e.activation(
                    stg[:, 0:D], xt[:], AF.Identity,
                    bias=colD[:, 3:4], scale=colC[:, 1:2]),
                    reads=[xt_b, colC_b, colD_b], writes=[stg_b])

        def emit_P1_tr(cc, tc):
            if True:
                for dk0 in range(0, n_dc, 4):
                    ng = min(4, n_dc - dk0)
                    fns = []
                    for q in range(ng):
                        dk = dk0 + q
                        fns.append(lambda t, dk=dk, q=q: t.transpose(
                            pbf(7)[:, q * P:(q + 1) * P],
                            stg[:, dk * P:(dk + 1) * P], ident_b[:]))
                    E.mm(fns, reads=[stg_b, ident_b_b], writes=[pb[7][1]])
                    dst = xnfm[:].rearrange("p (dk t) -> p dk t", t=CH)[
                        :, dk0:dk0 + ng, tc * P:(tc + 1) * P]
                    E.op("dve", lambda e, dst=dst, ng=ng: e.tensor_copy(
                        dst, pbf(7)[:, 0:ng * P].rearrange(
                            "p (q t) -> p q t", t=P)),
                        reads=[pb[7][1]], wpart=[xnfm_b])



        def emit_P1_tc(cc, tc):
            emit_P1_chain(cc, tc)
            emit_P1_tr(cc, tc)

        def emit_P1(cc):
            for tc in range(4):
                emit_P1_tc(cc, tc)

        emit_P1(0)

        wtmp = [sb(f"wtmp{i}", [P, HD], F32) for i in range(2)]
        wns, wns_b = sb("wns", [P, 3 * NH], F32)
        for h in range(NH):
            for wi, (wdin, mast, mast_b, nat, nat_b) in enumerate(
                    ((w0_d, m0, m0_b, w0n, w0n_b),
                     (w1_d, m1, m1_b, w1n2, w1n2_b),
                     (w2_d, m2, m2_b, w2n, w2n_b))):
                wt, wt_b = wtmp[(h * 3 + wi) % 2]
                sl = slice(h * HD, (h + 1) * HD)
                E.dma("qs", lambda e, s=wdin, w=wt, h=h: e.dma_start(
                    out=w[:], in_=s[h]), writes=[wt_b])
                if nat is not None:
                    E.op("dve", lambda e, w=wt, n=nat, sl=sl:
                         e.tensor_copy(n[:, sl], w[:]),
                         reads=[wt_b], wpart=[nat_b])
                E.mm([lambda t, w=wt: t.transpose(pb[7][0][:, 0:HD], w[:],
                                                  ident_f[:])],
                     reads=[wt_b, ident_f_b], writes=[pb[7][1]])
                E.op("dve", lambda e, m=mast, sl=sl:
                     e.tensor_copy(m[:, sl], pb[7][0][:, 0:HD]),
                     reads=[pb[7][1]], wpart=[mast_b])
                if wi == 1:
                    E.op("dve", lambda e, sl=sl:
                         e.tensor_copy(w1t[:, sl], m1[:, sl]),
                         reads=[m1_b], wpart=[w1t_b])
                E.op("act", lambda e, m=mast, sl=sl, wi=wi:
                     e.activation(pb[7][0][:, HD:2 * HD], m[:, sl], AF.Square,
                                  accum_out=colA[:, wi:wi + 1]),
                     reads=[mast_b], writes=[pb[7][1], colA_b])
                E.op("dve", lambda e, wi=wi, h=h: e.tensor_scalar(
                    wn[:, wi * NH + h:wi * NH + h + 1], colA[:, wi:wi + 1],
                    TINY, None, ALU.add),
                    reads=[colA_b], wpart=[wn_b])
        ncol = 3 * NH
        for c0 in range(0, ncol, 16):
            c1 = min(c0 + 16, ncol)
            _rsqrt(E, "dve", wns[:, c0:c1], wns_b, wn[:, c0:c1], wn_b,
                   colB[:, 0:c1 - c0], colB_b, magic[:, 0:c1 - c0], magic_b)
        E.op("dve", lambda e: e.tensor_tensor(wn[:, 0:ncol], wn[:, 0:ncol],
                                              wns[:, 0:ncol], ALU.mult),
             reads=[wn_b, wns_b], writes=[wn_b])

        wgT_B, wuT_B, wdT_B = Buf("wgT"), Buf("wuT"), Buf("wdT")
        _hop2 = {"g": (wg_blk, wg_blk_B, wgT, wgT_B, n_dc, HID),
                 "u": (wu_blk, wu_blk_B, wuT, wuT_B, n_dc, HID),
                 "d": (wd_blk, wd_blk_B, wdT, wdT_B, n_hc, D)}

        def emit_hop2(which):
            blk, blkB, dstT, dstB, nblk, rows = _hop2[which]
            for k in range(nblk):
                for r0 in range(0, rows, TS_HOP):
                    r1 = min(r0 + TS_HOP, rows)
                    E.dma("qs", lambda e, k=k, blk=blk, r0=r0, r1=r1:
                          e.dma_start_transpose(tT[:, 0:r1 - r0],
                                                blk[k][r0:r1, :]),
                          reads=[blkB[k]], writes=[tT_b])
                    E.dma("qp", lambda e, k=k, d=dstT, r0=r0, r1=r1:
                          e.dma_start(out=d[k * P:(k + 1) * P, r0:r1],
                                      in_=tT[:, 0:r1 - r0]),
                          reads=[tT_b], wpart=[dstB])


        for c in range(n_chunks):
            cs = ExitStack()

            def csb(name, shape, dtype):
                return sb(f"{name}_c{c}", shape, dtype, cs)

            qkv, qkv_b = csb("qkv", [P, 4 * D3], BF16)
            qkv_bJ = [Buf(f"qkv_c{c}_j{j}") for j in range(n_jg)]
            jg_of = lambda si, h: (si * D + h * HD) // JW
            wst = [csb(f"wst{i}", [P, n_dc * JW], BF16) for i in range(2)]
            onc, onc_b = csb("onc", [P, NH * CH], BF16)
            gaten, gaten_b = csb("gaten", [P, CH], F32)
            hbn, hbn_b = csb("hbn", [P, CH], F32)
            sig, sig_b = csb("sig", [P, CH], F32)
            f1s, f1s_b = csb("f1s", [P, CH], F32)
            sg, sg_b = csb("sg", [P, CH], BF16)
            hl1, hl1_b = csb("hl1", [P, CH], BF16)
            dh2, dh2_b = csb("dh2", [P, CH], BF16)
            dg0, dg0_b = csb("dg0", [P, CH], BF16)
            zss, zss_b = csb("zss", [P, CH], BF16)
            zfm, zfm_b = csb("zfm", [P, CH], BF16)
            sqs, sqs_b = csb("sqs", [P, CH], BF16)
            bcs, bcs_b = csb("bcs", [P, CH], F32)
            sqb, sqb_b = csb("sqb", [P, D], BF16)
            ew2, ew2_b = csb("ew2", [P, CH], F32)
            nsx = []
            for i in range(3):
                nsx.append({k: csb(f"ns_{k}{i}", [P, HD],
                                   BF16 if k in ("x", "xt", "as", "bm")
                                   else F32)
                            for k in ("x", "xt", "as", "bas", "bm")})
            kfm = [csb(f"kfm{i}", [P, CH], BF16) for i in range(2)]
            vfm = [csb(f"vfm{i}", [P, CH], BF16) for i in range(2)]
            qfm = [csb(f"qfm{i}", [P, CH], BF16) for i in range(2)]

            za = 3 * NH
            for tc in range(4):
                fns = []
                for dk in range(n_dc):
                    fns.append(lambda t, dk=dk, tc=tc: t.matmul(
                        pb[6][0][:, 0:za],
                        xnfm[:, dk * CH + tc * P:dk * CH + (tc + 1) * P],
                        wlr_bf[:, dk * za:(dk + 1) * za],
                        start=(dk == 0), stop=False))
                fns.append(lambda t: t.matmul(
                    pb[6][0][:, 0:za], ones_f[0:1, :],
                    blr_row[:], start=False, stop=True))
                E.mm(fns, reads=[xnfm_b, wlr_bf_b, ones_f_b, blr_row_b],
                     writes=[pb[6][1]])
                E.op("act", lambda e: e.activation(
                    gaten[:, 0:za], pb[6][0][:, 0:za], AF.Abs),
                    reads=[pb[6][1]], writes=[gaten_b])
                E.op("act", lambda e: e.activation(
                    gaten[:, 0:za], gaten[:, 0:za], AF.Exp, scale=-1.0),
                    reads=[gaten_b], writes=[gaten_b])
                E.op("act", lambda e: e.activation(
                    gaten[:, 0:za], gaten[:, 0:za], AF.Ln, bias=1.0),
                    reads=[gaten_b], writes=[gaten_b])
                E.op("act", lambda e: e.activation(
                    hbn[:, 0:za], pb[6][0][:, 0:za], AF.Relu),
                    reads=[pb[6][1]], writes=[hbn_b])
                E.op("dve", lambda e, tc=tc: e.tensor_tensor(
                    lrf[:, tc * za:(tc + 1) * za], hbn[:, 0:za],
                    gaten[:, 0:za], ALU.add),
                    reads=[hbn_b, gaten_b], wpart=[lrf_b])

            for jg in range(n_jg):
                wq, wq_b = wst[jg % 2]
                src = wqkv_bf.rearrange("(dk p) j -> p dk j", p=P)[
                    :, :, jg * JW:(jg + 1) * JW]
                E.dma("qs", lambda e, wq=wq, src=src: e.dma_start(
                    out=wq[:].rearrange("p (dk j) -> p dk j", j=JW), in_=src),
                    reads=[wqkv_bf_B], writes=[wq_b])
                for tc in range(4):
                    bank = pb[(jg * 4 + tc) % 2]
                    psl = bank[0][:, 0:JW]
                    fns = []
                    for dk in range(n_dc):
                        fns.append(lambda t, dk=dk, tc=tc, psl=psl, wq=wq:
                                   t.matmul(
                            psl, xnfm[:, dk * CH + tc * P:
                                      dk * CH + (tc + 1) * P],
                            wq[:, dk * JW:(dk + 1) * JW],
                            start=(dk == 0), stop=(dk == n_dc - 1)))
                    E.mm(fns, reads=[xnfm_b, wq_b], writes=[bank[1]])
                    E.op("act", lambda e, psl=psl, tc=tc, jg=jg: e.activation(
                        qkv[:, tc * D3 + jg * JW:tc * D3 + (jg + 1) * JW],
                        psl, AF.Silu),
                        reads=[bank[1]], wpart=[qkv_bJ[jg]])

            if c == 0:
                emit_hop2("g")
                emit_hop2("u")

            for tc in range(4):
                for si, (dst, dst_b) in enumerate(((colA, colA_b),
                                                   (colB, colB_b))):
                    qs0 = tc * D3 + si * D
                    E.op("dve", lambda e, qs0=qs0: e.tensor_tensor(
                        sqb[:], qkv[:, qs0:qs0 + D], qkv[:, qs0:qs0 + D],
                        ALU.mult),
                        reads=qkv_bJ[si * (D // JW):(si + 1) * (D // JW)],
                        writes=[sqb_b])
                    E.op("dve", lambda e, dst=dst: e.tensor_reduce(
                        dst[:, 0:NH], sqb[:].rearrange(
                            "p (h e) -> p h e", e=HD), X, ALU.add),
                        reads=[sqb_b], writes=[dst_b])
                    E.op("dve", lambda e, dst=dst: e.tensor_scalar(
                        dst[:, 0:NH], dst[:, 0:NH], TINY, None, ALU.add),
                        reads=[dst_b], wpart=[dst_b])
                for si, (src, src_b) in enumerate(((colA, colA_b),
                                                   (colB, colB_b))):
                    _inv_sqrt_eps(E, "dve", colD[:, 0:NH], colD_b,
                                  src[:, 0:NH], src_b, colC[:, 0:NH], colC_b,
                                  magic[:, 0:NH], magic_b, 1e-5)
                    for h in range(NH):
                        qs0 = tc * D3 + si * D + h * HD
                        jb = qkv_bJ[jg_of(si, h)]
                        E.op("dve", lambda e, qs0=qs0, h=h:
                             e.tensor_scalar(
                                 qkv[:, qs0:qs0 + HD],
                                 qkv[:, qs0:qs0 + HD],
                                 colD[:, h:h + 1], None, ALU.mult),
                             reads=[colD_b, jb], wpart=[jb])

            def head_TF(h):
                par = h % 2
                kf, kf_b = kfm[par]
                vf, vf_b = vfm[par]
                qf, qf_b = qfm[par]
                hsl = slice(h * HD, (h + 1) * HD)
                for (si, dst, dst_b) in ((1, kf, kf_b), (2, vf, vf_b),
                                         (0, qf, qf_b)):
                    fns = []
                    for tc in range(4):
                        src = qkv[:, tc * D3 + si * D + h * HD:
                                  tc * D3 + si * D + (h + 1) * HD]
                        fns.append(lambda t, src=src, tc=tc: t.transpose(
                            pbf(7)[:, tc * P:(tc + 1) * P], src, ident_b[:]))
                    E.mm(fns, reads=[qkv_bJ[jg_of(si, h)], ident_b_b],
                         writes=[pb[7][1]])
                    E.op("dve", lambda e, dst=dst: e.tensor_copy(
                        dst[:], pbf(7)[:, 0:512]),
                        reads=[pb[7][1]], writes=[dst_b])
                for (bank, wmat, wmat_b, opfm, opfm_b) in (
                        (pb[0], w0n, w0n_b, kf, kf_b),
                        (pb[1], w2n, w2n_b, kf, kf_b),
                        (pb[2], w1t, w1t_b, vf, vf_b)):
                    fns = []
                    for tc in range(4):
                        fns.append(lambda t, tc=tc, bank=bank, opfm=opfm,
                                   wmat=wmat, hsl=hsl: t.matmul(
                            bank[0][:, tc * P:(tc + 1) * P],
                            opfm[:, tc * P:(tc + 1) * P], wmat[:, hsl],
                            start=True, stop=True))
                    E.mm(fns, reads=[opfm_b, wmat_b], writes=[bank[1]])

            def head_E_early(h):
                E.op("act", lambda e: e.activation(sg[:], pb[0][0][:],
                                                   AF.Silu),
                     reads=[pb[0][1]], writes=[sg_b])
                E.op("act", lambda e: e.activation(sig[:], pb[0][0][:],
                                                   AF.Tanh, scale=0.5),
                     reads=[pb[0][1]], writes=[sig_b])
                E.op("dve", lambda e: e.tensor_scalar(
                    ew2[:], sg[:], -1.0, 1.0, ALU.mult, ALU.add),
                    reads=[sg_b], writes=[ew2_b])
                E.op("dve", lambda e: e.tensor_tensor(
                    ew2[:], pb[0][0][:], ew2[:], ALU.add),
                    reads=[pb[0][1], ew2_b], writes=[ew2_b])
                E.op("dve", lambda e: e.tensor_copy(hbn[:], pb[1][0][:]),
                     reads=[pb[1][1]], writes=[hbn_b])
                E.op("dve", lambda e: e.tensor_tensor(
                    f1s[:], pb[2][0][:], hbn[:], ALU.mult),
                    reads=[pb[2][1], hbn_b], writes=[f1s_b])
                for tc in range(4):
                    tsl = slice(tc * P, (tc + 1) * P)
                    E.op("dve", lambda e, tsl=tsl, tc=tc: e.tensor_scalar(
                        bcs[:, tsl], pb[2][0][:, tsl],
                        lrf[:, tc * za + 2 * NH + h:
                            tc * za + 2 * NH + h + 1], None, ALU.mult),
                        reads=[pb[2][1], lrf_b], wpart=[bcs_b])

            def head_O(h):
                par = h % 2
                qf, qf_b = qfm[par]
                hsl = slice(h * HD, (h + 1) * HD)
                E.mm([lambda t, hsl=hsl: t.matmul(
                    pb[0][0][:], w0n[:, hsl], qf[:], start=True, stop=True)],
                    reads=[w0n_b, qf_b], writes=[pb[0][1]])
                E.mm([lambda t, hsl=hsl: t.matmul(
                    pb[1][0][:], w2n[:, hsl], qf[:], start=True, stop=True)],
                    reads=[w2n_b, qf_b], writes=[pb[1][1]])
                E.op("act", lambda e: e.activation(zss[:], pb[0][0][:],
                                                   AF.Silu),
                     reads=[pb[0][1]], writes=[zss_b])
                E.op("dve", lambda e: e.tensor_tensor(
                    zfm[:], zss[:], pb[1][0][:], ALU.mult),
                    reads=[zss_b, pb[1][1]], writes=[zfm_b])
                E.mm([lambda t: t.matmul(pb[2][0][:], w1n2[:, hsl], zfm[:],
                                         start=True, stop=True)],
                     reads=[w1n2_b, zfm_b], writes=[pb[2][1]])
                E.op("act", lambda e: e.activation(sqs[:], pb[2][0][:],
                                                   AF.Square),
                     reads=[pb[2][1]], writes=[sqs_b])
                fns = [lambda t, tc=tc: t.matmul(
                    pb[6][0][:, 4 + tc:5 + tc],
                    sqs[:, tc * P:(tc + 1) * P], ones_b[:, 0:1],
                    start=True, stop=True) for tc in range(4)]
                E.mm(fns, reads=[sqs_b, ones_b_b], wpart=[pb[6][1]])
                E.op("dve", lambda e, h=h: e.tensor_scalar(
                    rmsS[:, h * 4:(h + 1) * 4], pb[6][0][:, 4:8], 1.0 / HD,
                    1e-5, ALU.mult, ALU.add),
                    reads=[pb[6][1]], wpart=[rmsS_b])
                E.op("act", lambda e, h=h: e.activation(
                    onc[:, h * CH:(h + 1) * CH], pb[2][0][:], AF.Copy,
                    scale=gam[:, 0:1]),
                    reads=[pb[2][1], gam_b], wpart=[onc_b])

            def head_E_late(h):
                E.op("dve", lambda e: e.tensor_scalar(
                    sig[:], sig[:], 0.5, 0.5, ALU.mult, ALU.add),
                    reads=[sig_b], writes=[sig_b])
                E.op("dve", lambda e: e.tensor_tensor(
                    sig[:], sig[:], ew2[:], ALU.mult),
                    reads=[sig_b, ew2_b], writes=[sig_b])
                E.op("dve", lambda e: e.tensor_tensor(
                    dh2[:], bcs[:], sg[:], ALU.mult),
                    reads=[bcs_b, sg_b], writes=[dh2_b])
                for tc in range(4):
                    tsl = slice(tc * P, (tc + 1) * P)
                    E.op("dve", lambda e, tsl=tsl, tc=tc:
                         e.scalar_tensor_tensor(
                             hl1[:, tsl], hbn[:, tsl],
                             lrf[:, tc * za + NH + h:tc * za + NH + h + 1],
                             sg[:, tsl], ALU.mult, ALU.mult),
                         reads=[hbn_b, lrf_b, sg_b], wpart=[hl1_b])
                    E.op("dve", lambda e, tsl=tsl, tc=tc:
                         e.scalar_tensor_tensor(
                             dg0[:, tsl], f1s[:, tsl],
                             lrf[:, tc * za + h:tc * za + h + 1],
                             sig[:, tsl], ALU.mult, ALU.mult),
                         reads=[f1s_b, lrf_b, sig_b], wpart=[dg0_b])

            def head_G(h):
                hsl = slice(h * HD, (h + 1) * HD)
                kts = [qkv[:, tc * D3 + D + hsl.start:
                           tc * D3 + D + hsl.stop] for tc in range(4)]
                vts = [qkv[:, tc * D3 + 2 * D + hsl.start:
                           tc * D3 + 2 * D + hsl.stop] for tc in range(4)]
                kb = qkv_bJ[jg_of(1, h)]
                vb = qkv_bJ[jg_of(2, h)]
                for gi, (bank, lhss, lhs_b, rhss, rhs_b) in enumerate((
                        (pb[3], [dg0[:, tc * P:(tc + 1) * P]
                                 for tc in range(4)], dg0_b, kts, kb),
                        (pb[4], vts, vb,
                         [hl1[:, tc * P:(tc + 1) * P]
                          for tc in range(4)], hl1_b),
                        (pb[5], [dh2[:, tc * P:(tc + 1) * P]
                                 for tc in range(4)], dh2_b, kts, kb))):
                    fns = []
                    for tc in range(4):
                        fns.append(lambda t, lh=lhss[tc], rh=rhss[tc],
                                   bank=bank, tc=tc:
                                   t.matmul(bank[0][:, 0:HD], lh, rh,
                                            start=(tc == 0),
                                            stop=(tc == 3)))
                    E.mm(fns, reads=[lhs_b, rhs_b], writes=[bank[1]])

            def head_NS(h):
                hsl = slice(h * HD, (h + 1) * HD)
                for i in range(3):
                    xg, xg_b = nsx[i]["x"]
                    E.op("act", lambda e, i=i, xg=xg: e.activation(
                        xg[:], pb[3 + i][0][:, 0:HD], AF.Copy),
                        reads=[pb[3 + i][1]], writes=[xg_b])
                    E.op("act", lambda e, i=i, xg=xg: e.activation(
                        sqs[:, 0:HD], xg[:], AF.Square,
                        accum_out=colA[:, i:i + 1]),
                        reads=[xg_b], writes=[sqs_b], wpart=[colA_b])
                fns = [lambda t, i=i: t.matmul(
                    pb[6][0][:, i:i + 1], ones_f[:], colA[:, i:i + 1],
                    start=True, stop=True) for i in range(3)]
                E.mm(fns, reads=[ones_f_b, colA_b], writes=[pb[6][1]])
                E.op("dve", lambda e: e.tensor_scalar(
                    colB[:, 0:3], pb[6][0][:, 0:3], TINY, None, ALU.add),
                    reads=[pb[6][1]], writes=[colB_b])
                _rsqrt(E, "dve", colC[:, 0:3], colC_b, colB[:, 0:3], colB_b,
                       colD[:, 0:3], colD_b, magic[:, 0:3], magic_b, iters=1)
                E.op("dve", lambda e: e.tensor_scalar(
                    colD[:, 0:3], colC[:, 0:3], -1e-7, 1.0, ALU.mult,
                    ALU.add),
                    reads=[colC_b], writes=[colD_b])
                E.op("dve", lambda e: e.tensor_tensor(
                    colC[:, 0:3], colC[:, 0:3], colD[:, 0:3], ALU.mult),
                    reads=[colC_b, colD_b], writes=[colC_b])
                E.op("dve", lambda e: e.tensor_tensor(
                    colB[:, 0:3], colC[:, 0:3], colC[:, 0:3], ALU.mult),
                    reads=[colC_b], writes=[colB_b])
                E.op("dve", lambda e: e.tensor_scalar(
                    colD[:, 0:3], colB[:, 0:3], NS_B, None, ALU.mult),
                    reads=[colB_b], writes=[colD_b])
                for it in range(2):
                    for i in range(3):
                        xg, xg_b = nsx[i]["x"]
                        xt, xt_b = nsx[i]["xt"]
                        E.mm([lambda t, xg=xg: t.transpose(
                            pbf(7)[:, 0:HD], xg[:], ident_b[:])],
                            reads=[xg_b, ident_b_b], writes=[pb[7][1]])
                        E.op("dve", lambda e, xt=xt: e.tensor_copy(
                            xt[:], pbf(7)[:, 0:HD]),
                            reads=[pb[7][1]], writes=[xt_b])
                    for i in range(3):
                        xt, xt_b = nsx[i]["xt"]
                        E.mm([lambda t, i=i, xt=xt: t.matmul(
                            pb[3 + i][0][:, 0:HD], xt[:], xt[:],
                            start=True, stop=True)],
                            reads=[xt_b], writes=[pb[3 + i][1]])
                    for i in range(3):
                        a_s, a_s_b = nsx[i]["as"]
                        E.op("act", lambda e, i=i, a_s=a_s: e.activation(
                            a_s[:], pb[3 + i][0][:, 0:HD], AF.Copy,
                            scale=colB[:, i:i + 1]),
                            reads=[pb[3 + i][1], colB_b], writes=[a_s_b])
                    for i in range(3):
                        a_s, a_s_b = nsx[i]["as"]
                        E.mm([lambda t, i=i, a_s=a_s: t.matmul(
                            pb[3 + i][0][:, HD:2 * HD], a_s[:], a_s[:],
                            start=True, stop=True)],
                            reads=[a_s_b], wpart=[pb[3 + i][1]])
                    for i in range(3):
                        bas, bas_b = nsx[i]["bas"]
                        E.op("dve", lambda e, i=i, bas=bas: e.tensor_scalar(
                            bas[:], pb[3 + i][0][:, HD:2 * HD], NS_C, None,
                            ALU.mult),
                            reads=[pb[3 + i][1]], writes=[bas_b])
                    for i in range(3):
                        bm, bm_b = nsx[i]["bm"]
                        bas, bas_b = nsx[i]["bas"]
                        E.op("dve", lambda e, i=i, bm=bm, bas=bas:
                             e.scalar_tensor_tensor(
                                 bm[:], pb[3 + i][0][:, 0:HD],
                                 colD[:, i:i + 1], bas[:],
                                 ALU.mult, ALU.add),
                             reads=[pb[3 + i][1], colD_b, bas_b],
                             writes=[bm_b])
                    for i in range(3):
                        bm, bm_b = nsx[i]["bm"]
                        xg, xg_b = nsx[i]["x"]
                        E.mm([lambda t, i=i, bm=bm, xg=xg: t.matmul(
                            pb[3 + i][0][:, 2 * HD:3 * HD], bm[:], xg[:],
                            start=True, stop=False),
                            lambda t, i=i, xg=xg: t.matmul(
                            pb[3 + i][0][:, 2 * HD:3 * HD], aI[:], xg[:],
                            start=False, stop=True)],
                            reads=[bm_b, xg_b, aI_b],
                            wpart=[pb[3 + i][1]])
                    if it == 0:
                        for i in range(3):
                            xg, xg_b = nsx[i]["x"]
                            E.op("act", lambda e, i=i, xg=xg: e.activation(
                                xg[:], pb[3 + i][0][:, 2 * HD:3 * HD],
                                AF.Copy),
                                reads=[pb[3 + i][1]], writes=[xg_b])

            def head_UPD(h):
                hsl = slice(h * HD, (h + 1) * HD)
                for i, (mast, mast_b) in enumerate(((m0, m0_b), (m1, m1_b),
                                                    (m2, m2_b))):
                    E.op("dve", lambda e, i=i, mast=mast, hsl=hsl:
                         e.scalar_tensor_tensor(
                             mast[:, hsl], pb[3 + i][0][:, 2 * HD:3 * HD],
                             colC[:, i:i + 1], mast[:, hsl],
                             ALU.mult, ALU.add),
                         reads=[pb[3 + i][1], mast_b, colC_b],
                         wpart=[mast_b])
                    E.op("act", lambda e, i=i, mast=mast, hsl=hsl:
                         e.activation(
                             sqs[:, 0:HD], mast[:, hsl], AF.Square,
                             accum_out=colA[:, i:i + 1]),
                         reads=[mast_b], writes=[sqs_b], wpart=[colA_b])
                E.op("dve", lambda e: e.tensor_scalar(
                    colB[:, 0:3], colA[:, 0:3], TINY, None, ALU.add),
                    reads=[colA_b], writes=[colB_b])
                _inv_sqrt_eps(E, "dve", colC[:, 0:3], colC_b, colB[:, 0:3],
                              colB_b, colD[:, 0:3], colD_b, magic[:, 0:3],
                              magic_b, 1e-5)
                wncols = wn[:].rearrange("p (w h) -> p w h", h=NH)[:, :, h]
                E.op("dve", lambda e, wncols=wncols: e.tensor_tensor(
                    colD[:, 0:3], wncols, colC[:, 0:3], ALU.mult),
                    reads=[wn_b, colC_b], writes=[colD_b])
                for i, (mast, mast_b) in enumerate(((m0, m0_b), (m1, m1_b),
                                                    (m2, m2_b))):
                    E.op("dve", lambda e, i=i, mast=mast, hsl=hsl:
                         e.tensor_scalar(
                             mast[:, hsl], mast[:, hsl], colD[:, i:i + 1],
                             None, ALU.mult),
                         reads=[colD_b, mast_b], wpart=[mast_b])
                for (mast, mast_b, dst, dst_b, dsl) in (
                        (m0, m0_b, w0n, w0n_b, hsl),
                        (m2, m2_b, w2n, w2n_b, hsl),
                        (m1, m1_b, w1n2, w1n2_b, hsl)):
                    E.mm([lambda t, mast=mast, hsl=hsl: t.transpose(
                        pb[7][0][:, 0:HD], mast[:, hsl], ident_f[:])],
                        reads=[mast_b, ident_f_b], writes=[pb[7][1]])
                    E.op("act", lambda e, dst=dst, dsl=dsl: e.activation(
                        dst[:, dsl], pb[7][0][:, 0:HD], AF.Copy),
                        reads=[pb[7][1]], wpart=[dst_b])
                E.op("dve", lambda e, hsl=hsl: e.tensor_copy(
                    w1t[:, hsl], m1[:, hsl]),
                    reads=[m1_b], wpart=[w1t_b])

            head_TF(0)
            for h in range(NH):
                head_E_early(h)
                if h > 0:
                    head_O(h - 1)
                if h + 1 < NH:
                    head_TF(h + 1)
                head_E_late(h)
                head_G(h)
                head_NS(h)
                head_UPD(h)
            head_O(NH - 1)

            for h0 in range(0, 4 * NH, 64):
                h1 = min(h0 + 64, 4 * NH)
                _rsqrt(E, "dve", colsR[:, h0:h1], colsR_b, rmsS[:, h0:h1],
                       rmsS_b, rmsT[:, h0:h1], rmsT_b, magic64[:, 0:h1 - h0],
                       magic64_b)
            for h in range(NH):
                fns = [lambda t, tc=tc, h=h: t.transpose(
                    pb[6][0][0:1, tc * P:(tc + 1) * P],
                    colsR[:, h * 4 + tc:h * 4 + tc + 1], ident_f[:])
                    for tc in range(4)]
                E.mm(fns, reads=[colsR_b, ident_f_b], writes=[pb[6][1]])
                E.op("dve", lambda e: e.tensor_copy(
                    rowA[:], pb[6][0][0:1, 0:CH]),
                    reads=[pb[6][1]], writes=[rowA_b])
                E.mm([lambda t: t.matmul(pb[6][0][:], ones_f[0:1, :],
                                         rowA[:], start=True, stop=True)],
                     reads=[ones_f_b, rowA_b], writes=[pb[6][1]])
                E.op("dve", lambda e: e.tensor_copy(bcs[:], pb[6][0][:]),
                     reads=[pb[6][1]], writes=[bcs_b])
                E.op("dve", lambda e, h=h: e.tensor_tensor(
                    onc[:, h * CH:(h + 1) * CH], onc[:, h * CH:(h + 1) * CH],
                    bcs[:], ALU.mult),
                    reads=[bcs_b, onc_b], wpart=[onc_b])

            att_B = Buf(f"att_c{c}")
            for dc in range(n_dc):
                wp, wp_b = wst[dc % 2]
                wpv = wp[:, 0:n_dc * P]
                src = wproj_bf.rearrange("(hk p) j -> p hk j", p=P)[
                    :, :, dc * P:(dc + 1) * P]
                E.dma("qs", lambda e, wpv=wpv, src=src: e.dma_start(
                    out=wpv.rearrange("p (hk j) -> p hk j", j=P), in_=src),
                    reads=[wproj_bf_B], writes=[wp_b])
                bank = pb[dc % 2]
                fns = []
                for hk in range(NH):
                    fns.append(lambda t, hk=hk, wpv=wpv, bank=bank: t.matmul(
                        bank[0][:], wpv[:, hk * P:(hk + 1) * P],
                        onc[:, hk * CH:(hk + 1) * CH],
                        start=(hk == 0), stop=(hk == NH - 1)))
                E.mm(fns, reads=[wp_b, onc_b], writes=[bank[1]])
                ssl = slice((dc % 2) * CH, (dc % 2) * CH + CH)
                E.op("act", lambda e, bank=bank, ssl=ssl: e.activation(
                    stg[:, ssl], bank[0][:], AF.Copy),
                    reads=[bank[1]], writes=[stg_b])
                E.dma("qp", lambda e, dc=dc, ssl=ssl: e.dma_start(
                    out=att_scr[dc * P:(dc + 1) * P, :], in_=stg[:, ssl]),
                    reads=[stg_b], wpart=[att_B])

            if c == 0:
                emit_hop2("d")

            cs.close()
            E.fence()

            fs = ExitStack()

            def fsb(name, shape, dtype):
                return sb(f"{name}_c{c}", shape, dtype, fs)

            gated, gated_b = fsb("gated", [P, n_hc * CH], BF16)
            wgp = [fsb(f"wgp{i}", [P, n_dc * P], BF16) for i in range(2)]
            wup = [fsb(f"wup{i}", [P, n_dc * P], BF16) for i in range(2)]
            nhh = n_hc // 2
            wdp = [fsb(f"wdp{i}", [P, nhh * P], BF16) for i in range(2)]
            zz, zz_b = fsb("zz", [P, CH], BF16)
            att2 = [fsb(f"att2{i}", [P, CH], BF16) for i in range(2)]
            xr2 = [fsb(f"xr2{i}", [P, CH], F32) for i in range(2)]
            ost, ost_b = fsb("ost", [P, CH], F32)
            sfm, sfm_b = fsb("sfm", [P, CH], F32)

            for half in range(2):
                wdx, wd_b2 = wdp[half]
                h0 = half * nhh
                src2 = wdT.rearrange("(hc p) j -> p hc j", p=P)[
                    :, h0:h0 + nhh, 0:P]
                E.dma("qs", lambda e, wdx=wdx, src2=src2: e.dma_start(
                    out=wdx[:].rearrange("p (hc j) -> p hc j", j=P),
                    in_=src2),
                    reads=[wdT_B], writes=[wd_b2])
            for hc in range(n_hc):
                wgx, wg_b2 = wgp[hc % 2]
                wux, wu_b2 = wup[hc % 2]
                for (wx, wxb, srcT, srcB) in ((wgx, wg_b2, wgT, wgT_B),
                                              (wux, wu_b2, wuT, wuT_B)):
                    src = srcT.rearrange("(dk p) j -> p dk j", p=P)[
                        :, :, hc * P:(hc + 1) * P]
                    E.dma("qs", lambda e, wx=wx, src=src: e.dma_start(
                        out=wx[:].rearrange("p (dk j) -> p dk j", j=P),
                        in_=src),
                        reads=[srcB], writes=[wxb])
                bg = pb[(hc % 2) * 2]
                bu = pb[(hc % 2) * 2 + 1]
                for (bank, wx, wxb) in ((bg, wgx, wg_b2), (bu, wux, wu_b2)):
                    fns = []
                    for dk in range(n_dc):
                        fns.append(lambda t, dk=dk, wx=wx, bank=bank:
                                   t.matmul(
                            bank[0][:], wx[:, dk * P:(dk + 1) * P],
                            xnfm[:, dk * CH:(dk + 1) * CH],
                            start=(dk == 0), stop=(dk == n_dc - 1)))
                    E.mm(fns, reads=[wxb, xnfm_b], writes=[bank[1]])
                E.op("act", lambda e, bg=bg: e.activation(
                    zz[:], bg[0][:], AF.Silu),
                    reads=[bg[1]], writes=[zz_b])
                E.op("dve", lambda e, hc=hc, bu=bu: e.tensor_tensor(
                    gated[:, hc * CH:(hc + 1) * CH], zz[:], bu[0][:],
                    ALU.mult),
                    reads=[zz_b, bu[1]], wpart=[gated_b])

            for dc in range(n_dc):
                bank = pb[4 + dc % 2]
                for half in range(2):
                    wdx, wd_b2 = wdp[half]
                    h0 = half * nhh
                    if dc > 0:
                        src2 = wdT.rearrange("(hc p) j -> p hc j", p=P)[
                            :, h0:h0 + nhh, dc * P:(dc + 1) * P]
                        E.dma("qs", lambda e, wdx=wdx, src2=src2:
                              e.dma_start(
                                  out=wdx[:].rearrange(
                                      "p (hc j) -> p hc j", j=P),
                                  in_=src2),
                              reads=[wdT_B], writes=[wd_b2])
                    fns = []
                    for hh in range(nhh):
                        hc = h0 + hh
                        fns.append(lambda t, hh=hh, hc=hc, wdx=wdx,
                                   bank=bank: t.matmul(
                            bank[0][:], wdx[:, hh * P:(hh + 1) * P],
                            gated[:, hc * CH:(hc + 1) * CH],
                            start=(hc == 0), stop=(hc == n_hc - 1)))
                    if half == 0:
                        E.mm(fns, reads=[wd_b2, gated_b], writes=[bank[1]])
                    else:
                        E.mm(fns, reads=[wd_b2, gated_b], wpart=[bank[1]])
                at, at_b = att2[dc % 2]
                E.dma("qs", lambda e, at=at, dc=dc: e.dma_start(
                    out=at[:], in_=att_scr[dc * P:(dc + 1) * P, :]),
                    reads=[att_B], writes=[at_b])
                xr, xr_b = xr2[dc % 2]
                xsrc = x_d[c * CH:(c + 1) * CH,
                           dc * P:(dc + 1) * P].rearrange(
                    "(tc p) d -> p tc d", p=P)
                E.dma("qs", lambda e, xr=xr, xsrc=xsrc: e.dma_start(
                    out=xr[:].rearrange("p (tc d) -> p tc d", d=P),
                    in_=xsrc), writes=[xr_b])
                E.op("dve", lambda e, bank=bank, at=at: e.tensor_tensor(
                    sfm[:], bank[0][:], at[:], ALU.add),
                    reads=[bank[1], at_b], writes=[sfm_b])
                tb = pb[6]
                fns = [lambda t, tc=tc, tb=tb: t.transpose(
                    tb[0][:, tc * P:(tc + 1) * P],
                    sfm[:, tc * P:(tc + 1) * P], ident_f[:])
                    for tc in range(4)]
                E.mm(fns, reads=[sfm_b, ident_f_b], writes=[tb[1]])
                E.op("dve", lambda e, tb=tb, xr=xr: e.tensor_tensor(
                    ost[:].rearrange("p (tc d) -> p tc d", d=P),
                    tb[0][:, 0:CH].rearrange("p (tc d) -> p tc d", d=P),
                    xr[:].rearrange("p (tc d) -> p tc d", d=P), ALU.add),
                    reads=[tb[1], xr_b], writes=[ost_b])
                ydst = y_d[c * CH:(c + 1) * CH, dc * P:(dc + 1) * P]
                E.dma("qp", lambda e, ydst=ydst: e.dma_start(
                    out=ydst.rearrange("(tc p) d -> p tc d", p=P),
                    in_=ost[:].rearrange("p (tc d) -> p tc d", d=P)),
                    reads=[ost_b])
                if c + 1 < n_chunks:
                    step = max(1, n_dc // 4)
                    if n_dc >= 8:
                        for tc_ in range(4):
                            if dc == tc_ * step:
                                emit_P1_chain(c + 1, tc_)
                            if dc == tc_ * step + step // 2:
                                emit_P1_tr(c + 1, tc_)
                    else:
                        for tc_ in range(4):
                            if dc == min(tc_ * step, n_dc - 1):
                                emit_P1_tc(c + 1, tc_)
            fs.close()
            E.fence()

        E.finish()

    return nc


_CACHE = {}


def _get_nc(key):
    if key not in _CACHE:
        _CACHE[key] = build_nc(*key)
    return _CACHE[key]


def kernel(L0_attn_x, L0_attn_w0, L0_attn_w1, L0_attn_w2, L0_attn_W_qkv,
           L0_attn_W_lr, L0_attn_b_lr, L0_attn_W_proj, L0_attn_onorm_w,
           L0_ffn_w_gate, L0_ffn_w_up, L0_ffn_w_down):
    x = np.ascontiguousarray(np.asarray(L0_attn_x, dtype=np.float32))
    B = x.shape[0]
    nc = _get_nc((16, 8, 64))
    common = dict(
        w0=np.ascontiguousarray(L0_attn_w0, np.float32),
        w1=np.ascontiguousarray(L0_attn_w1, np.float32),
        w2=np.ascontiguousarray(L0_attn_w2, np.float32),
        wqkv=np.ascontiguousarray(L0_attn_W_qkv, np.float32),
        wlr=np.ascontiguousarray(L0_attn_W_lr, np.float32),
        blr=np.ascontiguousarray(L0_attn_b_lr, np.float32),
        wproj=np.ascontiguousarray(L0_attn_W_proj, np.float32),
        onw=np.ascontiguousarray(L0_attn_onorm_w, np.float32),
        wg=np.ascontiguousarray(L0_ffn_w_gate, np.float32),
        wu=np.ascontiguousarray(L0_ffn_w_up, np.float32),
        wd=np.ascontiguousarray(L0_ffn_w_down, np.float32),
    )
    in_maps = [dict(common, x=np.ascontiguousarray(x[i])) for i in range(B)]
    res = run_bass_kernel_spmd(nc, in_maps, core_ids=list(range(B)))
    return np.stack([np.asarray(r["y"], dtype=np.float32)
                     for r in res.results], axis=0)


# version 16
# speedup vs baseline: 1.2131x; 1.0173x over previous
import math
from contextlib import ExitStack

import numpy as np

import concourse.bass as bass
import concourse.mybir as mybir
from concourse.bass_utils import run_bass_kernel_spmd

F32 = mybir.dt.float32
BF16 = mybir.dt.bfloat16
I32 = mybir.dt.int32
AF = mybir.ActivationFunctionType
ALU = mybir.AluOpType
X = mybir.AxisListType.X

NS_A, NS_B, NS_C = 3.4445, -4.775, 2.0315
BASE_LR_INV = float(np.log(np.expm1(0.01)))
MAGIC = 0x5F3759DF
TINY = 1e-37

CH = 512
HD = 128
P = 128


class Buf:
    __slots__ = ("writers", "readers", "name")

    def __init__(self, name=""):
        self.writers = []
        self.readers = []
        self.name = name


class Emit:

    N_QS = 16
    N_QP = 8

    def __init__(self, nc, stack, max_ops=None):
        self.max_ops = max_ops
        self.ncalls = 0
        self.nc = nc
        self.sems = {}
        self.cnt = {}
        for name in ("pe", "act", "dve", "pool"):
            self.sems[name] = stack.enter_context(nc.semaphore("sem_" + name))
            self.cnt[name] = 0
        for i in range(self.N_QS):
            self.sems[f"qs{i}"] = stack.enter_context(
                nc.semaphore(f"sem_qs{i}"))
            self.cnt[f"qs{i}"] = 0
        for i in range(self.N_QP):
            self.sems[f"qp{i}"] = stack.enter_context(
                nc.semaphore(f"sem_qp{i}"))
            self.cnt[f"qp{i}"] = 0
        self.rr = {"qs": 0, "qp": 0}
        self.eng = {"pe": nc.tensor, "act": nc.scalar, "dve": nc.vector,
                    "pool": nc.gpsimd, "sp": nc.sync}
        self.own = {"pe": "pe", "act": "act", "dve": "dve", "pool": "pool"}
        self.floors = {e: {} for e in self.eng}
        self.nwaits = 0

    def _wait(self, ename, sem_name, val):
        if val <= 0:
            return
        fl = self.floors[ename]
        if fl.get(sem_name, 0) >= val:
            return
        self.eng[ename].wait_ge(self.sems[sem_name], val)
        fl[sem_name] = val
        self.nwaits += 1

    def _need(self, ename, toks):
        best = {}
        for (s, v) in toks:
            if ename == "pe" and s == "pe":
                continue
            if best.get(s, 0) < v:
                best[s] = v
        for s in sorted(best):
            self._wait(ename, s, best[s])

    def _collect(self, reads, writes, wpart):
        toks = []
        for b in reads:
            toks += b.writers
        for b in writes:
            toks += b.readers + b.writers
        for b in wpart:
            toks += b.readers
        return toks

    def _apply(self, tok, reads, writes, wpart):
        for b in reads:
            b.readers.append(tok)
        for b in writes:
            b.writers = [tok]
            b.readers = []
        for b in wpart:
            b.writers.append(tok)

    def op(self, ename, fn, reads=(), writes=(), wpart=()):
        self.ncalls += 1
        if self.max_ops is not None and self.ncalls > self.max_ops:
            return ("pe", 0)
        self._need(ename, self._collect(reads, writes, wpart))
        instr = fn(self.eng[ename])
        sem = self.own[ename]
        self.cnt[sem] += 1
        instr.then_inc(self.sems[sem], 1)
        tok = (sem, self.cnt[sem])
        self._apply(tok, reads, writes, wpart)
        return tok

    def mm(self, fns, reads=(), writes=(), wpart=()):
        self.ncalls += 1
        if self.max_ops is not None and self.ncalls > self.max_ops:
            return ("pe", 0)
        self._need("pe", self._collect(reads, writes, wpart))
        instrs = [fn(self.nc.tensor) for fn in fns]
        self.cnt["pe"] += 1
        instrs[-1].then_inc(self.sems["pe"], 1)
        tok = ("pe", self.cnt["pe"])
        self._apply(tok, reads, writes, wpart)
        return tok

    def dma(self, ring, fn, reads=(), writes=(), wpart=()):
        self.ncalls += 1
        if self.max_ops is not None and self.ncalls > self.max_ops:
            return ("pe", 0)
        ename = "sp" if ring == "qs" else "pool"
        n = self.N_QS if ring == "qs" else self.N_QP
        sem = f"{ring}{self.rr[ring] % n}"
        self.rr[ring] += 1
        self._wait(ename, sem, self.cnt[sem])
        self._need(ename, self._collect(reads, writes, wpart))
        instr = fn(self.eng[ename])
        self.cnt[sem] += 16
        instr.then_inc(self.sems[sem], 16)
        tok = (sem, self.cnt[sem])
        self._apply(tok, reads, writes, wpart)
        return tok

    def fence(self):
        vals = dict(self.cnt)
        for e in self.eng:
            for s, v in vals.items():
                if self.own.get(e) == s:
                    continue
                self._wait(e, s, v)

    def finish(self):
        for s, v in self.cnt.items():
            self._wait("sp", s, v)


def _rsqrt(E, ename, out, out_b, s_ap, s_b, t1, t1_b, magic_ap, magic_b,
           iters=2):
    ov = out.bitcast(I32)
    sv = s_ap.bitcast(I32)
    E.op(ename, lambda e: e.tensor_single_scalar(ov, sv, 1,
                                                 ALU.arith_shift_right),
         reads=[s_b], writes=[out_b])
    E.op(ename, lambda e: e.tensor_tensor(ov, magic_ap, ov, ALU.subtract),
         reads=[magic_b, out_b], writes=[out_b])
    for _ in range(iters):
        E.op(ename, lambda e: e.tensor_tensor(t1, out, out, ALU.mult),
             reads=[out_b], writes=[t1_b])
        E.op(ename, lambda e: e.scalar_tensor_tensor(
            t1, t1, -0.5, s_ap, ALU.mult, ALU.mult),
            reads=[t1_b, s_b], writes=[t1_b])
        E.op(ename, lambda e: e.scalar_tensor_tensor(
            out, t1, 1.5, out, ALU.add, ALU.mult),
            reads=[out_b, t1_b], writes=[out_b])



def _inv_sqrt_eps(E, ename, out, out_b, s_ap, s_b, t1, t1_b, magic_ap,
                  magic_b, eps):
    _rsqrt(E, ename, out, out_b, s_ap, s_b, t1, t1_b, magic_ap, magic_b)
    E.op(ename, lambda e: e.tensor_scalar(t1, out, -eps, 1.0, ALU.mult,
                                          ALU.add),
         reads=[out_b], writes=[t1_b])
    E.op(ename, lambda e: e.tensor_tensor(out, out, t1, ALU.mult),
         reads=[out_b, t1_b], writes=[out_b])


def build_nc(n_dc=16, n_chunks=8, n_hc=64, ts_hop=4096,
             max_ops=None):
    D = n_dc * P
    HID = n_hc * P
    L = n_chunks * CH
    NH = n_dc
    D3 = 3 * D
    JW = 256
    n_jg = D3 // JW
    invD = 1.0 / D

    nc = bass.Bass("TRN2", target_bir_lowering=False)
    g = ExitStack()

    x_d = nc.dram_tensor("x", [L, D], F32, kind="ExternalInput").ap()
    w0_d = nc.dram_tensor("w0", [NH, HD, HD], F32, kind="ExternalInput").ap()
    w1_d = nc.dram_tensor("w1", [NH, HD, HD], F32, kind="ExternalInput").ap()
    w2_d = nc.dram_tensor("w2", [NH, HD, HD], F32, kind="ExternalInput").ap()
    wqkv_d = nc.dram_tensor("wqkv", [D, D3], F32, kind="ExternalInput").ap()
    wlr_d = nc.dram_tensor("wlr", [D, 3 * NH], F32, kind="ExternalInput").ap()
    blr_d = nc.dram_tensor("blr", [3 * NH], F32, kind="ExternalInput").ap()
    wproj_d = nc.dram_tensor("wproj", [D, D], F32, kind="ExternalInput").ap()
    onw_d = nc.dram_tensor("onw", [HD], F32, kind="ExternalInput").ap()
    wg_d = nc.dram_tensor("wg", [HID, D], F32, kind="ExternalInput").ap()
    wu_d = nc.dram_tensor("wu", [HID, D], F32, kind="ExternalInput").ap()
    wd_d = nc.dram_tensor("wd", [D, HID], F32, kind="ExternalInput").ap()
    y_d = nc.dram_tensor("y", [L, D], F32, kind="ExternalOutput").ap()

    wqkv_bf = nc.dram_tensor("wqkv_bf", [D, D3], BF16, kind="Internal").ap()
    wproj_bf = nc.dram_tensor("wproj_bf", [D, D], BF16, kind="Internal").ap()
    wg_blk = nc.dram_tensor("wg_blk", [n_dc, HID, P], BF16,
                            kind="Internal").ap()
    wu_blk = nc.dram_tensor("wu_blk", [n_dc, HID, P], BF16,
                            kind="Internal").ap()
    wd_blk = nc.dram_tensor("wd_blk", [n_hc, D, P], BF16,
                            kind="Internal").ap()
    wgT = nc.dram_tensor("wgT", [D, HID], BF16, kind="Internal").ap()
    wuT = nc.dram_tensor("wuT", [D, HID], BF16, kind="Internal").ap()
    wdT = nc.dram_tensor("wdT", [HID, D], BF16, kind="Internal").ap()
    att_scr = nc.dram_tensor("att_scr", [D, CH], BF16, kind="Internal").ap()

    import ml_dtypes
    ident_f_d = nc.inline_tensor(np.eye(P, dtype=np.float32), "c_ident_f").ap()
    ident_b_d = nc.inline_tensor(np.eye(P, dtype=ml_dtypes.bfloat16),
                                 "c_ident_b").ap()
    ones_f_d = nc.inline_tensor(np.ones((P, P), np.float32), "c_ones_f").ap()
    ones_b_d = nc.inline_tensor(np.ones((P, P), ml_dtypes.bfloat16),
                                "c_ones_b").ap()
    aI_d = nc.inline_tensor((NS_A * np.eye(P)).astype(ml_dtypes.bfloat16),
                            "c_aI_b").ap()
    magic_d = nc.inline_tensor(np.full((P, 16), MAGIC, np.int32),
                               "c_magic").ap()
    magic64_d = nc.inline_tensor(np.full((P, 4 * NH), MAGIC, np.int32),
                                 "c_magic64").ap()

    def sb(name, shape, dtype, st=g):
        t = st.enter_context(nc.sbuf_tensor(name, shape, dtype))
        return t, Buf(name)

    ident_f, ident_f_b = sb("ident_f", [P, P], F32)
    ident_b, ident_b_b = sb("ident_b", [P, P], BF16)
    ones_f, ones_f_b = sb("ones_f", [P, P], F32)
    ones_b, ones_b_b = sb("ones_b", [P, P], BF16)
    aI, aI_b = sb("aI", [P, P], BF16)
    magic, magic_b = sb("magic", [P, 16], I32)
    magic64, magic64_b = sb("magic64", [P, 4 * NH], I32)
    blr_row, blr_row_b = sb("blr_row", [1, 3 * NH], F32)
    gam, gam_b = sb("gam", [P, 1], F32)
    wlr_bf, wlr_bf_b = sb("wlr_bf", [P, n_dc * 3 * NH], BF16)

    m0, m0_b = sb("m0", [P, NH * HD], F32)
    m1, m1_b = sb("m1", [P, NH * HD], F32)
    m2, m2_b = sb("m2", [P, NH * HD], F32)
    wn, wn_b = sb("wn", [P, 3 * NH], F32)
    w0n, w0n_b = sb("w0n", [P, NH * HD], BF16)
    w2n, w2n_b = sb("w2n", [P, NH * HD], BF16)
    w1t, w1t_b = sb("w1t", [P, NH * HD], BF16)
    w1n2, w1n2_b = sb("w1n2", [P, NH * HD], BF16)

    xnfm, xnfm_b = sb("xnfm", [P, n_dc * CH], BF16)
    lrf, lrf_b = sb("lrf", [P, 4 * 3 * NH], F32)
    rmsS, rmsS_b = sb("rmsS", [P, 4 * NH], F32)
    colsR, colsR_b = sb("colsR", [P, 4 * NH], F32)
    rmsT, rmsT_b = sb("rmsT", [P, 4 * NH], F32)

    xf = [sb(f"xf{i}", [P, D], F32) for i in range(2)]
    stg, stg_b = sb("stg", [P, max(D, 2 * CH)], BF16)
    colA, colA_b = sb("colA", [P, 16], F32)
    colB, colB_b = sb("colB", [P, 16], F32)
    colC, colC_b = sb("colC", [P, 16], F32)
    colD, colD_b = sb("colD", [P, 16], F32)
    rowA, rowA_b = sb("rowA", [1, CH], F32)
    TS_HOP = min(ts_hop, HID, D)
    tT, tT_b = sb("hop2stg", [P, TS_HOP], BF16)

    pb = []
    for i in range(8):
        t = g.enter_context(nc.psum_tensor(f"pb{i}", [P, 512], F32))
        pb.append((t, Buf(f"pb{i}")))

    def pbf(i):
        return pb[i][0].ap().bitcast(BF16)

    E = Emit(nc, g, max_ops=max_ops)

    with nc.Block():
        for (dst, dst_b, src) in ((ident_f, ident_f_b, ident_f_d),
                                  (ident_b, ident_b_b, ident_b_d),
                                  (ones_f, ones_f_b, ones_f_d),
                                  (ones_b, ones_b_b, ones_b_d),
                                  (aI, aI_b, aI_d),
                                  (magic, magic_b, magic_d),
                                  (magic64, magic64_b, magic64_d)):
            E.dma("qs", lambda e, d=dst, s=src: e.dma_start(out=d[:], in_=s),
                  writes=[dst_b])
        E.dma("qs", lambda e: e.dma_start(out=blr_row[:], in_=blr_d[None, :]),
              writes=[blr_row_b])
        E.dma("qs", lambda e: e.dma_start(out=gam[:], in_=onw_d[:, None]),
              writes=[gam_b])
        E.op("dve", lambda e: e.tensor_scalar(
            blr_row[:], blr_row[:], BASE_LR_INV, None, ALU.add),
            reads=[blr_row_b], writes=[blr_row_b])
        for dk in range(n_dc):
            E.dma("qp", lambda e, dk=dk: e.dma_start(
                out=wlr_bf[:, dk * 3 * NH:(dk + 1) * 3 * NH],
                in_=wlr_d[dk * P:(dk + 1) * P, :]),
                wpart=[wlr_bf_b])

        wqkv_bf_B = Buf("wqkv_bf")
        wproj_bf_B = Buf("wproj_bf")
        E.dma("qp", lambda e: e.dma_start(out=wqkv_bf[:], in_=wqkv_d[:]),
              writes=[wqkv_bf_B])
        wg_blk_B = [Buf(f"wg_blk{i}") for i in range(n_dc)]
        wu_blk_B = [Buf(f"wu_blk{i}") for i in range(n_dc)]
        wd_blk_B = [Buf(f"wd_blk{i}") for i in range(n_hc)]
        for dk in range(n_dc):
            E.dma("qp", lambda e, dk=dk: e.dma_start(
                out=wg_blk[dk], in_=wg_d[:, dk * P:(dk + 1) * P]),
                writes=[wg_blk_B[dk]])
            E.dma("qp", lambda e, dk=dk: e.dma_start(
                out=wu_blk[dk], in_=wu_d[:, dk * P:(dk + 1) * P]),
                writes=[wu_blk_B[dk]])
        E.dma("qp", lambda e: e.dma_start(out=wproj_bf[:], in_=wproj_d[:]),
              writes=[wproj_bf_B])
        for hc in range(n_hc):
            E.dma("qp", lambda e, hc=hc: e.dma_start(
                out=wd_blk[hc], in_=wd_d[:, hc * P:(hc + 1) * P]),
                writes=[wd_blk_B[hc]])

        def emit_P1_chain(cc, tc):
            if True:
                xt, xt_b = xf[tc % 2]
                r0 = cc * CH + tc * P
                E.dma("qs", lambda e, xt=xt, r0=r0: e.dma_start(
                    out=xt[:], in_=x_d[r0:r0 + P, :]), writes=[xt_b])
                E.op("dve", lambda e, xt=xt, tc=tc: e.tensor_reduce(
                    colA[:, tc:tc + 1], xt[:], X, ALU.add),
                    reads=[xt_b], wpart=[colA_b])
                E.op("act", lambda e, xt=xt, tc=tc: e.activation(
                    stg[:, 0:D], xt[:], AF.Square,
                    accum_out=colB[:, tc:tc + 1]),
                    reads=[xt_b], writes=[stg_b], wpart=[colB_b])
                E.op("dve", lambda e, tc=tc: e.tensor_scalar(
                    colC[:, 0:1], colA[:, tc:tc + 1], invD, None, ALU.mult),
                    reads=[colA_b], writes=[colC_b])
                E.op("dve", lambda e: e.tensor_tensor(
                    colD[:, 0:1], colC[:, 0:1], colC[:, 0:1], ALU.mult),
                    reads=[colC_b], writes=[colD_b])
                E.op("dve", lambda e, tc=tc: e.scalar_tensor_tensor(
                    colD[:, 1:2], colB[:, tc:tc + 1], invD, colD[:, 0:1],
                    ALU.mult, ALU.subtract),
                    reads=[colB_b, colD_b], wpart=[colD_b])
                E.op("dve", lambda e: e.tensor_scalar(
                    colD[:, 1:2], colD[:, 1:2], 1e-5, None, ALU.add),
                    reads=[colD_b], wpart=[colD_b])
                _rsqrt(E, "dve", colC[:, 1:2], colC_b, colD[:, 1:2], colD_b,
                       colD[:, 2:3], colD_b, magic[:, 0:1], magic_b)
                E.op("dve", lambda e: e.scalar_tensor_tensor(
                    colD[:, 3:4], colC[:, 0:1], -1.0, colC[:, 1:2],
                    ALU.mult, ALU.mult),
                    reads=[colC_b], wpart=[colD_b])
                E.op("act", lambda e, xt=xt: e.activation(
                    stg[:, 0:D], xt[:], AF.Identity,
                    bias=colD[:, 3:4], scale=colC[:, 1:2]),
                    reads=[xt_b, colC_b, colD_b], writes=[stg_b])

        def emit_P1_tr(cc, tc):
            if True:
                for dk0 in range(0, n_dc, 4):
                    ng = min(4, n_dc - dk0)
                    fns = []
                    for q in range(ng):
                        dk = dk0 + q
                        fns.append(lambda t, dk=dk, q=q: t.transpose(
                            pbf(7)[:, q * P:(q + 1) * P],
                            stg[:, dk * P:(dk + 1) * P], ident_b[:]))
                    E.mm(fns, reads=[stg_b, ident_b_b], writes=[pb[7][1]])
                    dst = xnfm[:].rearrange("p (dk t) -> p dk t", t=CH)[
                        :, dk0:dk0 + ng, tc * P:(tc + 1) * P]
                    E.op("dve", lambda e, dst=dst, ng=ng: e.tensor_copy(
                        dst, pbf(7)[:, 0:ng * P].rearrange(
                            "p (q t) -> p q t", t=P)),
                        reads=[pb[7][1]], wpart=[xnfm_b])



        def emit_P1_tc(cc, tc):
            emit_P1_chain(cc, tc)
            emit_P1_tr(cc, tc)

        def emit_P1(cc):
            for tc in range(4):
                emit_P1_tc(cc, tc)

        emit_P1(0)

        wtmp = [sb(f"wtmp{i}", [P, HD], F32) for i in range(2)]
        wns, wns_b = sb("wns", [P, 3 * NH], F32)
        for h in range(NH):
            for wi, (wdin, mast, mast_b, nat, nat_b) in enumerate(
                    ((w0_d, m0, m0_b, w0n, w0n_b),
                     (w1_d, m1, m1_b, w1n2, w1n2_b),
                     (w2_d, m2, m2_b, w2n, w2n_b))):
                wt, wt_b = wtmp[(h * 3 + wi) % 2]
                sl = slice(h * HD, (h + 1) * HD)
                E.dma("qs", lambda e, s=wdin, w=wt, h=h: e.dma_start(
                    out=w[:], in_=s[h]), writes=[wt_b])
                if nat is not None:
                    E.op("dve", lambda e, w=wt, n=nat, sl=sl:
                         e.tensor_copy(n[:, sl], w[:]),
                         reads=[wt_b], wpart=[nat_b])
                E.mm([lambda t, w=wt: t.transpose(pb[7][0][:, 0:HD], w[:],
                                                  ident_f[:])],
                     reads=[wt_b, ident_f_b], writes=[pb[7][1]])
                E.op("dve", lambda e, m=mast, sl=sl:
                     e.tensor_copy(m[:, sl], pb[7][0][:, 0:HD]),
                     reads=[pb[7][1]], wpart=[mast_b])
                if wi == 1:
                    E.op("dve", lambda e, sl=sl:
                         e.tensor_copy(w1t[:, sl], m1[:, sl]),
                         reads=[m1_b], wpart=[w1t_b])
                E.op("act", lambda e, m=mast, sl=sl, wi=wi:
                     e.activation(pb[7][0][:, HD:2 * HD], m[:, sl], AF.Square,
                                  accum_out=colA[:, wi:wi + 1]),
                     reads=[mast_b], writes=[pb[7][1], colA_b])
                E.op("dve", lambda e, wi=wi, h=h: e.tensor_scalar(
                    wn[:, wi * NH + h:wi * NH + h + 1], colA[:, wi:wi + 1],
                    TINY, None, ALU.add),
                    reads=[colA_b], wpart=[wn_b])
        ncol = 3 * NH
        for c0 in range(0, ncol, 16):
            c1 = min(c0 + 16, ncol)
            _rsqrt(E, "dve", wns[:, c0:c1], wns_b, wn[:, c0:c1], wn_b,
                   colB[:, 0:c1 - c0], colB_b, magic[:, 0:c1 - c0], magic_b)
        E.op("dve", lambda e: e.tensor_tensor(wn[:, 0:ncol], wn[:, 0:ncol],
                                              wns[:, 0:ncol], ALU.mult),
             reads=[wn_b, wns_b], writes=[wn_b])

        wgT_B, wuT_B, wdT_B = Buf("wgT"), Buf("wuT"), Buf("wdT")
        _hop2 = {"g": (wg_blk, wg_blk_B, wgT, wgT_B, n_dc, HID),
                 "u": (wu_blk, wu_blk_B, wuT, wuT_B, n_dc, HID),
                 "d": (wd_blk, wd_blk_B, wdT, wdT_B, n_hc, D)}

        def emit_hop2(which):
            blk, blkB, dstT, dstB, nblk, rows = _hop2[which]
            for k in range(nblk):
                for r0 in range(0, rows, TS_HOP):
                    r1 = min(r0 + TS_HOP, rows)
                    E.dma("qs", lambda e, k=k, blk=blk, r0=r0, r1=r1:
                          e.dma_start_transpose(tT[:, 0:r1 - r0],
                                                blk[k][r0:r1, :]),
                          reads=[blkB[k]], writes=[tT_b])
                    E.dma("qp", lambda e, k=k, d=dstT, r0=r0, r1=r1:
                          e.dma_start(out=d[k * P:(k + 1) * P, r0:r1],
                                      in_=tT[:, 0:r1 - r0]),
                          reads=[tT_b], wpart=[dstB])


        for c in range(n_chunks):
            cs = ExitStack()

            def csb(name, shape, dtype):
                return sb(f"{name}_c{c}", shape, dtype, cs)

            qkv, qkv_b = csb("qkv", [P, 4 * D3], BF16)
            qkv_bJ = [Buf(f"qkv_c{c}_j{j}") for j in range(n_jg)]
            jg_of = lambda si, h: (si * D + h * HD) // JW
            wst = [csb(f"wst{i}", [P, n_dc * JW], BF16) for i in range(2)]
            onc, onc_b = csb("onc", [P, NH * CH], BF16)
            gaten, gaten_b = csb("gaten", [P, CH], F32)
            hbn, hbn_b = csb("hbn", [P, CH], F32)
            sig, sig_b = csb("sig", [P, CH], F32)
            f1s, f1s_b = csb("f1s", [P, CH], F32)
            sg, sg_b = csb("sg", [P, CH], BF16)
            hl1, hl1_b = csb("hl1", [P, CH], BF16)
            dh2, dh2_b = csb("dh2", [P, CH], BF16)
            dg0, dg0_b = csb("dg0", [P, CH], BF16)
            zss, zss_b = csb("zss", [P, CH], BF16)
            zfm, zfm_b = csb("zfm", [P, CH], BF16)
            sqs, sqs_b = csb("sqs", [P, CH], BF16)
            bcs, bcs_b = csb("bcs", [P, CH], F32)
            sqb, sqb_b = csb("sqb", [P, D], BF16)
            ew2, ew2_b = csb("ew2", [P, CH], F32)
            nsx = []
            for i in range(3):
                nsx.append({k: csb(f"ns_{k}{i}", [P, HD],
                                   BF16 if k in ("x", "xt", "as", "bm")
                                   else F32)
                            for k in ("x", "xt", "as", "bas", "bm")})
            kfm = [csb(f"kfm{i}", [P, CH], BF16) for i in range(2)]
            vfm = [csb(f"vfm{i}", [P, CH], BF16) for i in range(2)]
            qfm = [csb(f"qfm{i}", [P, CH], BF16) for i in range(2)]

            za = 3 * NH
            for tc in range(4):
                fns = []
                for dk in range(n_dc):
                    fns.append(lambda t, dk=dk, tc=tc: t.matmul(
                        pb[6][0][:, 0:za],
                        xnfm[:, dk * CH + tc * P:dk * CH + (tc + 1) * P],
                        wlr_bf[:, dk * za:(dk + 1) * za],
                        start=(dk == 0), stop=False))
                fns.append(lambda t: t.matmul(
                    pb[6][0][:, 0:za], ones_f[0:1, :],
                    blr_row[:], start=False, stop=True))
                E.mm(fns, reads=[xnfm_b, wlr_bf_b, ones_f_b, blr_row_b],
                     writes=[pb[6][1]])
                E.op("act", lambda e: e.activation(
                    gaten[:, 0:za], pb[6][0][:, 0:za], AF.Abs),
                    reads=[pb[6][1]], writes=[gaten_b])
                E.op("act", lambda e: e.activation(
                    gaten[:, 0:za], gaten[:, 0:za], AF.Exp, scale=-1.0),
                    reads=[gaten_b], writes=[gaten_b])
                E.op("act", lambda e: e.activation(
                    gaten[:, 0:za], gaten[:, 0:za], AF.Ln, bias=1.0),
                    reads=[gaten_b], writes=[gaten_b])
                E.op("act", lambda e: e.activation(
                    bcs[:, 0:za], pb[6][0][:, 0:za], AF.Relu),
                    reads=[pb[6][1]], writes=[bcs_b])
                E.op("dve", lambda e, tc=tc: e.tensor_tensor(
                    lrf[:, tc * za:(tc + 1) * za], bcs[:, 0:za],
                    gaten[:, 0:za], ALU.add),
                    reads=[bcs_b, gaten_b], wpart=[lrf_b])

            def emit_P2b():
              for jg in range(n_jg):
                wq, wq_b = wst[jg % 2]
                src = wqkv_bf.rearrange("(dk p) j -> p dk j", p=P)[
                    :, :, jg * JW:(jg + 1) * JW]
                E.dma("qs", lambda e, wq=wq, src=src: e.dma_start(
                    out=wq[:].rearrange("p (dk j) -> p dk j", j=JW), in_=src),
                    reads=[wqkv_bf_B], writes=[wq_b])
                for tc in range(4):
                    bank = pb[(jg * 4 + tc) % 2]
                    psl = bank[0][:, 0:JW]
                    fns = []
                    for dk in range(n_dc):
                        fns.append(lambda t, dk=dk, tc=tc, psl=psl, wq=wq:
                                   t.matmul(
                            psl, xnfm[:, dk * CH + tc * P:
                                      dk * CH + (tc + 1) * P],
                            wq[:, dk * JW:(dk + 1) * JW],
                            start=(dk == 0), stop=(dk == n_dc - 1)))
                    E.mm(fns, reads=[xnfm_b, wq_b], writes=[bank[1]])
                    E.op("act", lambda e, psl=psl, tc=tc, jg=jg: e.activation(
                        qkv[:, tc * D3 + jg * JW:tc * D3 + (jg + 1) * JW],
                        psl, AF.Silu),
                        reads=[bank[1]], wpart=[qkv_bJ[jg]])
                if jg == D // JW - 1:
                    emit_P2c(0)
                if jg == 2 * D // JW - 1:
                    emit_P2c(1)
                if jg >= 2 * D // JW:
                    hready = ((jg + 1) * JW - 2 * D) // HD - 1
                    p3_step_until(min(hready - 1, NH - 2))

            def emit_P2c(si):
                dst, dst_b = ((colA, colA_b), (colB, colB_b))[si]
                for tc in range(4):
                    qs0 = tc * D3 + si * D
                    E.op("dve", lambda e, qs0=qs0: e.tensor_tensor(
                        sqb[:], qkv[:, qs0:qs0 + D], qkv[:, qs0:qs0 + D],
                        ALU.mult),
                        reads=qkv_bJ[si * (D // JW):(si + 1) * (D // JW)],
                        writes=[sqb_b])
                    E.op("dve", lambda e, dst=dst: e.tensor_reduce(
                        dst[:, 0:NH], sqb[:].rearrange(
                            "p (h e) -> p h e", e=HD), X, ALU.add),
                        reads=[sqb_b], writes=[dst_b])
                    E.op("dve", lambda e, dst=dst: e.tensor_scalar(
                        dst[:, 0:NH], dst[:, 0:NH], TINY, None, ALU.add),
                        reads=[dst_b], wpart=[dst_b])
                    _inv_sqrt_eps(E, "dve", colD[:, 0:NH], colD_b,
                                  dst[:, 0:NH], dst_b, colC[:, 0:NH], colC_b,
                                  magic[:, 0:NH], magic_b, 1e-5)
                    for h in range(NH):
                        qs0h = tc * D3 + si * D + h * HD
                        jb = qkv_bJ[jg_of(si, h)]
                        E.op("dve", lambda e, qs0h=qs0h, h=h:
                             e.tensor_scalar(
                                 qkv[:, qs0h:qs0h + HD],
                                 qkv[:, qs0h:qs0h + HD],
                                 colD[:, h:h + 1], None, ALU.mult),
                             reads=[colD_b, jb], wpart=[jb])

            def head_TF(h):
                par = h % 2
                kf, kf_b = kfm[par]
                vf, vf_b = vfm[par]
                qf, qf_b = qfm[par]
                hsl = slice(h * HD, (h + 1) * HD)
                for (si, dst, dst_b) in ((1, kf, kf_b), (2, vf, vf_b),
                                         (0, qf, qf_b)):
                    fns = []
                    for tc in range(4):
                        src = qkv[:, tc * D3 + si * D + h * HD:
                                  tc * D3 + si * D + (h + 1) * HD]
                        fns.append(lambda t, src=src, tc=tc: t.transpose(
                            pbf(7)[:, tc * P:(tc + 1) * P], src, ident_b[:]))
                    E.mm(fns, reads=[qkv_bJ[jg_of(si, h)], ident_b_b],
                         writes=[pb[7][1]])
                    E.op("dve", lambda e, dst=dst: e.tensor_copy(
                        dst[:], pbf(7)[:, 0:512]),
                        reads=[pb[7][1]], writes=[dst_b])
                for (bank, wmat, wmat_b, opfm, opfm_b) in (
                        (pb[0], w0n, w0n_b, kf, kf_b),
                        (pb[1], w2n, w2n_b, kf, kf_b),
                        (pb[2], w1t, w1t_b, vf, vf_b)):
                    fns = []
                    for tc in range(4):
                        fns.append(lambda t, tc=tc, bank=bank, opfm=opfm,
                                   wmat=wmat, hsl=hsl: t.matmul(
                            bank[0][:, tc * P:(tc + 1) * P],
                            opfm[:, tc * P:(tc + 1) * P], wmat[:, hsl],
                            start=True, stop=True))
                    E.mm(fns, reads=[opfm_b, wmat_b], writes=[bank[1]])

            def head_E_early(h):
                E.op("act", lambda e: e.activation(sg[:], pb[0][0][:],
                                                   AF.Silu),
                     reads=[pb[0][1]], writes=[sg_b])
                E.op("act", lambda e: e.activation(sig[:], pb[0][0][:],
                                                   AF.Tanh, scale=0.5),
                     reads=[pb[0][1]], writes=[sig_b])
                E.op("dve", lambda e: e.tensor_scalar(
                    ew2[:], sg[:], -1.0, 1.0, ALU.mult, ALU.add),
                    reads=[sg_b], writes=[ew2_b])
                E.op("dve", lambda e: e.tensor_tensor(
                    ew2[:], pb[0][0][:], ew2[:], ALU.add),
                    reads=[pb[0][1], ew2_b], writes=[ew2_b])
                E.op("dve", lambda e: e.tensor_copy(hbn[:], pb[1][0][:]),
                     reads=[pb[1][1]], writes=[hbn_b])
                E.op("dve", lambda e: e.tensor_tensor(
                    f1s[:], pb[2][0][:], hbn[:], ALU.mult),
                    reads=[pb[2][1], hbn_b], writes=[f1s_b])
                for tc in range(4):
                    tsl = slice(tc * P, (tc + 1) * P)
                    E.op("dve", lambda e, tsl=tsl, tc=tc: e.tensor_scalar(
                        bcs[:, tsl], pb[2][0][:, tsl],
                        lrf[:, tc * za + 2 * NH + h:
                            tc * za + 2 * NH + h + 1], None, ALU.mult),
                        reads=[pb[2][1], lrf_b], wpart=[bcs_b])

            def head_O(h):
                par = h % 2
                qf, qf_b = qfm[par]
                hsl = slice(h * HD, (h + 1) * HD)
                E.mm([lambda t, hsl=hsl: t.matmul(
                    pb[0][0][:], w0n[:, hsl], qf[:], start=True, stop=True)],
                    reads=[w0n_b, qf_b], writes=[pb[0][1]])
                E.mm([lambda t, hsl=hsl: t.matmul(
                    pb[1][0][:], w2n[:, hsl], qf[:], start=True, stop=True)],
                    reads=[w2n_b, qf_b], writes=[pb[1][1]])
                E.op("act", lambda e: e.activation(zss[:], pb[0][0][:],
                                                   AF.Silu),
                     reads=[pb[0][1]], writes=[zss_b])
                E.op("dve", lambda e: e.tensor_tensor(
                    zfm[:], zss[:], pb[1][0][:], ALU.mult),
                    reads=[zss_b, pb[1][1]], writes=[zfm_b])
                E.mm([lambda t: t.matmul(pb[2][0][:], w1n2[:, hsl], zfm[:],
                                         start=True, stop=True)],
                     reads=[w1n2_b, zfm_b], writes=[pb[2][1]])
                E.op("act", lambda e: e.activation(sqs[:], pb[2][0][:],
                                                   AF.Square),
                     reads=[pb[2][1]], writes=[sqs_b])
                fns = [lambda t, tc=tc: t.matmul(
                    pb[6][0][:, 4 + tc:5 + tc],
                    sqs[:, tc * P:(tc + 1) * P], ones_b[:, 0:1],
                    start=True, stop=True) for tc in range(4)]
                E.mm(fns, reads=[sqs_b, ones_b_b], wpart=[pb[6][1]])
                E.op("dve", lambda e, h=h: e.tensor_scalar(
                    rmsS[:, h * 4:(h + 1) * 4], pb[6][0][:, 4:8], 1.0 / HD,
                    1e-5, ALU.mult, ALU.add),
                    reads=[pb[6][1]], wpart=[rmsS_b])
                E.op("act", lambda e, h=h: e.activation(
                    onc[:, h * CH:(h + 1) * CH], pb[2][0][:], AF.Copy,
                    scale=gam[:, 0:1]),
                    reads=[pb[2][1], gam_b], wpart=[onc_b])

            def head_E_late(h):
                E.op("dve", lambda e: e.tensor_scalar(
                    sig[:], sig[:], 0.5, 0.5, ALU.mult, ALU.add),
                    reads=[sig_b], writes=[sig_b])
                E.op("dve", lambda e: e.tensor_tensor(
                    sig[:], sig[:], ew2[:], ALU.mult),
                    reads=[sig_b, ew2_b], writes=[sig_b])
                E.op("dve", lambda e: e.tensor_tensor(
                    dh2[:], bcs[:], sg[:], ALU.mult),
                    reads=[bcs_b, sg_b], writes=[dh2_b])
                for tc in range(4):
                    tsl = slice(tc * P, (tc + 1) * P)
                    E.op("dve", lambda e, tsl=tsl, tc=tc:
                         e.scalar_tensor_tensor(
                             hl1[:, tsl], hbn[:, tsl],
                             lrf[:, tc * za + NH + h:tc * za + NH + h + 1],
                             sg[:, tsl], ALU.mult, ALU.mult),
                         reads=[hbn_b, lrf_b, sg_b], wpart=[hl1_b])
                    E.op("dve", lambda e, tsl=tsl, tc=tc:
                         e.scalar_tensor_tensor(
                             dg0[:, tsl], f1s[:, tsl],
                             lrf[:, tc * za + h:tc * za + h + 1],
                             sig[:, tsl], ALU.mult, ALU.mult),
                         reads=[f1s_b, lrf_b, sig_b], wpart=[dg0_b])

            def head_G(h):
                hsl = slice(h * HD, (h + 1) * HD)
                kts = [qkv[:, tc * D3 + D + hsl.start:
                           tc * D3 + D + hsl.stop] for tc in range(4)]
                vts = [qkv[:, tc * D3 + 2 * D + hsl.start:
                           tc * D3 + 2 * D + hsl.stop] for tc in range(4)]
                kb = qkv_bJ[jg_of(1, h)]
                vb = qkv_bJ[jg_of(2, h)]
                for gi, (bank, lhss, lhs_b, rhss, rhs_b) in enumerate((
                        (pb[3], [dg0[:, tc * P:(tc + 1) * P]
                                 for tc in range(4)], dg0_b, kts, kb),
                        (pb[4], vts, vb,
                         [hl1[:, tc * P:(tc + 1) * P]
                          for tc in range(4)], hl1_b),
                        (pb[5], [dh2[:, tc * P:(tc + 1) * P]
                                 for tc in range(4)], dh2_b, kts, kb))):
                    fns = []
                    for tc in range(4):
                        fns.append(lambda t, lh=lhss[tc], rh=rhss[tc],
                                   bank=bank, tc=tc:
                                   t.matmul(bank[0][:, 0:HD], lh, rh,
                                            start=(tc == 0),
                                            stop=(tc == 3)))
                    E.mm(fns, reads=[lhs_b, rhs_b], writes=[bank[1]])

            def head_NS(h):
                hsl = slice(h * HD, (h + 1) * HD)
                for i in range(3):
                    xg, xg_b = nsx[i]["x"]
                    E.op("act", lambda e, i=i, xg=xg: e.activation(
                        xg[:], pb[3 + i][0][:, 0:HD], AF.Copy),
                        reads=[pb[3 + i][1]], writes=[xg_b])
                    E.op("act", lambda e, i=i, xg=xg: e.activation(
                        sqs[:, 0:HD], xg[:], AF.Square,
                        accum_out=colA[:, i:i + 1]),
                        reads=[xg_b], writes=[sqs_b], wpart=[colA_b])
                fns = [lambda t, i=i: t.matmul(
                    pb[6][0][:, i:i + 1], ones_f[:], colA[:, i:i + 1],
                    start=True, stop=True) for i in range(3)]
                E.mm(fns, reads=[ones_f_b, colA_b], writes=[pb[6][1]])
                E.op("dve", lambda e: e.tensor_scalar(
                    colB[:, 0:3], pb[6][0][:, 0:3], TINY, None, ALU.add),
                    reads=[pb[6][1]], writes=[colB_b])
                _rsqrt(E, "dve", colC[:, 0:3], colC_b, colB[:, 0:3], colB_b,
                       colD[:, 0:3], colD_b, magic[:, 0:3], magic_b, iters=1)
                E.op("dve", lambda e: e.tensor_scalar(
                    colD[:, 0:3], colC[:, 0:3], -1e-7, 1.0, ALU.mult,
                    ALU.add),
                    reads=[colC_b], writes=[colD_b])
                E.op("dve", lambda e: e.tensor_tensor(
                    colC[:, 0:3], colC[:, 0:3], colD[:, 0:3], ALU.mult),
                    reads=[colC_b, colD_b], writes=[colC_b])
                E.op("dve", lambda e: e.tensor_tensor(
                    colB[:, 0:3], colC[:, 0:3], colC[:, 0:3], ALU.mult),
                    reads=[colC_b], writes=[colB_b])
                E.op("dve", lambda e: e.tensor_scalar(
                    colD[:, 0:3], colB[:, 0:3], NS_B, None, ALU.mult),
                    reads=[colB_b], writes=[colD_b])
                for it in range(2):
                    for i in range(3):
                        xg, xg_b = nsx[i]["x"]
                        xt, xt_b = nsx[i]["xt"]
                        E.mm([lambda t, xg=xg: t.transpose(
                            pbf(7)[:, 0:HD], xg[:], ident_b[:])],
                            reads=[xg_b, ident_b_b], writes=[pb[7][1]])
                        E.op("dve", lambda e, xt=xt: e.tensor_copy(
                            xt[:], pbf(7)[:, 0:HD]),
                            reads=[pb[7][1]], writes=[xt_b])
                    for i in range(3):
                        xt, xt_b = nsx[i]["xt"]
                        E.mm([lambda t, i=i, xt=xt: t.matmul(
                            pb[3 + i][0][:, 0:HD], xt[:], xt[:],
                            start=True, stop=True)],
                            reads=[xt_b], writes=[pb[3 + i][1]])
                    for i in range(3):
                        a_s, a_s_b = nsx[i]["as"]
                        E.op("act", lambda e, i=i, a_s=a_s: e.activation(
                            a_s[:], pb[3 + i][0][:, 0:HD], AF.Copy,
                            scale=colB[:, i:i + 1]),
                            reads=[pb[3 + i][1], colB_b], writes=[a_s_b])
                    for i in range(3):
                        a_s, a_s_b = nsx[i]["as"]
                        E.mm([lambda t, i=i, a_s=a_s: t.matmul(
                            pb[3 + i][0][:, HD:2 * HD], a_s[:], a_s[:],
                            start=True, stop=True)],
                            reads=[a_s_b], wpart=[pb[3 + i][1]])
                    for i in range(3):
                        bas, bas_b = nsx[i]["bas"]
                        E.op("dve", lambda e, i=i, bas=bas: e.tensor_scalar(
                            bas[:], pb[3 + i][0][:, HD:2 * HD], NS_C, None,
                            ALU.mult),
                            reads=[pb[3 + i][1]], writes=[bas_b])
                    for i in range(3):
                        bm, bm_b = nsx[i]["bm"]
                        bas, bas_b = nsx[i]["bas"]
                        E.op("dve", lambda e, i=i, bm=bm, bas=bas:
                             e.scalar_tensor_tensor(
                                 bm[:], pb[3 + i][0][:, 0:HD],
                                 colD[:, i:i + 1], bas[:],
                                 ALU.mult, ALU.add),
                             reads=[pb[3 + i][1], colD_b, bas_b],
                             writes=[bm_b])
                    for i in range(3):
                        bm, bm_b = nsx[i]["bm"]
                        xg, xg_b = nsx[i]["x"]
                        E.mm([lambda t, i=i, bm=bm, xg=xg: t.matmul(
                            pb[3 + i][0][:, 2 * HD:3 * HD], bm[:], xg[:],
                            start=True, stop=False),
                            lambda t, i=i, xg=xg: t.matmul(
                            pb[3 + i][0][:, 2 * HD:3 * HD], aI[:], xg[:],
                            start=False, stop=True)],
                            reads=[bm_b, xg_b, aI_b],
                            wpart=[pb[3 + i][1]])
                    if it == 0:
                        for i in range(3):
                            xg, xg_b = nsx[i]["x"]
                            E.op("act", lambda e, i=i, xg=xg: e.activation(
                                xg[:], pb[3 + i][0][:, 2 * HD:3 * HD],
                                AF.Copy),
                                reads=[pb[3 + i][1]], writes=[xg_b])

            def head_UPD(h):
                hsl = slice(h * HD, (h + 1) * HD)
                for i, (mast, mast_b) in enumerate(((m0, m0_b), (m1, m1_b),
                                                    (m2, m2_b))):
                    E.op("dve", lambda e, i=i, mast=mast, hsl=hsl:
                         e.scalar_tensor_tensor(
                             mast[:, hsl], pb[3 + i][0][:, 2 * HD:3 * HD],
                             colC[:, i:i + 1], mast[:, hsl],
                             ALU.mult, ALU.add),
                         reads=[pb[3 + i][1], mast_b, colC_b],
                         wpart=[mast_b])
                    E.op("act", lambda e, i=i, mast=mast, hsl=hsl:
                         e.activation(
                             sqs[:, 0:HD], mast[:, hsl], AF.Square,
                             accum_out=colA[:, i:i + 1]),
                         reads=[mast_b], writes=[sqs_b], wpart=[colA_b])
                E.op("dve", lambda e: e.tensor_scalar(
                    colB[:, 0:3], colA[:, 0:3], TINY, None, ALU.add),
                    reads=[colA_b], writes=[colB_b])
                _inv_sqrt_eps(E, "dve", colC[:, 0:3], colC_b, colB[:, 0:3],
                              colB_b, colD[:, 0:3], colD_b, magic[:, 0:3],
                              magic_b, 1e-5)
                wncols = wn[:].rearrange("p (w h) -> p w h", h=NH)[:, :, h]
                E.op("dve", lambda e, wncols=wncols: e.tensor_tensor(
                    colD[:, 0:3], wncols, colC[:, 0:3], ALU.mult),
                    reads=[wn_b, colC_b], writes=[colD_b])
                for i, (mast, mast_b) in enumerate(((m0, m0_b), (m1, m1_b),
                                                    (m2, m2_b))):
                    E.op("dve", lambda e, i=i, mast=mast, hsl=hsl:
                         e.tensor_scalar(
                             mast[:, hsl], mast[:, hsl], colD[:, i:i + 1],
                             None, ALU.mult),
                         reads=[colD_b, mast_b], wpart=[mast_b])
                for (mast, mast_b, dst, dst_b, dsl) in (
                        (m0, m0_b, w0n, w0n_b, hsl),
                        (m2, m2_b, w2n, w2n_b, hsl),
                        (m1, m1_b, w1n2, w1n2_b, hsl)):
                    E.mm([lambda t, mast=mast, hsl=hsl: t.transpose(
                        pb[7][0][:, 0:HD], mast[:, hsl], ident_f[:])],
                        reads=[mast_b, ident_f_b], writes=[pb[7][1]])
                    E.op("act", lambda e, dst=dst, dsl=dsl: e.activation(
                        dst[:, dsl], pb[7][0][:, 0:HD], AF.Copy),
                        reads=[pb[7][1]], wpart=[dst_b])
                E.op("dve", lambda e, hsl=hsl: e.tensor_copy(
                    w1t[:, hsl], m1[:, hsl]),
                    reads=[m1_b], wpart=[w1t_b])

            def p3_gen():
                head_TF(0)
                head_E_early(0)
                for h in range(NH):
                    if h > 0:
                        head_O(h - 1)
                    head_E_late(h)
                    if h + 1 < NH:
                        head_TF(h + 1)
                        head_E_early(h + 1)
                    head_G(h)
                    head_NS(h)
                    head_UPD(h)
                    yield h
                head_O(NH - 1)

            p3it = p3_gen()
            p3done = [-1]

            def p3_step_until(hmax):
                while p3done[0] < hmax:
                    try:
                        p3done[0] = next(p3it)
                    except StopIteration:
                        p3done[0] = NH
                        break

            emit_P2b()
            if c == 0:
                emit_hop2("g")
                emit_hop2("u")
            for _ in p3it:
                pass

            for h0 in range(0, 4 * NH, 64):
                h1 = min(h0 + 64, 4 * NH)
                _rsqrt(E, "dve", colsR[:, h0:h1], colsR_b, rmsS[:, h0:h1],
                       rmsS_b, rmsT[:, h0:h1], rmsT_b, magic64[:, 0:h1 - h0],
                       magic64_b)
            for h in range(NH):
                fns = [lambda t, tc=tc, h=h: t.transpose(
                    pb[6][0][0:1, tc * P:(tc + 1) * P],
                    colsR[:, h * 4 + tc:h * 4 + tc + 1], ident_f[:])
                    for tc in range(4)]
                E.mm(fns, reads=[colsR_b, ident_f_b], writes=[pb[6][1]])
                E.op("dve", lambda e: e.tensor_copy(
                    rowA[:], pb[6][0][0:1, 0:CH]),
                    reads=[pb[6][1]], writes=[rowA_b])
                E.mm([lambda t: t.matmul(pb[6][0][:], ones_f[0:1, :],
                                         rowA[:], start=True, stop=True)],
                     reads=[ones_f_b, rowA_b], writes=[pb[6][1]])
                E.op("dve", lambda e: e.tensor_copy(bcs[:], pb[6][0][:]),
                     reads=[pb[6][1]], writes=[bcs_b])
                E.op("dve", lambda e, h=h: e.tensor_tensor(
                    onc[:, h * CH:(h + 1) * CH], onc[:, h * CH:(h + 1) * CH],
                    bcs[:], ALU.mult),
                    reads=[bcs_b, onc_b], wpart=[onc_b])

            att_B = Buf(f"att_c{c}")
            for dc in range(n_dc):
                wp, wp_b = wst[dc % 2]
                wpv = wp[:, 0:n_dc * P]
                src = wproj_bf.rearrange("(hk p) j -> p hk j", p=P)[
                    :, :, dc * P:(dc + 1) * P]
                E.dma("qs", lambda e, wpv=wpv, src=src: e.dma_start(
                    out=wpv.rearrange("p (hk j) -> p hk j", j=P), in_=src),
                    reads=[wproj_bf_B], writes=[wp_b])
                bank = pb[dc % 2]
                fns = []
                for hk in range(NH):
                    fns.append(lambda t, hk=hk, wpv=wpv, bank=bank: t.matmul(
                        bank[0][:], wpv[:, hk * P:(hk + 1) * P],
                        onc[:, hk * CH:(hk + 1) * CH],
                        start=(hk == 0), stop=(hk == NH - 1)))
                E.mm(fns, reads=[wp_b, onc_b], writes=[bank[1]])
                ssl = slice((dc % 2) * CH, (dc % 2) * CH + CH)
                E.op("act", lambda e, bank=bank, ssl=ssl: e.activation(
                    stg[:, ssl], bank[0][:], AF.Copy),
                    reads=[bank[1]], writes=[stg_b])
                E.dma("qp", lambda e, dc=dc, ssl=ssl: e.dma_start(
                    out=att_scr[dc * P:(dc + 1) * P, :], in_=stg[:, ssl]),
                    reads=[stg_b], wpart=[att_B])

            if c == 0:
                emit_hop2("d")

            cs.close()
            E.fence()

            fs = ExitStack()

            def fsb(name, shape, dtype):
                return sb(f"{name}_c{c}", shape, dtype, fs)

            gated, gated_b = fsb("gated", [P, n_hc * CH], BF16)
            wgp = [fsb(f"wgp{i}", [P, n_dc * P], BF16) for i in range(2)]
            wup = [fsb(f"wup{i}", [P, n_dc * P], BF16) for i in range(2)]
            nhh = n_hc // 2
            wdp = [fsb(f"wdp{i}", [P, nhh * P], BF16) for i in range(2)]
            zz, zz_b = fsb("zz", [P, CH], BF16)
            att2 = [fsb(f"att2{i}", [P, CH], BF16) for i in range(2)]
            xr2 = [fsb(f"xr2{i}", [P, CH], F32) for i in range(2)]
            ost, ost_b = fsb("ost", [P, CH], F32)
            sfm, sfm_b = fsb("sfm", [P, CH], F32)

            for half in range(2):
                wdx, wd_b2 = wdp[half]
                h0 = half * nhh
                src2 = wdT.rearrange("(hc p) j -> p hc j", p=P)[
                    :, h0:h0 + nhh, 0:P]
                E.dma("qs", lambda e, wdx=wdx, src2=src2: e.dma_start(
                    out=wdx[:].rearrange("p (hc j) -> p hc j", j=P),
                    in_=src2),
                    reads=[wdT_B], writes=[wd_b2])
            for hc in range(n_hc):
                wgx, wg_b2 = wgp[hc % 2]
                wux, wu_b2 = wup[hc % 2]
                for (wx, wxb, srcT, srcB) in ((wgx, wg_b2, wgT, wgT_B),
                                              (wux, wu_b2, wuT, wuT_B)):
                    src = srcT.rearrange("(dk p) j -> p dk j", p=P)[
                        :, :, hc * P:(hc + 1) * P]
                    E.dma("qs", lambda e, wx=wx, src=src: e.dma_start(
                        out=wx[:].rearrange("p (dk j) -> p dk j", j=P),
                        in_=src),
                        reads=[srcB], writes=[wxb])
                bg = pb[(hc % 2) * 2]
                bu = pb[(hc % 2) * 2 + 1]
                for (bank, wx, wxb) in ((bg, wgx, wg_b2), (bu, wux, wu_b2)):
                    fns = []
                    for dk in range(n_dc):
                        fns.append(lambda t, dk=dk, wx=wx, bank=bank:
                                   t.matmul(
                            bank[0][:], wx[:, dk * P:(dk + 1) * P],
                            xnfm[:, dk * CH:(dk + 1) * CH],
                            start=(dk == 0), stop=(dk == n_dc - 1)))
                    E.mm(fns, reads=[wxb, xnfm_b], writes=[bank[1]])
                E.op("act", lambda e, bg=bg: e.activation(
                    zz[:], bg[0][:], AF.Silu),
                    reads=[bg[1]], writes=[zz_b])
                E.op("dve", lambda e, hc=hc, bu=bu: e.tensor_tensor(
                    gated[:, hc * CH:(hc + 1) * CH], zz[:], bu[0][:],
                    ALU.mult),
                    reads=[zz_b, bu[1]], wpart=[gated_b])

            for dc in range(n_dc):
                bank = pb[4 + dc % 2]
                for half in range(2):
                    wdx, wd_b2 = wdp[half]
                    h0 = half * nhh
                    if dc > 0:
                        src2 = wdT.rearrange("(hc p) j -> p hc j", p=P)[
                            :, h0:h0 + nhh, dc * P:(dc + 1) * P]
                        E.dma("qs", lambda e, wdx=wdx, src2=src2:
                              e.dma_start(
                                  out=wdx[:].rearrange(
                                      "p (hc j) -> p hc j", j=P),
                                  in_=src2),
                              reads=[wdT_B], writes=[wd_b2])
                    fns = []
                    for hh in range(nhh):
                        hc = h0 + hh
                        fns.append(lambda t, hh=hh, hc=hc, wdx=wdx,
                                   bank=bank: t.matmul(
                            bank[0][:], wdx[:, hh * P:(hh + 1) * P],
                            gated[:, hc * CH:(hc + 1) * CH],
                            start=(hc == 0), stop=(hc == n_hc - 1)))
                    if half == 0:
                        E.mm(fns, reads=[wd_b2, gated_b], writes=[bank[1]])
                    else:
                        E.mm(fns, reads=[wd_b2, gated_b], wpart=[bank[1]])
                at, at_b = att2[dc % 2]
                E.dma("qs", lambda e, at=at, dc=dc: e.dma_start(
                    out=at[:], in_=att_scr[dc * P:(dc + 1) * P, :]),
                    reads=[att_B], writes=[at_b])
                xr, xr_b = xr2[dc % 2]
                xsrc = x_d[c * CH:(c + 1) * CH,
                           dc * P:(dc + 1) * P].rearrange(
                    "(tc p) d -> p tc d", p=P)
                E.dma("qs", lambda e, xr=xr, xsrc=xsrc: e.dma_start(
                    out=xr[:].rearrange("p (tc d) -> p tc d", d=P),
                    in_=xsrc), writes=[xr_b])
                E.op("dve", lambda e, bank=bank, at=at: e.tensor_tensor(
                    sfm[:], bank[0][:], at[:], ALU.add),
                    reads=[bank[1], at_b], writes=[sfm_b])
                tb = pb[6]
                fns = [lambda t, tc=tc, tb=tb: t.transpose(
                    tb[0][:, tc * P:(tc + 1) * P],
                    sfm[:, tc * P:(tc + 1) * P], ident_f[:])
                    for tc in range(4)]
                E.mm(fns, reads=[sfm_b, ident_f_b], writes=[tb[1]])
                E.op("dve", lambda e, tb=tb, xr=xr: e.tensor_tensor(
                    ost[:].rearrange("p (tc d) -> p tc d", d=P),
                    tb[0][:, 0:CH].rearrange("p (tc d) -> p tc d", d=P),
                    xr[:].rearrange("p (tc d) -> p tc d", d=P), ALU.add),
                    reads=[tb[1], xr_b], writes=[ost_b])
                ydst = y_d[c * CH:(c + 1) * CH, dc * P:(dc + 1) * P]
                E.dma("qp", lambda e, ydst=ydst: e.dma_start(
                    out=ydst.rearrange("(tc p) d -> p tc d", p=P),
                    in_=ost[:].rearrange("p (tc d) -> p tc d", d=P)),
                    reads=[ost_b])
                if c + 1 < n_chunks:
                    step = max(1, n_dc // 4)
                    if n_dc >= 8:
                        for tc_ in range(4):
                            if dc == tc_ * step:
                                emit_P1_chain(c + 1, tc_)
                            if dc == tc_ * step + step // 2:
                                emit_P1_tr(c + 1, tc_)
                    else:
                        for tc_ in range(4):
                            if dc == min(tc_ * step, n_dc - 1):
                                emit_P1_tc(c + 1, tc_)
            fs.close()
            E.fence()

        E.finish()

    return nc


_CACHE = {}


def _get_nc(key):
    if key not in _CACHE:
        _CACHE[key] = build_nc(*key)
    return _CACHE[key]


def kernel(L0_attn_x, L0_attn_w0, L0_attn_w1, L0_attn_w2, L0_attn_W_qkv,
           L0_attn_W_lr, L0_attn_b_lr, L0_attn_W_proj, L0_attn_onorm_w,
           L0_ffn_w_gate, L0_ffn_w_up, L0_ffn_w_down):
    x = np.ascontiguousarray(np.asarray(L0_attn_x, dtype=np.float32))
    B = x.shape[0]
    nc = _get_nc((16, 8, 64))
    common = dict(
        w0=np.ascontiguousarray(L0_attn_w0, np.float32),
        w1=np.ascontiguousarray(L0_attn_w1, np.float32),
        w2=np.ascontiguousarray(L0_attn_w2, np.float32),
        wqkv=np.ascontiguousarray(L0_attn_W_qkv, np.float32),
        wlr=np.ascontiguousarray(L0_attn_W_lr, np.float32),
        blr=np.ascontiguousarray(L0_attn_b_lr, np.float32),
        wproj=np.ascontiguousarray(L0_attn_W_proj, np.float32),
        onw=np.ascontiguousarray(L0_attn_onorm_w, np.float32),
        wg=np.ascontiguousarray(L0_ffn_w_gate, np.float32),
        wu=np.ascontiguousarray(L0_ffn_w_up, np.float32),
        wd=np.ascontiguousarray(L0_ffn_w_down, np.float32),
    )
    in_maps = [dict(common, x=np.ascontiguousarray(x[i])) for i in range(B)]
    res = run_bass_kernel_spmd(nc, in_maps, core_ids=list(range(B)))
    return np.stack([np.asarray(r["y"], dtype=np.float32)
                     for r in res.results], axis=0)


# version 17 (final)
# speedup vs baseline: 1.2407x; 1.0228x over previous
import math
from contextlib import ExitStack

import numpy as np

import concourse.bass as bass
import concourse.mybir as mybir
from concourse.bass_utils import run_bass_kernel_spmd

F32 = mybir.dt.float32
BF16 = mybir.dt.bfloat16
I32 = mybir.dt.int32
AF = mybir.ActivationFunctionType
ALU = mybir.AluOpType
X = mybir.AxisListType.X

NS_A, NS_B, NS_C = 3.4445, -4.775, 2.0315
BASE_LR_INV = float(np.log(np.expm1(0.01)))
MAGIC = 0x5F3759DF
TINY = 1e-37

CH = 512
HD = 128
P = 128


class Buf:
    __slots__ = ("writers", "readers", "name")

    def __init__(self, name=""):
        self.writers = []
        self.readers = []
        self.name = name


class Emit:

    N_QS = 16
    N_QP = 8

    def __init__(self, nc, stack, max_ops=None):
        self.max_ops = max_ops
        self.ncalls = 0
        self.nc = nc
        self.sems = {}
        self.cnt = {}
        for name in ("pe", "act", "dve", "pool"):
            self.sems[name] = stack.enter_context(nc.semaphore("sem_" + name))
            self.cnt[name] = 0
        for i in range(self.N_QS):
            self.sems[f"qs{i}"] = stack.enter_context(
                nc.semaphore(f"sem_qs{i}"))
            self.cnt[f"qs{i}"] = 0
        for i in range(self.N_QP):
            self.sems[f"qp{i}"] = stack.enter_context(
                nc.semaphore(f"sem_qp{i}"))
            self.cnt[f"qp{i}"] = 0
        self.rr = {"qs": 0, "qp": 0}
        self.eng = {"pe": nc.tensor, "act": nc.scalar, "dve": nc.vector,
                    "pool": nc.gpsimd, "sp": nc.sync}
        self.own = {"pe": "pe", "act": "act", "dve": "dve", "pool": "pool"}
        self.floors = {e: {} for e in self.eng}
        self.nwaits = 0

    def _wait(self, ename, sem_name, val):
        if val <= 0:
            return
        fl = self.floors[ename]
        if fl.get(sem_name, 0) >= val:
            return
        self.eng[ename].wait_ge(self.sems[sem_name], val)
        fl[sem_name] = val
        self.nwaits += 1

    def _need(self, ename, toks):
        best = {}
        for (s, v) in toks:
            if ename == "pe" and s == "pe":
                continue
            if best.get(s, 0) < v:
                best[s] = v
        for s in sorted(best):
            self._wait(ename, s, best[s])

    def _collect(self, reads, writes, wpart):
        toks = []
        for b in reads:
            toks += b.writers
        for b in writes:
            toks += b.readers + b.writers
        for b in wpart:
            toks += b.readers
        return toks

    def _apply(self, tok, reads, writes, wpart):
        for b in reads:
            b.readers.append(tok)
        for b in writes:
            b.writers = [tok]
            b.readers = []
        for b in wpart:
            b.writers.append(tok)

    def op(self, ename, fn, reads=(), writes=(), wpart=()):
        self.ncalls += 1
        if self.max_ops is not None and self.ncalls > self.max_ops:
            return ("pe", 0)
        self._need(ename, self._collect(reads, writes, wpart))
        instr = fn(self.eng[ename])
        sem = self.own[ename]
        self.cnt[sem] += 1
        instr.then_inc(self.sems[sem], 1)
        tok = (sem, self.cnt[sem])
        self._apply(tok, reads, writes, wpart)
        return tok

    def mm(self, fns, reads=(), writes=(), wpart=()):
        self.ncalls += 1
        if self.max_ops is not None and self.ncalls > self.max_ops:
            return ("pe", 0)
        self._need("pe", self._collect(reads, writes, wpart))
        instrs = [fn(self.nc.tensor) for fn in fns]
        self.cnt["pe"] += 1
        instrs[-1].then_inc(self.sems["pe"], 1)
        tok = ("pe", self.cnt["pe"])
        self._apply(tok, reads, writes, wpart)
        return tok

    def dma(self, ring, fn, reads=(), writes=(), wpart=()):
        self.ncalls += 1
        if self.max_ops is not None and self.ncalls > self.max_ops:
            return ("pe", 0)
        ename = "sp" if ring == "qs" else "pool"
        n = self.N_QS if ring == "qs" else self.N_QP
        sem = f"{ring}{self.rr[ring] % n}"
        self.rr[ring] += 1
        self._wait(ename, sem, self.cnt[sem])
        self._need(ename, self._collect(reads, writes, wpart))
        instr = fn(self.eng[ename])
        self.cnt[sem] += 16
        instr.then_inc(self.sems[sem], 16)
        tok = (sem, self.cnt[sem])
        self._apply(tok, reads, writes, wpart)
        return tok

    def fence(self):
        vals = dict(self.cnt)
        for e in self.eng:
            for s, v in vals.items():
                if self.own.get(e) == s:
                    continue
                self._wait(e, s, v)

    def finish(self):
        for s, v in self.cnt.items():
            self._wait("sp", s, v)


def _rsqrt(E, ename, out, out_b, s_ap, s_b, t1, t1_b, magic_ap, magic_b,
           iters=2):
    ov = out.bitcast(I32)
    sv = s_ap.bitcast(I32)
    E.op(ename, lambda e: e.tensor_single_scalar(ov, sv, 1,
                                                 ALU.arith_shift_right),
         reads=[s_b], writes=[out_b])
    E.op(ename, lambda e: e.tensor_tensor(ov, magic_ap, ov, ALU.subtract),
         reads=[magic_b, out_b], writes=[out_b])
    for _ in range(iters):
        E.op(ename, lambda e: e.tensor_tensor(t1, out, out, ALU.mult),
             reads=[out_b], writes=[t1_b])
        E.op(ename, lambda e: e.scalar_tensor_tensor(
            t1, t1, -0.5, s_ap, ALU.mult, ALU.mult),
            reads=[t1_b, s_b], writes=[t1_b])
        E.op(ename, lambda e: e.scalar_tensor_tensor(
            out, t1, 1.5, out, ALU.add, ALU.mult),
            reads=[out_b, t1_b], writes=[out_b])



def _inv_sqrt_eps(E, ename, out, out_b, s_ap, s_b, t1, t1_b, magic_ap,
                  magic_b, eps):
    _rsqrt(E, ename, out, out_b, s_ap, s_b, t1, t1_b, magic_ap, magic_b)
    E.op(ename, lambda e: e.tensor_scalar(t1, out, -eps, 1.0, ALU.mult,
                                          ALU.add),
         reads=[out_b], writes=[t1_b])
    E.op(ename, lambda e: e.tensor_tensor(out, out, t1, ALU.mult),
         reads=[out_b, t1_b], writes=[out_b])


def build_nc(n_dc=16, n_chunks=8, n_hc=64, ts_hop=4096,
             max_ops=None):
    D = n_dc * P
    HID = n_hc * P
    L = n_chunks * CH
    NH = n_dc
    D3 = 3 * D
    JW = 256
    n_jg = D3 // JW
    invD = 1.0 / D

    nc = bass.Bass("TRN2", target_bir_lowering=False)
    g = ExitStack()

    x_d = nc.dram_tensor("x", [L, D], F32, kind="ExternalInput").ap()
    w0_d = nc.dram_tensor("w0", [NH, HD, HD], F32, kind="ExternalInput").ap()
    w1_d = nc.dram_tensor("w1", [NH, HD, HD], F32, kind="ExternalInput").ap()
    w2_d = nc.dram_tensor("w2", [NH, HD, HD], F32, kind="ExternalInput").ap()
    wqkv_d = nc.dram_tensor("wqkv", [D, D3], F32, kind="ExternalInput").ap()
    wlr_d = nc.dram_tensor("wlr", [D, 3 * NH], F32, kind="ExternalInput").ap()
    blr_d = nc.dram_tensor("blr", [3 * NH], F32, kind="ExternalInput").ap()
    wproj_d = nc.dram_tensor("wproj", [D, D], F32, kind="ExternalInput").ap()
    onw_d = nc.dram_tensor("onw", [HD], F32, kind="ExternalInput").ap()
    wg_d = nc.dram_tensor("wg", [HID, D], F32, kind="ExternalInput").ap()
    wu_d = nc.dram_tensor("wu", [HID, D], F32, kind="ExternalInput").ap()
    wd_d = nc.dram_tensor("wd", [D, HID], F32, kind="ExternalInput").ap()
    y_d = nc.dram_tensor("y", [L, D], F32, kind="ExternalOutput").ap()

    wqkv_bf = nc.dram_tensor("wqkv_bf", [D, D3], BF16, kind="Internal").ap()
    wproj_bf = nc.dram_tensor("wproj_bf", [D, D], BF16, kind="Internal").ap()
    wg_blk = nc.dram_tensor("wg_blk", [n_dc, HID, P], BF16,
                            kind="Internal").ap()
    wu_blk = nc.dram_tensor("wu_blk", [n_dc, HID, P], BF16,
                            kind="Internal").ap()
    wd_blk = nc.dram_tensor("wd_blk", [n_hc, D, P], BF16,
                            kind="Internal").ap()
    wgT = nc.dram_tensor("wgT", [D, HID], BF16, kind="Internal").ap()
    wuT = nc.dram_tensor("wuT", [D, HID], BF16, kind="Internal").ap()
    wdT = nc.dram_tensor("wdT", [HID, D], BF16, kind="Internal").ap()
    att_scr = nc.dram_tensor("att_scr", [D, CH], BF16, kind="Internal").ap()

    import ml_dtypes
    ident_f_d = nc.inline_tensor(np.eye(P, dtype=np.float32), "c_ident_f").ap()
    ident_b_d = nc.inline_tensor(np.eye(P, dtype=ml_dtypes.bfloat16),
                                 "c_ident_b").ap()
    ones_f_d = nc.inline_tensor(np.ones((P, P), np.float32), "c_ones_f").ap()
    ones_b_d = nc.inline_tensor(np.ones((P, P), ml_dtypes.bfloat16),
                                "c_ones_b").ap()
    aI_d = nc.inline_tensor((NS_A * np.eye(P)).astype(ml_dtypes.bfloat16),
                            "c_aI_b").ap()
    magic_d = nc.inline_tensor(np.full((P, 16), MAGIC, np.int32),
                               "c_magic").ap()
    magic64_d = nc.inline_tensor(np.full((P, 4 * NH), MAGIC, np.int32),
                                 "c_magic64").ap()

    def sb(name, shape, dtype, st=g):
        t = st.enter_context(nc.sbuf_tensor(name, shape, dtype))
        return t, Buf(name)

    ident_f, ident_f_b = sb("ident_f", [P, P], F32)
    ident_b, ident_b_b = sb("ident_b", [P, P], BF16)
    ones_f, ones_f_b = sb("ones_f", [P, P], F32)
    ones_b, ones_b_b = sb("ones_b", [P, P], BF16)
    aI, aI_b = sb("aI", [P, P], BF16)
    magic, magic_b = sb("magic", [P, 16], I32)
    magic64, magic64_b = sb("magic64", [P, 4 * NH], I32)
    blr_row, blr_row_b = sb("blr_row", [1, 3 * NH], F32)
    gam, gam_b = sb("gam", [P, 1], F32)
    wlr_bf, wlr_bf_b = sb("wlr_bf", [P, n_dc * 3 * NH], BF16)

    m0, m0_b = sb("m0", [P, NH * HD], F32)
    m1, m1_b = sb("m1", [P, NH * HD], F32)
    m2, m2_b = sb("m2", [P, NH * HD], F32)
    wn, wn_b = sb("wn", [P, 3 * NH], F32)
    w0n, w0n_b = sb("w0n", [P, NH * HD], BF16)
    w2n, w2n_b = sb("w2n", [P, NH * HD], BF16)
    w1t, w1t_b = sb("w1t", [P, NH * HD], BF16)
    w1n2, w1n2_b = sb("w1n2", [P, NH * HD], BF16)

    xnfm, xnfm_b = sb("xnfm", [P, n_dc * CH], BF16)
    lrf, lrf_b = sb("lrf", [P, 4 * 3 * NH], F32)
    rmsS, rmsS_b = sb("rmsS", [P, 4 * NH], F32)
    colsR, colsR_b = sb("colsR", [P, 4 * NH], F32)
    rmsT, rmsT_b = sb("rmsT", [P, 4 * NH], F32)

    xf = [sb(f"xf{i}", [P, D], F32) for i in range(2)]
    stg, stg_b = sb("stg", [P, max(D, 2 * CH)], BF16)
    colA, colA_b = sb("colA", [P, 16], F32)
    colB, colB_b = sb("colB", [P, 16], F32)
    colC, colC_b = sb("colC", [P, 16], F32)
    colD, colD_b = sb("colD", [P, 16], F32)
    rowA, rowA_b = sb("rowA", [1, CH], F32)
    TS_HOP = min(ts_hop, HID, D)
    tT, tT_b = sb("hop2stg", [P, TS_HOP], BF16)

    pb = []
    for i in range(8):
        t = g.enter_context(nc.psum_tensor(f"pb{i}", [P, 512], F32))
        pb.append((t, Buf(f"pb{i}")))

    def pbf(i):
        return pb[i][0].ap().bitcast(BF16)

    E = Emit(nc, g, max_ops=max_ops)

    with nc.Block():
        for (dst, dst_b, src) in ((ident_f, ident_f_b, ident_f_d),
                                  (ident_b, ident_b_b, ident_b_d),
                                  (ones_f, ones_f_b, ones_f_d),
                                  (ones_b, ones_b_b, ones_b_d),
                                  (aI, aI_b, aI_d),
                                  (magic, magic_b, magic_d),
                                  (magic64, magic64_b, magic64_d)):
            E.dma("qs", lambda e, d=dst, s=src: e.dma_start(out=d[:], in_=s),
                  writes=[dst_b])
        E.dma("qs", lambda e: e.dma_start(out=blr_row[:], in_=blr_d[None, :]),
              writes=[blr_row_b])
        E.dma("qs", lambda e: e.dma_start(out=gam[:], in_=onw_d[:, None]),
              writes=[gam_b])
        E.op("dve", lambda e: e.tensor_scalar(
            blr_row[:], blr_row[:], BASE_LR_INV, None, ALU.add),
            reads=[blr_row_b], writes=[blr_row_b])
        for dk in range(n_dc):
            E.dma("qp", lambda e, dk=dk: e.dma_start(
                out=wlr_bf[:, dk * 3 * NH:(dk + 1) * 3 * NH],
                in_=wlr_d[dk * P:(dk + 1) * P, :]),
                wpart=[wlr_bf_b])

        wqkv_bf_B = Buf("wqkv_bf")
        wproj_bf_B = Buf("wproj_bf")
        E.dma("qp", lambda e: e.dma_start(out=wqkv_bf[:], in_=wqkv_d[:]),
              writes=[wqkv_bf_B])
        wg_blk_B = [Buf(f"wg_blk{i}") for i in range(n_dc)]
        wu_blk_B = [Buf(f"wu_blk{i}") for i in range(n_dc)]
        wd_blk_B = [Buf(f"wd_blk{i}") for i in range(n_hc)]
        for dk in range(n_dc):
            E.dma("qp", lambda e, dk=dk: e.dma_start(
                out=wg_blk[dk], in_=wg_d[:, dk * P:(dk + 1) * P]),
                writes=[wg_blk_B[dk]])
            E.dma("qp", lambda e, dk=dk: e.dma_start(
                out=wu_blk[dk], in_=wu_d[:, dk * P:(dk + 1) * P]),
                writes=[wu_blk_B[dk]])
        E.dma("qp", lambda e: e.dma_start(out=wproj_bf[:], in_=wproj_d[:]),
              writes=[wproj_bf_B])
        for hc in range(n_hc):
            E.dma("qp", lambda e, hc=hc: e.dma_start(
                out=wd_blk[hc], in_=wd_d[:, hc * P:(hc + 1) * P]),
                writes=[wd_blk_B[hc]])

        def emit_P1_chain(cc, tc):
            if True:
                xt, xt_b = xf[tc % 2]
                r0 = cc * CH + tc * P
                E.dma("qs", lambda e, xt=xt, r0=r0: e.dma_start(
                    out=xt[:], in_=x_d[r0:r0 + P, :]), writes=[xt_b])
                E.op("dve", lambda e, xt=xt, tc=tc: e.tensor_reduce(
                    colA[:, tc:tc + 1], xt[:], X, ALU.add),
                    reads=[xt_b], wpart=[colA_b])
                E.op("act", lambda e, xt=xt, tc=tc: e.activation(
                    stg[:, 0:D], xt[:], AF.Square,
                    accum_out=colB[:, tc:tc + 1]),
                    reads=[xt_b], writes=[stg_b], wpart=[colB_b])
                E.op("dve", lambda e, tc=tc: e.tensor_scalar(
                    colC[:, 0:1], colA[:, tc:tc + 1], invD, None, ALU.mult),
                    reads=[colA_b], writes=[colC_b])
                E.op("dve", lambda e: e.tensor_tensor(
                    colD[:, 0:1], colC[:, 0:1], colC[:, 0:1], ALU.mult),
                    reads=[colC_b], writes=[colD_b])
                E.op("dve", lambda e, tc=tc: e.scalar_tensor_tensor(
                    colD[:, 1:2], colB[:, tc:tc + 1], invD, colD[:, 0:1],
                    ALU.mult, ALU.subtract),
                    reads=[colB_b, colD_b], wpart=[colD_b])
                E.op("dve", lambda e: e.tensor_scalar(
                    colD[:, 1:2], colD[:, 1:2], 1e-5, None, ALU.add),
                    reads=[colD_b], wpart=[colD_b])
                _rsqrt(E, "dve", colC[:, 1:2], colC_b, colD[:, 1:2], colD_b,
                       colD[:, 2:3], colD_b, magic[:, 0:1], magic_b)
                E.op("dve", lambda e: e.scalar_tensor_tensor(
                    colD[:, 3:4], colC[:, 0:1], -1.0, colC[:, 1:2],
                    ALU.mult, ALU.mult),
                    reads=[colC_b], wpart=[colD_b])
                E.op("act", lambda e, xt=xt: e.activation(
                    stg[:, 0:D], xt[:], AF.Identity,
                    bias=colD[:, 3:4], scale=colC[:, 1:2]),
                    reads=[xt_b, colC_b, colD_b], writes=[stg_b])

        def emit_P1_tr(cc, tc):
            if True:
                for dk0 in range(0, n_dc, 4):
                    ng = min(4, n_dc - dk0)
                    fns = []
                    for q in range(ng):
                        dk = dk0 + q
                        fns.append(lambda t, dk=dk, q=q: t.transpose(
                            pbf(7)[:, q * P:(q + 1) * P],
                            stg[:, dk * P:(dk + 1) * P], ident_b[:]))
                    E.mm(fns, reads=[stg_b, ident_b_b], writes=[pb[7][1]])
                    dst = xnfm[:].rearrange("p (dk t) -> p dk t", t=CH)[
                        :, dk0:dk0 + ng, tc * P:(tc + 1) * P]
                    E.op("dve", lambda e, dst=dst, ng=ng: e.tensor_copy(
                        dst, pbf(7)[:, 0:ng * P].rearrange(
                            "p (q t) -> p q t", t=P)),
                        reads=[pb[7][1]], wpart=[xnfm_b])



        def emit_P1_tc(cc, tc):
            emit_P1_chain(cc, tc)
            emit_P1_tr(cc, tc)

        def emit_P1(cc):
            for tc in range(4):
                emit_P1_tc(cc, tc)

        emit_P1(0)

        wtmp = [sb(f"wtmp{i}", [P, HD], F32) for i in range(2)]
        wns, wns_b = sb("wns", [P, 3 * NH], F32)
        for h in range(NH):
            for wi, (wdin, mast, mast_b, nat, nat_b) in enumerate(
                    ((w0_d, m0, m0_b, w0n, w0n_b),
                     (w1_d, m1, m1_b, w1n2, w1n2_b),
                     (w2_d, m2, m2_b, w2n, w2n_b))):
                wt, wt_b = wtmp[(h * 3 + wi) % 2]
                sl = slice(h * HD, (h + 1) * HD)
                E.dma("qs", lambda e, s=wdin, w=wt, h=h: e.dma_start(
                    out=w[:], in_=s[h]), writes=[wt_b])
                if nat is not None:
                    E.op("dve", lambda e, w=wt, n=nat, sl=sl:
                         e.tensor_copy(n[:, sl], w[:]),
                         reads=[wt_b], wpart=[nat_b])
                E.mm([lambda t, w=wt: t.transpose(pb[7][0][:, 0:HD], w[:],
                                                  ident_f[:])],
                     reads=[wt_b, ident_f_b], writes=[pb[7][1]])
                E.op("dve", lambda e, m=mast, sl=sl:
                     e.tensor_copy(m[:, sl], pb[7][0][:, 0:HD]),
                     reads=[pb[7][1]], wpart=[mast_b])
                if wi == 1:
                    E.op("dve", lambda e, sl=sl:
                         e.tensor_copy(w1t[:, sl], m1[:, sl]),
                         reads=[m1_b], wpart=[w1t_b])
                E.op("act", lambda e, m=mast, sl=sl, wi=wi:
                     e.activation(pb[7][0][:, HD:2 * HD], m[:, sl], AF.Square,
                                  accum_out=colA[:, wi:wi + 1]),
                     reads=[mast_b], writes=[pb[7][1], colA_b])
                E.op("dve", lambda e, wi=wi, h=h: e.tensor_scalar(
                    wn[:, wi * NH + h:wi * NH + h + 1], colA[:, wi:wi + 1],
                    TINY, None, ALU.add),
                    reads=[colA_b], wpart=[wn_b])
        ncol = 3 * NH
        for c0 in range(0, ncol, 16):
            c1 = min(c0 + 16, ncol)
            _rsqrt(E, "dve", wns[:, c0:c1], wns_b, wn[:, c0:c1], wn_b,
                   colB[:, 0:c1 - c0], colB_b, magic[:, 0:c1 - c0], magic_b)
        E.op("dve", lambda e: e.tensor_tensor(wn[:, 0:ncol], wn[:, 0:ncol],
                                              wns[:, 0:ncol], ALU.mult),
             reads=[wn_b, wns_b], writes=[wn_b])

        wgT_B, wuT_B, wdT_B = Buf("wgT"), Buf("wuT"), Buf("wdT")
        _hop2 = {"g": (wg_blk, wg_blk_B, wgT, wgT_B, n_dc, HID),
                 "u": (wu_blk, wu_blk_B, wuT, wuT_B, n_dc, HID),
                 "d": (wd_blk, wd_blk_B, wdT, wdT_B, n_hc, D)}

        def emit_hop2(which):
            blk, blkB, dstT, dstB, nblk, rows = _hop2[which]
            for k in range(nblk):
                for r0 in range(0, rows, TS_HOP):
                    r1 = min(r0 + TS_HOP, rows)
                    E.dma("qs", lambda e, k=k, blk=blk, r0=r0, r1=r1:
                          e.dma_start_transpose(tT[:, 0:r1 - r0],
                                                blk[k][r0:r1, :]),
                          reads=[blkB[k]], writes=[tT_b])
                    E.dma("qp", lambda e, k=k, d=dstT, r0=r0, r1=r1:
                          e.dma_start(out=d[k * P:(k + 1) * P, r0:r1],
                                      in_=tT[:, 0:r1 - r0]),
                          reads=[tT_b], wpart=[dstB])


        for c in range(n_chunks):
            cs = ExitStack()

            def csb(name, shape, dtype):
                return sb(f"{name}_c{c}", shape, dtype, cs)

            qkv, qkv_b = csb("qkv", [P, 4 * D3], BF16)
            qkv_bJ = [Buf(f"qkv_c{c}_j{j}") for j in range(n_jg)]
            jg_of = lambda si, h: (si * D + h * HD) // JW
            wst = [csb(f"wst{i}", [P, n_dc * JW], BF16) for i in range(2)]
            onc, onc_b = csb("onc", [P, NH * CH], BF16)
            gaten, gaten_b = csb("gaten", [P, CH], F32)
            hbn, hbn_b = csb("hbn", [P, CH], F32)
            sig, sig_b = csb("sig", [P, CH], F32)
            f1s, f1s_b = csb("f1s", [P, CH], F32)
            sg, sg_b = csb("sg", [P, CH], BF16)
            hl1, hl1_b = csb("hl1", [P, CH], BF16)
            dh2, dh2_b = csb("dh2", [P, CH], BF16)
            dg0, dg0_b = csb("dg0", [P, CH], BF16)
            zss, zss_b = csb("zss", [P, CH], BF16)
            zfm, zfm_b = csb("zfm", [P, CH], BF16)
            sqs, sqs_b = csb("sqs", [P, CH], BF16)
            bcs, bcs_b = csb("bcs", [P, CH], F32)
            sqb, sqb_b = csb("sqb", [P, D], BF16)
            ew2, ew2_b = csb("ew2", [P, CH], F32)
            nsx = []
            for i in range(3):
                nsx.append({k: csb(f"ns_{k}{i}", [P, HD],
                                   BF16 if k in ("x", "xt", "as", "bm")
                                   else F32)
                            for k in ("x", "xt", "as", "bas", "bm")})
            kfm = [csb(f"kfm{i}", [P, CH], BF16) for i in range(2)]
            vfm = [csb(f"vfm{i}", [P, CH], BF16) for i in range(2)]
            qfm = [csb(f"qfm{i}", [P, CH], BF16) for i in range(2)]

            za = 3 * NH
            for tc in range(4):
                fns = []
                for dk in range(n_dc):
                    fns.append(lambda t, dk=dk, tc=tc: t.matmul(
                        pb[6][0][:, 0:za],
                        xnfm[:, dk * CH + tc * P:dk * CH + (tc + 1) * P],
                        wlr_bf[:, dk * za:(dk + 1) * za],
                        start=(dk == 0), stop=False))
                fns.append(lambda t: t.matmul(
                    pb[6][0][:, 0:za], ones_f[0:1, :],
                    blr_row[:], start=False, stop=True))
                E.mm(fns, reads=[xnfm_b, wlr_bf_b, ones_f_b, blr_row_b],
                     writes=[pb[6][1]])
                E.op("act", lambda e: e.activation(
                    gaten[:, 0:za], pb[6][0][:, 0:za], AF.Abs),
                    reads=[pb[6][1]], writes=[gaten_b])
                E.op("act", lambda e: e.activation(
                    gaten[:, 0:za], gaten[:, 0:za], AF.Exp, scale=-1.0),
                    reads=[gaten_b], writes=[gaten_b])
                E.op("act", lambda e: e.activation(
                    gaten[:, 0:za], gaten[:, 0:za], AF.Ln, bias=1.0),
                    reads=[gaten_b], writes=[gaten_b])
                E.op("act", lambda e: e.activation(
                    bcs[:, 0:za], pb[6][0][:, 0:za], AF.Relu),
                    reads=[pb[6][1]], writes=[bcs_b])
                E.op("dve", lambda e, tc=tc: e.tensor_tensor(
                    lrf[:, tc * za:(tc + 1) * za], bcs[:, 0:za],
                    gaten[:, 0:za], ALU.add),
                    reads=[bcs_b, gaten_b], wpart=[lrf_b])

            def emit_P2b():
              for jg in range(n_jg):
                wq, wq_b = wst[jg % 2]
                src = wqkv_bf.rearrange("(dk p) j -> p dk j", p=P)[
                    :, :, jg * JW:(jg + 1) * JW]
                E.dma("qs", lambda e, wq=wq, src=src: e.dma_start(
                    out=wq[:].rearrange("p (dk j) -> p dk j", j=JW), in_=src),
                    reads=[wqkv_bf_B], writes=[wq_b])
                for tc in range(4):
                    bank = pb[(jg * 4 + tc) % 2]
                    psl = bank[0][:, 0:JW]
                    fns = []
                    for dk in range(n_dc):
                        fns.append(lambda t, dk=dk, tc=tc, psl=psl, wq=wq:
                                   t.matmul(
                            psl, xnfm[:, dk * CH + tc * P:
                                      dk * CH + (tc + 1) * P],
                            wq[:, dk * JW:(dk + 1) * JW],
                            start=(dk == 0), stop=(dk == n_dc - 1)))
                    E.mm(fns, reads=[xnfm_b, wq_b], writes=[bank[1]])
                    E.op("act", lambda e, psl=psl, tc=tc, jg=jg: e.activation(
                        qkv[:, tc * D3 + jg * JW:tc * D3 + (jg + 1) * JW],
                        psl, AF.Silu),
                        reads=[bank[1]], wpart=[qkv_bJ[jg]])
                    if jg >= 2 * D // JW:
                        if tc == 3:
                            hallow[0] = ((jg + 1) * JW - 2 * D) // HD
                        gd = (jg - 2 * D // JW) * 4 + tc + 1
                        ngrp = (n_jg - 2 * D // JW) * 4
                        p3_step((gd * TOT + ngrp - 1) // ngrp)
                if jg == D // JW - 1:
                    emit_P2c(0)
                if jg == 2 * D // JW - 1:
                    emit_P2c(1)

            def emit_P2c(si):
                dst, dst_b = ((colA, colA_b), (colB, colB_b))[si]
                for tc in range(4):
                    qs0 = tc * D3 + si * D
                    E.op("dve", lambda e, qs0=qs0: e.tensor_tensor(
                        sqb[:], qkv[:, qs0:qs0 + D], qkv[:, qs0:qs0 + D],
                        ALU.mult),
                        reads=qkv_bJ[si * (D // JW):(si + 1) * (D // JW)],
                        writes=[sqb_b])
                    E.op("dve", lambda e, dst=dst: e.tensor_reduce(
                        dst[:, 0:NH], sqb[:].rearrange(
                            "p (h e) -> p h e", e=HD), X, ALU.add),
                        reads=[sqb_b], writes=[dst_b])
                    E.op("dve", lambda e, dst=dst: e.tensor_scalar(
                        dst[:, 0:NH], dst[:, 0:NH], TINY, None, ALU.add),
                        reads=[dst_b], wpart=[dst_b])
                    _inv_sqrt_eps(E, "dve", colD[:, 0:NH], colD_b,
                                  dst[:, 0:NH], dst_b, colC[:, 0:NH], colC_b,
                                  magic[:, 0:NH], magic_b, 1e-5)
                    for h in range(NH):
                        qs0h = tc * D3 + si * D + h * HD
                        jb = qkv_bJ[jg_of(si, h)]
                        E.op("dve", lambda e, qs0h=qs0h, h=h:
                             e.tensor_scalar(
                                 qkv[:, qs0h:qs0h + HD],
                                 qkv[:, qs0h:qs0h + HD],
                                 colD[:, h:h + 1], None, ALU.mult),
                             reads=[colD_b, jb], wpart=[jb])

            def head_TF(h):
                par = h % 2
                kf, kf_b = kfm[par]
                vf, vf_b = vfm[par]
                qf, qf_b = qfm[par]
                hsl = slice(h * HD, (h + 1) * HD)
                for (si, dst, dst_b) in ((1, kf, kf_b), (2, vf, vf_b),
                                         (0, qf, qf_b)):
                    fns = []
                    for tc in range(4):
                        src = qkv[:, tc * D3 + si * D + h * HD:
                                  tc * D3 + si * D + (h + 1) * HD]
                        fns.append(lambda t, src=src, tc=tc: t.transpose(
                            pbf(7)[:, tc * P:(tc + 1) * P], src, ident_b[:]))
                    E.mm(fns, reads=[qkv_bJ[jg_of(si, h)], ident_b_b],
                         writes=[pb[7][1]])
                    E.op("dve", lambda e, dst=dst: e.tensor_copy(
                        dst[:], pbf(7)[:, 0:512]),
                        reads=[pb[7][1]], writes=[dst_b])
                for (bank, wmat, wmat_b, opfm, opfm_b) in (
                        (pb[0], w0n, w0n_b, kf, kf_b),
                        (pb[1], w2n, w2n_b, kf, kf_b),
                        (pb[2], w1t, w1t_b, vf, vf_b)):
                    fns = []
                    for tc in range(4):
                        fns.append(lambda t, tc=tc, bank=bank, opfm=opfm,
                                   wmat=wmat, hsl=hsl: t.matmul(
                            bank[0][:, tc * P:(tc + 1) * P],
                            opfm[:, tc * P:(tc + 1) * P], wmat[:, hsl],
                            start=True, stop=True))
                    E.mm(fns, reads=[opfm_b, wmat_b], writes=[bank[1]])

            def head_E_early(h):
                E.op("act", lambda e: e.activation(sg[:], pb[0][0][:],
                                                   AF.Silu),
                     reads=[pb[0][1]], writes=[sg_b])
                E.op("act", lambda e: e.activation(sig[:], pb[0][0][:],
                                                   AF.Tanh, scale=0.5),
                     reads=[pb[0][1]], writes=[sig_b])
                E.op("dve", lambda e: e.tensor_scalar(
                    ew2[:], sg[:], -1.0, 1.0, ALU.mult, ALU.add),
                    reads=[sg_b], writes=[ew2_b])
                E.op("dve", lambda e: e.tensor_tensor(
                    ew2[:], pb[0][0][:], ew2[:], ALU.add),
                    reads=[pb[0][1], ew2_b], writes=[ew2_b])
                E.op("dve", lambda e: e.tensor_copy(hbn[:], pb[1][0][:]),
                     reads=[pb[1][1]], writes=[hbn_b])
                E.op("dve", lambda e: e.tensor_tensor(
                    f1s[:], pb[2][0][:], hbn[:], ALU.mult),
                    reads=[pb[2][1], hbn_b], writes=[f1s_b])
                for tc in range(4):
                    tsl = slice(tc * P, (tc + 1) * P)
                    E.op("dve", lambda e, tsl=tsl, tc=tc: e.tensor_scalar(
                        bcs[:, tsl], pb[2][0][:, tsl],
                        lrf[:, tc * za + 2 * NH + h:
                            tc * za + 2 * NH + h + 1], None, ALU.mult),
                        reads=[pb[2][1], lrf_b], wpart=[bcs_b])

            def head_O(h):
                par = h % 2
                qf, qf_b = qfm[par]
                hsl = slice(h * HD, (h + 1) * HD)
                E.mm([lambda t, hsl=hsl: t.matmul(
                    pb[0][0][:], w0n[:, hsl], qf[:], start=True, stop=True)],
                    reads=[w0n_b, qf_b], writes=[pb[0][1]])
                E.mm([lambda t, hsl=hsl: t.matmul(
                    pb[1][0][:], w2n[:, hsl], qf[:], start=True, stop=True)],
                    reads=[w2n_b, qf_b], writes=[pb[1][1]])
                E.op("act", lambda e: e.activation(zss[:], pb[0][0][:],
                                                   AF.Silu),
                     reads=[pb[0][1]], writes=[zss_b])
                E.op("dve", lambda e: e.tensor_tensor(
                    zfm[:], zss[:], pb[1][0][:], ALU.mult),
                    reads=[zss_b, pb[1][1]], writes=[zfm_b])
                E.mm([lambda t: t.matmul(pb[2][0][:], w1n2[:, hsl], zfm[:],
                                         start=True, stop=True)],
                     reads=[w1n2_b, zfm_b], writes=[pb[2][1]])
                E.op("act", lambda e: e.activation(sqs[:], pb[2][0][:],
                                                   AF.Square),
                     reads=[pb[2][1]], writes=[sqs_b])
                fns = [lambda t, tc=tc: t.matmul(
                    pb[6][0][:, 4 + tc:5 + tc],
                    sqs[:, tc * P:(tc + 1) * P], ones_b[:, 0:1],
                    start=True, stop=True) for tc in range(4)]
                E.mm(fns, reads=[sqs_b, ones_b_b], wpart=[pb[6][1]])
                E.op("dve", lambda e, h=h: e.tensor_scalar(
                    rmsS[:, h * 4:(h + 1) * 4], pb[6][0][:, 4:8], 1.0 / HD,
                    1e-5, ALU.mult, ALU.add),
                    reads=[pb[6][1]], wpart=[rmsS_b])
                E.op("act", lambda e, h=h: e.activation(
                    onc[:, h * CH:(h + 1) * CH], pb[2][0][:], AF.Copy,
                    scale=gam[:, 0:1]),
                    reads=[pb[2][1], gam_b], wpart=[onc_b])

            def head_E_late(h):
                E.op("dve", lambda e: e.tensor_scalar(
                    sig[:], sig[:], 0.5, 0.5, ALU.mult, ALU.add),
                    reads=[sig_b], writes=[sig_b])
                E.op("dve", lambda e: e.tensor_tensor(
                    sig[:], sig[:], ew2[:], ALU.mult),
                    reads=[sig_b, ew2_b], writes=[sig_b])
                E.op("dve", lambda e: e.tensor_tensor(
                    dh2[:], bcs[:], sg[:], ALU.mult),
                    reads=[bcs_b, sg_b], writes=[dh2_b])
                for tc in range(4):
                    tsl = slice(tc * P, (tc + 1) * P)
                    E.op("dve", lambda e, tsl=tsl, tc=tc:
                         e.scalar_tensor_tensor(
                             hl1[:, tsl], hbn[:, tsl],
                             lrf[:, tc * za + NH + h:tc * za + NH + h + 1],
                             sg[:, tsl], ALU.mult, ALU.mult),
                         reads=[hbn_b, lrf_b, sg_b], wpart=[hl1_b])
                    E.op("dve", lambda e, tsl=tsl, tc=tc:
                         e.scalar_tensor_tensor(
                             dg0[:, tsl], f1s[:, tsl],
                             lrf[:, tc * za + h:tc * za + h + 1],
                             sig[:, tsl], ALU.mult, ALU.mult),
                         reads=[f1s_b, lrf_b, sig_b], wpart=[dg0_b])

            def head_G(h):
                hsl = slice(h * HD, (h + 1) * HD)
                kts = [qkv[:, tc * D3 + D + hsl.start:
                           tc * D3 + D + hsl.stop] for tc in range(4)]
                vts = [qkv[:, tc * D3 + 2 * D + hsl.start:
                           tc * D3 + 2 * D + hsl.stop] for tc in range(4)]
                kb = qkv_bJ[jg_of(1, h)]
                vb = qkv_bJ[jg_of(2, h)]
                for gi, (bank, lhss, lhs_b, rhss, rhs_b) in enumerate((
                        (pb[3], [dg0[:, tc * P:(tc + 1) * P]
                                 for tc in range(4)], dg0_b, kts, kb),
                        (pb[4], vts, vb,
                         [hl1[:, tc * P:(tc + 1) * P]
                          for tc in range(4)], hl1_b),
                        (pb[5], [dh2[:, tc * P:(tc + 1) * P]
                                 for tc in range(4)], dh2_b, kts, kb))):
                    fns = []
                    for tc in range(4):
                        fns.append(lambda t, lh=lhss[tc], rh=rhss[tc],
                                   bank=bank, tc=tc:
                                   t.matmul(bank[0][:, 0:HD], lh, rh,
                                            start=(tc == 0),
                                            stop=(tc == 3)))
                    E.mm(fns, reads=[lhs_b, rhs_b], writes=[bank[1]])

            def head_NS(h):
                hsl = slice(h * HD, (h + 1) * HD)
                for i in range(3):
                    xg, xg_b = nsx[i]["x"]
                    E.op("act", lambda e, i=i, xg=xg: e.activation(
                        xg[:], pb[3 + i][0][:, 0:HD], AF.Copy),
                        reads=[pb[3 + i][1]], writes=[xg_b])
                    E.op("act", lambda e, i=i, xg=xg: e.activation(
                        sqs[:, 0:HD], xg[:], AF.Square,
                        accum_out=colA[:, i:i + 1]),
                        reads=[xg_b], writes=[sqs_b], wpart=[colA_b])
                fns = [lambda t, i=i: t.matmul(
                    pb[6][0][:, i:i + 1], ones_f[:], colA[:, i:i + 1],
                    start=True, stop=True) for i in range(3)]
                E.mm(fns, reads=[ones_f_b, colA_b], writes=[pb[6][1]])
                E.op("dve", lambda e: e.tensor_scalar(
                    colB[:, 0:3], pb[6][0][:, 0:3], TINY, None, ALU.add),
                    reads=[pb[6][1]], writes=[colB_b])
                _rsqrt(E, "dve", colC[:, 0:3], colC_b, colB[:, 0:3], colB_b,
                       colD[:, 0:3], colD_b, magic[:, 0:3], magic_b, iters=1)
                E.op("dve", lambda e: e.tensor_scalar(
                    colD[:, 0:3], colC[:, 0:3], -1e-7, 1.0, ALU.mult,
                    ALU.add),
                    reads=[colC_b], writes=[colD_b])
                E.op("dve", lambda e: e.tensor_tensor(
                    colC[:, 0:3], colC[:, 0:3], colD[:, 0:3], ALU.mult),
                    reads=[colC_b, colD_b], writes=[colC_b])
                E.op("dve", lambda e: e.tensor_tensor(
                    colB[:, 0:3], colC[:, 0:3], colC[:, 0:3], ALU.mult),
                    reads=[colC_b], writes=[colB_b])
                E.op("dve", lambda e: e.tensor_scalar(
                    colD[:, 0:3], colB[:, 0:3], NS_B, None, ALU.mult),
                    reads=[colB_b], writes=[colD_b])
                for it in range(2):
                    for i in range(3):
                        xg, xg_b = nsx[i]["x"]
                        xt, xt_b = nsx[i]["xt"]
                        E.mm([lambda t, xg=xg: t.transpose(
                            pbf(7)[:, 0:HD], xg[:], ident_b[:])],
                            reads=[xg_b, ident_b_b], writes=[pb[7][1]])
                        E.op("dve", lambda e, xt=xt: e.tensor_copy(
                            xt[:], pbf(7)[:, 0:HD]),
                            reads=[pb[7][1]], writes=[xt_b])
                    for i in range(3):
                        xt, xt_b = nsx[i]["xt"]
                        E.mm([lambda t, i=i, xt=xt: t.matmul(
                            pb[3 + i][0][:, 0:HD], xt[:], xt[:],
                            start=True, stop=True)],
                            reads=[xt_b], writes=[pb[3 + i][1]])
                    for i in range(3):
                        a_s, a_s_b = nsx[i]["as"]
                        E.op("act", lambda e, i=i, a_s=a_s: e.activation(
                            a_s[:], pb[3 + i][0][:, 0:HD], AF.Copy,
                            scale=colB[:, i:i + 1]),
                            reads=[pb[3 + i][1], colB_b], writes=[a_s_b])
                    for i in range(3):
                        a_s, a_s_b = nsx[i]["as"]
                        E.mm([lambda t, i=i, a_s=a_s: t.matmul(
                            pb[3 + i][0][:, HD:2 * HD], a_s[:], a_s[:],
                            start=True, stop=True)],
                            reads=[a_s_b], wpart=[pb[3 + i][1]])
                    for i in range(3):
                        bas, bas_b = nsx[i]["bas"]
                        E.op("dve", lambda e, i=i, bas=bas: e.tensor_scalar(
                            bas[:], pb[3 + i][0][:, HD:2 * HD], NS_C, None,
                            ALU.mult),
                            reads=[pb[3 + i][1]], writes=[bas_b])
                    for i in range(3):
                        bm, bm_b = nsx[i]["bm"]
                        bas, bas_b = nsx[i]["bas"]
                        E.op("dve", lambda e, i=i, bm=bm, bas=bas:
                             e.scalar_tensor_tensor(
                                 bm[:], pb[3 + i][0][:, 0:HD],
                                 colD[:, i:i + 1], bas[:],
                                 ALU.mult, ALU.add),
                             reads=[pb[3 + i][1], colD_b, bas_b],
                             writes=[bm_b])
                    for i in range(3):
                        bm, bm_b = nsx[i]["bm"]
                        xg, xg_b = nsx[i]["x"]
                        E.mm([lambda t, i=i, bm=bm, xg=xg: t.matmul(
                            pb[3 + i][0][:, 2 * HD:3 * HD], bm[:], xg[:],
                            start=True, stop=False),
                            lambda t, i=i, xg=xg: t.matmul(
                            pb[3 + i][0][:, 2 * HD:3 * HD], aI[:], xg[:],
                            start=False, stop=True)],
                            reads=[bm_b, xg_b, aI_b],
                            wpart=[pb[3 + i][1]])
                    if it == 0:
                        for i in range(3):
                            xg, xg_b = nsx[i]["x"]
                            E.op("act", lambda e, i=i, xg=xg: e.activation(
                                xg[:], pb[3 + i][0][:, 2 * HD:3 * HD],
                                AF.Copy),
                                reads=[pb[3 + i][1]], writes=[xg_b])

            def head_UPD(h):
                hsl = slice(h * HD, (h + 1) * HD)
                for i, (mast, mast_b) in enumerate(((m0, m0_b), (m1, m1_b),
                                                    (m2, m2_b))):
                    E.op("dve", lambda e, i=i, mast=mast, hsl=hsl:
                         e.scalar_tensor_tensor(
                             mast[:, hsl], pb[3 + i][0][:, 2 * HD:3 * HD],
                             colC[:, i:i + 1], mast[:, hsl],
                             ALU.mult, ALU.add),
                         reads=[pb[3 + i][1], mast_b, colC_b],
                         wpart=[mast_b])
                    E.op("act", lambda e, i=i, mast=mast, hsl=hsl:
                         e.activation(
                             sqs[:, 0:HD], mast[:, hsl], AF.Square,
                             accum_out=colA[:, i:i + 1]),
                         reads=[mast_b], writes=[sqs_b], wpart=[colA_b])
                E.op("dve", lambda e: e.tensor_scalar(
                    colB[:, 0:3], colA[:, 0:3], TINY, None, ALU.add),
                    reads=[colA_b], writes=[colB_b])
                _inv_sqrt_eps(E, "dve", colC[:, 0:3], colC_b, colB[:, 0:3],
                              colB_b, colD[:, 0:3], colD_b, magic[:, 0:3],
                              magic_b, 1e-5)
                wncols = wn[:].rearrange("p (w h) -> p w h", h=NH)[:, :, h]
                E.op("dve", lambda e, wncols=wncols: e.tensor_tensor(
                    colD[:, 0:3], wncols, colC[:, 0:3], ALU.mult),
                    reads=[wn_b, colC_b], writes=[colD_b])
                for i, (mast, mast_b) in enumerate(((m0, m0_b), (m1, m1_b),
                                                    (m2, m2_b))):
                    E.op("dve", lambda e, i=i, mast=mast, hsl=hsl:
                         e.tensor_scalar(
                             mast[:, hsl], mast[:, hsl], colD[:, i:i + 1],
                             None, ALU.mult),
                         reads=[colD_b, mast_b], wpart=[mast_b])
                for (mast, mast_b, dst, dst_b, dsl) in (
                        (m0, m0_b, w0n, w0n_b, hsl),
                        (m2, m2_b, w2n, w2n_b, hsl),
                        (m1, m1_b, w1n2, w1n2_b, hsl)):
                    E.mm([lambda t, mast=mast, hsl=hsl: t.transpose(
                        pb[7][0][:, 0:HD], mast[:, hsl], ident_f[:])],
                        reads=[mast_b, ident_f_b], writes=[pb[7][1]])
                    E.op("act", lambda e, dst=dst, dsl=dsl: e.activation(
                        dst[:, dsl], pb[7][0][:, 0:HD], AF.Copy),
                        reads=[pb[7][1]], wpart=[dst_b])
                E.op("dve", lambda e, hsl=hsl: e.tensor_copy(
                    w1t[:, hsl], m1[:, hsl]),
                    reads=[m1_b], wpart=[w1t_b])

            hallow = [0]
            wreq = [0]

            def p3_gen():
                while hallow[0] < 1:
                    wreq[0] = 1
                    yield
                wreq[0] = 0
                head_TF(0)
                head_E_early(0)
                yield
                for h in range(NH):
                    if h > 0:
                        head_O(h - 1)
                        yield
                    head_E_late(h)
                    yield
                    if h + 1 < NH:
                        while hallow[0] < h + 2:
                            wreq[0] = h + 2
                            yield
                        wreq[0] = 0
                        head_TF(h + 1)
                        head_E_early(h + 1)
                        yield
                    head_G(h)
                    yield
                    head_NS(h)
                    yield
                    head_UPD(h)
                    yield
                head_O(NH - 1)

            p3it = p3_gen()
            subs = [0]
            TOT = 1 + (NH - 1) * 2 + NH * 4

            def p3_step(n):
                while subs[0] < n and wreq[0] <= hallow[0]:
                    try:
                        next(p3it)
                        subs[0] += 1
                    except StopIteration:
                        subs[0] = 10 ** 9
                        break

            emit_P2b()
            if c == 0:
                emit_hop2("g")
                emit_hop2("u")
            for _ in p3it:
                pass

            for h0 in range(0, 4 * NH, 64):
                h1 = min(h0 + 64, 4 * NH)
                _rsqrt(E, "dve", colsR[:, h0:h1], colsR_b, rmsS[:, h0:h1],
                       rmsS_b, rmsT[:, h0:h1], rmsT_b, magic64[:, 0:h1 - h0],
                       magic64_b)
            for h in range(NH):
                fns = [lambda t, tc=tc, h=h: t.transpose(
                    pb[6][0][0:1, tc * P:(tc + 1) * P],
                    colsR[:, h * 4 + tc:h * 4 + tc + 1], ident_f[:])
                    for tc in range(4)]
                E.mm(fns, reads=[colsR_b, ident_f_b], writes=[pb[6][1]])
                E.op("dve", lambda e: e.tensor_copy(
                    rowA[:], pb[6][0][0:1, 0:CH]),
                    reads=[pb[6][1]], writes=[rowA_b])
                E.mm([lambda t: t.matmul(pb[6][0][:], ones_f[0:1, :],
                                         rowA[:], start=True, stop=True)],
                     reads=[ones_f_b, rowA_b], writes=[pb[6][1]])
                E.op("dve", lambda e: e.tensor_copy(bcs[:], pb[6][0][:]),
                     reads=[pb[6][1]], writes=[bcs_b])
                E.op("dve", lambda e, h=h: e.tensor_tensor(
                    onc[:, h * CH:(h + 1) * CH], onc[:, h * CH:(h + 1) * CH],
                    bcs[:], ALU.mult),
                    reads=[bcs_b, onc_b], wpart=[onc_b])

            att_B = Buf(f"att_c{c}")
            for dc in range(n_dc):
                wp, wp_b = wst[dc % 2]
                wpv = wp[:, 0:n_dc * P]
                src = wproj_bf.rearrange("(hk p) j -> p hk j", p=P)[
                    :, :, dc * P:(dc + 1) * P]
                E.dma("qs", lambda e, wpv=wpv, src=src: e.dma_start(
                    out=wpv.rearrange("p (hk j) -> p hk j", j=P), in_=src),
                    reads=[wproj_bf_B], writes=[wp_b])
                bank = pb[dc % 2]
                fns = []
                for hk in range(NH):
                    fns.append(lambda t, hk=hk, wpv=wpv, bank=bank: t.matmul(
                        bank[0][:], wpv[:, hk * P:(hk + 1) * P],
                        onc[:, hk * CH:(hk + 1) * CH],
                        start=(hk == 0), stop=(hk == NH - 1)))
                E.mm(fns, reads=[wp_b, onc_b], writes=[bank[1]])
                ssl = slice((dc % 2) * CH, (dc % 2) * CH + CH)
                E.op("act", lambda e, bank=bank, ssl=ssl: e.activation(
                    stg[:, ssl], bank[0][:], AF.Copy),
                    reads=[bank[1]], writes=[stg_b])
                E.dma("qp", lambda e, dc=dc, ssl=ssl: e.dma_start(
                    out=att_scr[dc * P:(dc + 1) * P, :], in_=stg[:, ssl]),
                    reads=[stg_b], wpart=[att_B])

            if c == 0:
                emit_hop2("d")

            cs.close()
            E.fence()

            fs = ExitStack()

            def fsb(name, shape, dtype):
                return sb(f"{name}_c{c}", shape, dtype, fs)

            gated, gated_b = fsb("gated", [P, n_hc * CH], BF16)
            wgp = [fsb(f"wgp{i}", [P, n_dc * P], BF16) for i in range(2)]
            wup = [fsb(f"wup{i}", [P, n_dc * P], BF16) for i in range(2)]
            nhh = n_hc // 2
            wdp = [fsb(f"wdp{i}", [P, nhh * P], BF16) for i in range(2)]
            zz, zz_b = fsb("zz", [P, CH], BF16)
            att2 = [fsb(f"att2{i}", [P, CH], BF16) for i in range(2)]
            xr2 = [fsb(f"xr2{i}", [P, CH], F32) for i in range(2)]
            ost, ost_b = fsb("ost", [P, CH], F32)
            sfm, sfm_b = fsb("sfm", [P, CH], F32)

            for half in range(2):
                wdx, wd_b2 = wdp[half]
                h0 = half * nhh
                src2 = wdT.rearrange("(hc p) j -> p hc j", p=P)[
                    :, h0:h0 + nhh, 0:P]
                E.dma("qs", lambda e, wdx=wdx, src2=src2: e.dma_start(
                    out=wdx[:].rearrange("p (hc j) -> p hc j", j=P),
                    in_=src2),
                    reads=[wdT_B], writes=[wd_b2])
            for hc in range(n_hc):
                wgx, wg_b2 = wgp[hc % 2]
                wux, wu_b2 = wup[hc % 2]
                for (wx, wxb, srcT, srcB) in ((wgx, wg_b2, wgT, wgT_B),
                                              (wux, wu_b2, wuT, wuT_B)):
                    src = srcT.rearrange("(dk p) j -> p dk j", p=P)[
                        :, :, hc * P:(hc + 1) * P]
                    E.dma("qs", lambda e, wx=wx, src=src: e.dma_start(
                        out=wx[:].rearrange("p (dk j) -> p dk j", j=P),
                        in_=src),
                        reads=[srcB], writes=[wxb])
                bg = pb[(hc % 2) * 2]
                bu = pb[(hc % 2) * 2 + 1]
                for (bank, wx, wxb) in ((bg, wgx, wg_b2), (bu, wux, wu_b2)):
                    fns = []
                    for dk in range(n_dc):
                        fns.append(lambda t, dk=dk, wx=wx, bank=bank:
                                   t.matmul(
                            bank[0][:], wx[:, dk * P:(dk + 1) * P],
                            xnfm[:, dk * CH:(dk + 1) * CH],
                            start=(dk == 0), stop=(dk == n_dc - 1)))
                    E.mm(fns, reads=[wxb, xnfm_b], writes=[bank[1]])
                E.op("act", lambda e, bg=bg: e.activation(
                    zz[:], bg[0][:], AF.Silu),
                    reads=[bg[1]], writes=[zz_b])
                E.op("dve", lambda e, hc=hc, bu=bu: e.tensor_tensor(
                    gated[:, hc * CH:(hc + 1) * CH], zz[:], bu[0][:],
                    ALU.mult),
                    reads=[zz_b, bu[1]], wpart=[gated_b])

            for dc in range(n_dc):
                bank = pb[4 + dc % 2]
                for half in range(2):
                    wdx, wd_b2 = wdp[half]
                    h0 = half * nhh
                    if dc > 0:
                        src2 = wdT.rearrange("(hc p) j -> p hc j", p=P)[
                            :, h0:h0 + nhh, dc * P:(dc + 1) * P]
                        E.dma("qs", lambda e, wdx=wdx, src2=src2:
                              e.dma_start(
                                  out=wdx[:].rearrange(
                                      "p (hc j) -> p hc j", j=P),
                                  in_=src2),
                              reads=[wdT_B], writes=[wd_b2])
                    fns = []
                    for hh in range(nhh):
                        hc = h0 + hh
                        fns.append(lambda t, hh=hh, hc=hc, wdx=wdx,
                                   bank=bank: t.matmul(
                            bank[0][:], wdx[:, hh * P:(hh + 1) * P],
                            gated[:, hc * CH:(hc + 1) * CH],
                            start=(hc == 0), stop=(hc == n_hc - 1)))
                    if half == 0:
                        E.mm(fns, reads=[wd_b2, gated_b], writes=[bank[1]])
                    else:
                        E.mm(fns, reads=[wd_b2, gated_b], wpart=[bank[1]])
                at, at_b = att2[dc % 2]
                E.dma("qs", lambda e, at=at, dc=dc: e.dma_start(
                    out=at[:], in_=att_scr[dc * P:(dc + 1) * P, :]),
                    reads=[att_B], writes=[at_b])
                xr, xr_b = xr2[dc % 2]
                xsrc = x_d[c * CH:(c + 1) * CH,
                           dc * P:(dc + 1) * P].rearrange(
                    "(tc p) d -> p tc d", p=P)
                E.dma("qs", lambda e, xr=xr, xsrc=xsrc: e.dma_start(
                    out=xr[:].rearrange("p (tc d) -> p tc d", d=P),
                    in_=xsrc), writes=[xr_b])
                E.op("dve", lambda e, bank=bank, at=at: e.tensor_tensor(
                    sfm[:], bank[0][:], at[:], ALU.add),
                    reads=[bank[1], at_b], writes=[sfm_b])
                tb = pb[6]
                fns = [lambda t, tc=tc, tb=tb: t.transpose(
                    tb[0][:, tc * P:(tc + 1) * P],
                    sfm[:, tc * P:(tc + 1) * P], ident_f[:])
                    for tc in range(4)]
                E.mm(fns, reads=[sfm_b, ident_f_b], writes=[tb[1]])
                E.op("dve", lambda e, tb=tb, xr=xr: e.tensor_tensor(
                    ost[:].rearrange("p (tc d) -> p tc d", d=P),
                    tb[0][:, 0:CH].rearrange("p (tc d) -> p tc d", d=P),
                    xr[:].rearrange("p (tc d) -> p tc d", d=P), ALU.add),
                    reads=[tb[1], xr_b], writes=[ost_b])
                ydst = y_d[c * CH:(c + 1) * CH, dc * P:(dc + 1) * P]
                E.dma("qp", lambda e, ydst=ydst: e.dma_start(
                    out=ydst.rearrange("(tc p) d -> p tc d", p=P),
                    in_=ost[:].rearrange("p (tc d) -> p tc d", d=P)),
                    reads=[ost_b])
                if c + 1 < n_chunks:
                    step = max(1, n_dc // 4)
                    if n_dc >= 8:
                        for tc_ in range(4):
                            if dc == tc_ * step:
                                emit_P1_chain(c + 1, tc_)
                            if dc == tc_ * step + step // 2:
                                emit_P1_tr(c + 1, tc_)
                    else:
                        for tc_ in range(4):
                            if dc == min(tc_ * step, n_dc - 1):
                                emit_P1_tc(c + 1, tc_)
            fs.close()
            E.fence()

        E.finish()

    return nc


_CACHE = {}


def _get_nc(key):
    if key not in _CACHE:
        _CACHE[key] = build_nc(*key)
    return _CACHE[key]


def kernel(L0_attn_x, L0_attn_w0, L0_attn_w1, L0_attn_w2, L0_attn_W_qkv,
           L0_attn_W_lr, L0_attn_b_lr, L0_attn_W_proj, L0_attn_onorm_w,
           L0_ffn_w_gate, L0_ffn_w_up, L0_ffn_w_down):
    x = np.ascontiguousarray(np.asarray(L0_attn_x, dtype=np.float32))
    B = x.shape[0]
    nc = _get_nc((16, 8, 64))
    common = dict(
        w0=np.ascontiguousarray(L0_attn_w0, np.float32),
        w1=np.ascontiguousarray(L0_attn_w1, np.float32),
        w2=np.ascontiguousarray(L0_attn_w2, np.float32),
        wqkv=np.ascontiguousarray(L0_attn_W_qkv, np.float32),
        wlr=np.ascontiguousarray(L0_attn_W_lr, np.float32),
        blr=np.ascontiguousarray(L0_attn_b_lr, np.float32),
        wproj=np.ascontiguousarray(L0_attn_W_proj, np.float32),
        onw=np.ascontiguousarray(L0_attn_onorm_w, np.float32),
        wg=np.ascontiguousarray(L0_ffn_w_gate, np.float32),
        wu=np.ascontiguousarray(L0_ffn_w_up, np.float32),
        wd=np.ascontiguousarray(L0_ffn_w_down, np.float32),
    )
    in_maps = [dict(common, x=np.ascontiguousarray(x[i])) for i in range(B)]
    res = run_bass_kernel_spmd(nc, in_maps, core_ids=list(range(B)))
    return np.stack([np.asarray(r["y"], dtype=np.float32)
                     for r in res.results], axis=0)
